# version 13
# baseline (speedup 1.0000x reference)
import contextlib
import numpy as np
import ml_dtypes
import concourse.bass as bass
import concourse.mybir as mybir
from concourse.bass_utils import run_bass_kernel_spmd

F32 = mybir.dt.float32
BF16 = mybir.dt.bfloat16
F16 = mybir.dt.float16
I32 = mybir.dt.int32
AF = mybir.ActivationFunctionType
ALU = mybir.AluOpType
AX = mybir.AxisListType
BF = ml_dtypes.bfloat16

B, T, D = 4, 4096, 2048
TPC = 2048
NSLOT = 16
FC = D // 128
LN_EPS = 1e-5
NEGB = -30000.0
ISCALE = 0.08838834764831845
L1A_DEBUG = False
L0F_DEBUG = False
L0F_SERIAL_CC = True
L0A_PHASES = 4
L0A_P2 = 0
L0A_QT_OUT = False



def own_positions(core):
    q = core % 2
    pos = []
    for j in range(NSLOT):
        base = (2 * j + q) * 128
        pos.extend(range(base, base + 128))
    return np.array(pos, dtype=np.int64)


def pre_fmaj(W):
    Fdim, O = W.shape
    return np.ascontiguousarray(
        W.reshape(Fdim // 128, 128, O).transpose(1, 0, 2).reshape(128, -1))


def pre_headmaj(W, hd=128):
    Fdim, O = W.shape
    w = W.reshape(Fdim // 128, 128, O // hd, hd).transpose(2, 1, 0, 3)
    return np.ascontiguousarray(w.reshape(O // hd, 128, (Fdim // 128) * hd))


def l0_rope_tabs(core):
    hd = 128
    freqs = 1.0 / (10000.0 ** (np.arange(0, hd, 2, dtype=np.float64) / hd))
    emb = np.outer(np.arange(T, dtype=np.float64), freqs)
    cos = np.concatenate([np.cos(emb), np.cos(emb)], -1).astype(np.float32)
    sin = np.concatenate([np.sin(emb), np.sin(emb)], -1).astype(np.float32)
    op = own_positions(core)
    kp = np.concatenate([op, own_positions(core ^ 1)])
    return (np.ascontiguousarray(cos[op].T.astype(BF)),
            np.ascontiguousarray(sin[op].T.astype(BF)),
            np.ascontiguousarray(cos[kp].T.astype(BF)),
            np.ascontiguousarray(sin[kp].T.astype(BF)))


def l0_pmask(core):
    m = np.zeros((128, NSLOT + 1), dtype=np.float32)
    if core % 2 == 0:
        m[:, :NSLOT] = NEGB
    return m


def f32a(x):
    return np.ascontiguousarray(x, dtype=np.float32)



class Ctr:
    def __init__(self):
        self.v = 0

    def inc(self, n=1):
        self.v += n
        return self.v


def emit_ln_sc(s, sem, base, x_ap, h_ap, stats, junk, prewait=None):
    if prewait is not None:
        s.wait_ge(prewait[0], prewait[1])
    s.activation(junk, x_ap, AF.Copy, accum_out=stats[:, 0:1])
    s.activation(h_ap, x_ap, AF.Square,
                 accum_out=stats[:, 1:2]).then_inc(sem, 1)
    s.wait_ge(sem, base + 4)
    s.activation(stats[:, 5:6], stats[:, 4:5],
                 AF.Sqrt).then_inc(sem, 1)


def emit_ln_tile(blk, sem, c, x_ap, h_ap, stats, junk, width=D, eps=LN_EPS,
                 prewait=None):
    base = c.v

    @blk.scalar
    def _(s):
        emit_ln_sc(s, sem, base, x_ap, h_ap, stats, junk, prewait)

    @blk.vector
    def _(v):
        emit_ln_vec(v, sem, base, x_ap, h_ap, stats, width, eps)
    c.v = base + 8


def emit_ln_vec(v, sem, base, x_ap, h_ap, stats, width=D, eps=LN_EPS):
    v.wait_ge(sem, base + 1)
    v.tensor_tensor(stats[:, 2:3], stats[:, 0:1], stats[:, 0:1],
                    ALU.mult)
    v.tensor_scalar_mul(stats[:, 7:8], stats[:, 0:1],
                        -1.0 / width).then_inc(sem, 1)
    v.wait_ge(sem, base + 2)
    v.scalar_tensor_tensor(stats[:, 3:4], stats[:, 2:3], -1.0 / width,
                           stats[:, 1:2], ALU.mult,
                           ALU.add).then_inc(sem, 1)
    v.wait_ge(sem, base + 3)
    v.tensor_scalar(stats[:, 4:5], stats[:, 3:4], 1.0 / width, eps,
                    ALU.mult, ALU.add).then_inc(sem, 1)
    v.wait_ge(sem, base + 5)
    v.reciprocal(stats[:, 5:6], stats[:, 5:6]).then_inc(sem, 1)
    v.wait_ge(sem, base + 6)
    v.tensor_tensor(stats[:, 6:7], stats[:, 7:8], stats[:, 5:6],
                    ALU.mult).then_inc(sem, 1)
    v.wait_ge(sem, base + 7)
    v.tensor_scalar(h_ap, x_ap, stats[:, 5:6], stats[:, 6:7],
                    ALU.mult, ALU.add).then_inc(sem, 1)


def emit_ln_phase(nc, idn, x_d, hT, xt, htile, junk, stats, tps0, tps1,
                  s_ld, s_pa, s_mm, s_pb, ldS, pa, mm, pb, ntiles=16,
                  src_fn=None, dst_fn=None):
    if src_fn is None:
        def src_fn(g):
            return x_d[g * 128:(g + 1) * 128, :]
    if dst_fn is None:
        def dst_fn(g, fc):
            return hT[:, fc * TPC + g * 128:fc * TPC + (g + 1) * 128]
    with nc.Block() as blk:
        ld0 = ldS.v
        pa0 = pa.v
        mm0 = mm.v
        pb0 = pb.v

        @blk.sync
        def _(s):
            s.dma_start(xt[:, 0:D], src_fn(0)).then_inc(s_ld, 16)
            for g in range(1, ntiles):
                if g >= 2:
                    s.wait_ge(s_pa, pa0 + 8 * (g - 2) + 8)
                s.dma_start(xt[:, (g % 2) * D:(g % 2 + 1) * D],
                            src_fn(g)).then_inc(s_ld, 16)
        ldS.inc(16 * ntiles)

        @blk.scalar
        def _(s):
            for g in range(ntiles):
                off = (g % 2) * 8
                if g >= 2:
                    s.wait_ge(s_mm, mm0 + 16 * (g - 1))
                emit_ln_sc(s, s_pa, pa0 + 8 * g,
                           xt[:, (g % 2) * D:(g % 2 + 1) * D],
                           htile[:, (g % 2) * D:(g % 2 + 1) * D],
                           stats[:, off:off + 8], junk[:, :],
                           prewait=(s_ld, ld0 + 16 * (g + 1)))

        @blk.vector
        def _(v):
            for g in range(ntiles):
                off = (g % 2) * 8
                emit_ln_vec(v, s_pa, pa0 + 8 * g,
                            xt[:, (g % 2) * D:(g % 2 + 1) * D],
                            htile[:, (g % 2) * D:(g % 2 + 1) * D],
                            stats[:, off:off + 8])
                for fc in range(FC):
                    ps = tps0 if fc % 2 == 0 else tps1
                    v.wait_ge(s_mm, mm0 + 16 * g + fc + 1)
                    v.tensor_copy(dst_fn(g, fc),
                                  ps[:, 0:128]).then_inc(s_pb, 1)

        @blk.tensor
        def _(t):
            for g in range(ntiles):
                t.wait_ge(s_pa, pa0 + 8 * g + 8)
                for fc in range(FC):
                    k = 16 * g + fc
                    ps = tps0 if fc % 2 == 0 else tps1
                    if k >= 2:
                        t.wait_ge(s_pb, pb0 + k - 1)
                    t.transpose(ps[:, 0:128],
                                htile[:, (g % 2) * D + fc * 128:
                                       (g % 2) * D + (fc + 1) * 128],
                                idn[:, :]).then_inc(s_mm, 1)
        pa.v = pa0 + 8 * ntiles
        mm.inc(16 * ntiles)
        pb.v = pb0 + 16 * ntiles


def emit_rope128(v, dst, src, cos, sin, scr, width):
    v.tensor_copy(scr[0:64, 0:width], src[64:128, :])
    v.tensor_copy(scr[64:128, 0:width], src[0:64, :])
    v.tensor_tensor(scr[:, 0:width], scr[:, 0:width], sin[:, :], ALU.mult)
    v.tensor_tensor(dst, src, cos, ALU.mult)
    v.tensor_tensor(dst[0:64, :], dst[0:64, :], scr[0:64, 0:width],
                    ALU.subtract)
    return v.tensor_tensor(dst[64:128, :], dst[64:128, :],
                           scr[64:128, 0:width], ALU.add)


class Prog:
    def __init__(self):
        self.nc = bass.Bass("TRN2", target_bir_lowering=False)
        self._cache = {}
        self.override = {}
        self._pfx = ""

    def din(self, name, shape, dtype):
        if name in self.override:
            return self.override[name]
        if name in self._cache:
            return self._cache[name]
        ap = self.nc.dram_tensor(name, list(shape), dtype,
                                 kind="ExternalInput").ap()
        self._cache[name] = ap
        return ap

    def dout(self, name, shape, dtype):
        if name in self.override:
            return self.override[name]
        return self.nc.dram_tensor(name, list(shape), dtype,
                                   kind="ExternalOutput").ap()

    def dtmp(self, name, shape, dtype, addr_space="Local"):
        return self.nc.dram_tensor(self._pfx + name, list(shape), dtype,
                                   kind="Internal",
                                   addr_space=addr_space).ap()


class _NCPfx:

    def __init__(self, nc, pfx):
        self._nc = nc
        self._pfx = pfx

    def sbuf_tensor(self, name, *a, **k):
        return self._nc.sbuf_tensor(self._pfx + name, *a, **k)

    def psum_tensor(self, name, *a, **k):
        return self._nc.psum_tensor(self._pfx + name, *a, **k)

    def semaphore(self, name):
        return self._nc.semaphore(self._pfx + name)

    def __getattr__(self, n):
        return getattr(self._nc, n)



class _StopEmit(Exception):
    pass


def build_l0a(p=None, pfx=""):
    try:
        return _build_l0a(p, pfx)
    except _StopEmit:
        return _BUILD_HOLDER[0]


_BUILD_HOLDER = [None]


def _build_l0a(p=None, pfx=""):
    if p is None:
        p = Prog()
    p._pfx = pfx
    _BUILD_HOLDER[0] = p
    nc = _NCPfx(p.nc, pfx)
    x_own = p.din("x_own", (TPC, D), F32)
    x_oth = p.din("x_oth", (TPC, D), F32)
    Wq_d = p.din("Wq_pre", (16, 128, FC * 128), F32)
    Wk_d = p.din("Wk_pre", (128, FC * 512), F32)
    Wv_d = p.din("Wv_pre", (128, FC * 512), F32)
    Wo_d = p.din("Wo_pre", (128, FC * D), F32)
    qcos_d = p.din("qcos", (128, TPC), BF16)
    qsin_d = p.din("qsin", (128, TPC), BF16)
    kcos_d = p.din("kcos", (128, 2 * TPC), BF16)
    ksin_d = p.din("ksin", (128, 2 * TPC), BF16)
    pmask_d = p.din("pmask", (128, NSLOT + 1), F32)
    ident_d = p.din("ident", (128, 128), BF16)
    ones_d = p.din("ones_bf", (128, 128), BF16)
    x1_out = p.dout("x1", (TPC, D), F32)
    if L0A_QT_OUT:
        qT_dram = p.dout("qT_scr", (16, 128, TPC), BF16)
    else:
        qT_dram = p.dtmp("qT_scr", (16, 128, TPC), BF16)

    ldS = Ctr(); ldW = Ctr(); pa = Ctr(); pb = Ctr(); mm = Ctr(); ex = Ctr(); pv = Ctr()
    qd = Ctr(); st = Ctr(); ev = Ctr()

    with (
        nc.sbuf_tensor("kT", [128, 4 * 4096], BF16) as kT,
        nc.sbuf_tensor("v_sb", [128, 32 * 512], BF16) as v_sb,
        nc.sbuf_tensor("idn", [128, 128], BF16) as idn,
        nc.sbuf_tensor("ones", [128, 128], BF16) as ones,
        nc.sbuf_tensor("pmsk", [128, NSLOT + 1], F32) as pmsk,
        nc.sbuf_tensor("stats", [128, 8], F32) as stats,
        nc.semaphore("s_ld") as s_ld,
        nc.semaphore("s_lw") as s_lw,
        nc.semaphore("s_pa") as s_pa,
        nc.semaphore("s_pb") as s_pb,
        nc.semaphore("s_mm") as s_mm,
        nc.semaphore("s_ex") as s_ex,
        nc.semaphore("s_pv") as s_pv,
        nc.semaphore("s_qd") as s_qd,
        nc.semaphore("s_st") as s_st,
        nc.semaphore("s_ev") as s_ev,
    ):
        with nc.Block() as blk:
            @blk.sync
            def _(s):
                s.dma_start(idn[:, :], ident_d[:, :]).then_inc(s_ld, 16)
                s.dma_start(ones[:, :], ones_d[:, :]).then_inc(s_ld, 16)
                s.dma_start(pmsk[:, :], pmask_d[:, :]).then_inc(s_ld, 16)
                s.wait_ge(s_ld, ldS.inc(48))

        with nc.sbuf_tensor("hT", [128, FC * TPC], BF16, side="right") as hT:
            with contextlib.ExitStack() as _es1:
                _e = _es1.enter_context
                xt = _e(nc.sbuf_tensor("xt", [128, 2 * D], F32, side="right"))
                htile = _e(nc.sbuf_tensor("htile", [128, 2 * D], BF16,
                                          side="right"))
                junk = _e(nc.sbuf_tensor("junk", [128, D], BF16, side="right"))
                lstats = _e(nc.sbuf_tensor("lstats", [128, 16], F32,
                                           side="right"))
                kraw = _e(nc.sbuf_tensor("kraw", [128, 2048], BF16,
                                         side="right"))
                kcs = _e(nc.sbuf_tensor("kcs", [128, 512], BF16, side="right"))
                ksn = _e(nc.sbuf_tensor("ksn", [128, 512], BF16, side="right"))
                rsc = _e(nc.sbuf_tensor("rsc", [128, 256], BF16, side="right"))
                Wk_sb = _e(nc.sbuf_tensor("Wk_sb", [128, FC * 512], BF16, side="right"))
                Wv_sb = _e(nc.sbuf_tensor("Wv_sb", [128, FC * 512], BF16, side="right"))
                with nc.Block() as blk:
                    @blk.gpsimd
                    def _(g):
                        for i in range(4):
                            g.dma_start(Wk_sb[:, i * 2048:(i + 1) * 2048],
                                        Wk_d[:, i * 2048:(i + 1) * 2048]
                                        ).then_inc(s_lw, 16)
                            g.dma_start(Wv_sb[:, i * 2048:(i + 1) * 2048],
                                        Wv_d[:, i * 2048:(i + 1) * 2048]
                                        ).then_inc(s_lw, 16)
                        g.wait_ge(s_lw, ldW.inc(128))

                def _p1_kv(tag, prs):
                    with contextlib.ExitStack() as _eps:
                        kps = _eps.enter_context(
                            nc.psum_tensor(f"kps{tag}", [128, 2048], F32))
                        vps = _eps.enter_context(
                            nc.psum_tensor(f"vps{tag}", [128, 2048], F32))
                        with nc.Block() as blk:
                            pa0 = pa.v
                            ld0 = ldS.v

                            @blk.sync
                            def _(s):
                                for j, pr in enumerate(prs):
                                    sl = (j % 2) * 256
                                    if j >= 2:
                                        s.wait_ge(s_pa, pa0 + 4 * (j - 2) + 4)
                                    s.dma_start(
                                        kcs[:, sl:sl + 256],
                                        kcos_d[:, pr * 256:(pr + 1) * 256]
                                        ).then_inc(s_ld, 16)
                                    s.dma_start(
                                        ksn[:, sl:sl + 256],
                                        ksin_d[:, pr * 256:(pr + 1) * 256]
                                        ).then_inc(s_ld, 16)
                            ldS.inc(32 * len(prs))

                            @blk.tensor
                            def _(t):
                                for j, pr in enumerate(prs):
                                    o = (j % 2) * 1024
                                    if j >= 2:
                                        t.wait_ge(s_pa, pa0 + 4 * (j - 2) + 4)
                                    bt = (2 * pr) % 16
                                    mi = None
                                    for kb in range(4):
                                        for fc in range(FC):
                                            mi = t.matmul(
                                                kps[:, o + kb * 256:
                                                    o + (kb + 1) * 256],
                                                Wk_sb[:, fc * 512 + kb * 128:
                                                      fc * 512 + (kb + 1) * 128],
                                                hT[:, fc * TPC + bt * 128:
                                                   fc * TPC + (bt + 2) * 128],
                                                start=(fc == 0),
                                                stop=(fc == FC - 1))
                                    mi.then_inc(s_pa, 1)
                                    for i in range(2):
                                        for fc in range(FC):
                                            mi = t.matmul(
                                                vps[:, o + i * 512:
                                                    o + (i + 1) * 512],
                                                hT[:, fc * TPC + (bt + i) * 128:
                                                   fc * TPC + (bt + i + 1) * 128],
                                                Wv_sb[:, fc * 512:
                                                      (fc + 1) * 512],
                                                start=(fc == 0),
                                                stop=(fc == FC - 1))
                                    mi.then_inc(s_pa, 1)

                            @blk.scalar
                            def _(s):
                                for j, pr in enumerate(prs):
                                    o = (j % 2) * 1024
                                    s.wait_ge(s_pa, pa0 + 4 * j + 1)
                                    si = None
                                    for kb in range(4):
                                        si = s.activation(
                                            kraw[:, o + kb * 256:
                                                 o + (kb + 1) * 256],
                                            kps[:, o + kb * 256:
                                                o + (kb + 1) * 256],
                                            AF.Copy)
                                    si.then_inc(s_pa, 1)

                            @blk.vector
                            def _(v):
                                for j, pr in enumerate(prs):
                                    o = (j % 2) * 1024
                                    tb = (j % 2) * 256
                                    s0 = 2 * pr
                                    v.wait_ge(s_pa, pa0 + 4 * j + 3)
                                    v.wait_ge(s_ld, ld0 + 32 * (j + 1))
                                    for kb in range(4):
                                        emit_rope128(
                                            v,
                                            kT[:, kb * 4096 + s0 * 128:
                                               kb * 4096 + (s0 + 2) * 128],
                                            kraw[:, o + kb * 256:
                                                 o + (kb + 1) * 256],
                                            kcs[:, tb:tb + 256],
                                            ksn[:, tb:tb + 256],
                                            rsc[:, :], 256)
                                    vi = None
                                    for i in range(2):
                                        vi = v.tensor_copy(
                                            v_sb[:, (s0 + i) * 512:
                                                 (s0 + i + 1) * 512],
                                            vps[:, o + i * 512:
                                                o + (i + 1) * 512])
                                    vi.then_inc(s_pa, 1)
                            pa.v = pa0 + 4 * len(prs)

                for half, x_src in ((1, x_oth), (0, x_own)):
                    with contextlib.ExitStack() as _eps:
                        tps0 = _eps.enter_context(
                            nc.psum_tensor(f"tps0h{half}", [128, 1024], BF16))
                        tps1 = _eps.enter_context(
                            nc.psum_tensor(f"tps1h{half}", [128, 1024], BF16))
                        emit_ln_phase(nc, idn, x_src, hT, xt, htile, junk,
                                      lstats, tps0, tps1, s_ld, s_pa, s_mm,
                                      s_pb, ldS, pa, mm, pb, ntiles=16)
                    _p1_kv(half, list(range(8 * half, 8 * half + 8)))

            with contextlib.ExitStack() as _es2:
                if L0A_PHASES < 2:
                    raise _StopEmit()
                _e = _es2.enter_context
                Wq_sb = _e(nc.sbuf_tensor("Wq_sb", [128, 2 * FC * 128], BF16, side="right"))
                qcs = _e(nc.sbuf_tensor("qcs", [128, TPC], BF16, side="right"))
                qsn = _e(nc.sbuf_tensor("qsn", [128, TPC], BF16, side="right"))
                qraw = _e(nc.sbuf_tensor("qraw", [128, TPC], BF16,
                                         side="right"))
                qrope = _e(nc.sbuf_tensor("qrope", [128, TPC], BF16, side="right"))
                qsc = _e(nc.sbuf_tensor("qsc", [128, TPC], BF16, side="right"))
                qps = _e(nc.psum_tensor("qps", [128, 2048], F32))
                with nc.Block() as blk:
                    ld_tab = ldS.v + 32
                    lw_b = ldW.v
                    mm_b = mm.v
                    qd_b = qd.v
                    st_b = st.v

                    @blk.sync
                    def _(s):
                        s.dma_start(qcs[:, :], qcos_d[:, :]).then_inc(s_ld, 16)
                        s.dma_start(qsn[:, :], qsin_d[:, :]).then_inc(s_ld, 16)
                        if L0A_P2 < 1:
                            for h in range(16):
                                s.wait_ge(s_qd, qd_b + h + 1)
                                s.dma_start(qT_dram[h, :, :], qrope[:, :]
                                            ).then_inc(s_st, 16)
                            s.wait_ge(s_st, st_b + 16 * 16)
                    ldS.inc(32)

                    @blk.gpsimd
                    def _(g):
                        for h in range(16):
                            if h >= 2:
                                g.wait_ge(s_mm, mm_b + h - 1)
                            g.dma_start(Wq_sb[:, (h % 2) * FC * 128:
                                              (h % 2 + 1) * FC * 128],
                                        Wq_d[h, :, :]).then_inc(s_lw, 16)

                    @blk.tensor
                    def _(t):
                        for h in range(16):
                            t.wait_ge(s_lw, lw_b + 16 * (h + 1))
                            if h >= 1:
                                t.wait_ge(s_qd, qd_b + h)
                            mi = None
                            for cc in range(4):
                                for fc in range(FC):
                                    mi = t.matmul(
                                        qps[:, cc * 512:(cc + 1) * 512],
                                        Wq_sb[:, (h % 2) * FC * 128 + fc * 128:
                                              (h % 2) * FC * 128 + (fc + 1) * 128],
                                        hT[:, fc * TPC + cc * 512:
                                           fc * TPC + (cc + 1) * 512],
                                        start=(fc == 0), stop=(fc == FC - 1))
                            mi.then_inc(s_mm, 1)

                    @blk.vector
                    def _(v):
                        v.wait_ge(s_ld, ld_tab)
                        for h in range(16):
                            v.wait_ge(s_mm, mm_b + h + 1)
                            if h >= 1 and L0A_P2 < 1:
                                v.wait_ge(s_st, st_b + 16 * h)
                            v.tensor_copy(qraw[:, :], qps[:, :]
                                          ).then_inc(s_ev, 1)
                            v.wait_ge(s_ev, ev.v + h + 1)
                            if L0A_P2 < 2:
                                vi = emit_rope128(v, qrope[:, :], qraw[:, :],
                                                  qcs[:, :], qsn[:, :],
                                                  qsc[:, :], TPC)
                            else:
                                vi = v.tensor_copy(qrope[:, :], qraw[:, :])
                            vi.then_inc(s_qd, 1)

                    ldW.inc(16 * 16)
                    mm.inc(16)
                    qd.inc(16)
                    ev.inc(16)
                    if L0A_P2 < 1:
                        st.inc(16 * 16)

        if L0A_PHASES < 3:
            raise _StopEmit()
        with (
            nc.sbuf_tensor("attnT", [128, 16 * 1024], BF16, side="right") as attnT,
            nc.sbuf_tensor("Wo_sb", [128, FC * D], BF16, side="right") as Wo_sb,
        ):
            with nc.Block() as blk:
                lw_wo = ldW.v

                @blk.gpsimd
                def _(g):
                    for dc in range(FC):
                        g.dma_start(Wo_sb[:, dc * D:(dc + 1) * D],
                                    Wo_d[:, dc * D:(dc + 1) * D]
                                    ).then_inc(s_lw, 16)
                ldW.inc(16 * 16)

            for qh in range(2):
                jmax = 8 * qh + 7
                qoff = 1024 * qh

                with contextlib.ExitStack() as _es3:
                    _e = _es3.enter_context
                    qTh = _e(nc.sbuf_tensor(f"qTh{qh}", [128, 2 * 1024], BF16, side="right"))
                    expT = _e(nc.sbuf_tensor(f"expT{qh}", [128, 2 * 1024], BF16, side="right"))
                    zinv = _e(nc.sbuf_tensor(f"zinv{qh}", [128, 1024], F32, side="right"))
                    sT = _e(nc.psum_tensor(f"sT{qh}", [128, 2048], F32))
                    aps = _e(nc.psum_tensor(f"aps{qh}", [128, 1024], F32))
                    zps = _e(nc.psum_tensor(f"zps{qh}", [128, 1024], F32))
                    with nc.Block() as blk:
                        @blk.sync
                        def _(s):
                            s.dma_start(qTh[:, 0:1024],
                                        qT_dram[0, :, qoff:qoff + 1024]
                                        ).then_inc(s_ld, 16)
                            s.wait_ge(s_ld, ldS.inc(16))

                    def _p3_head(h, qh, qoff, jmax, qTh, expT, zinv, sT,
                                 aps, zps):
                        kh = h // 4
                        chunks = [(True, kc) for kc in range(jmax + 1)] + \
                                 [(False, kc) for kc in range(jmax + 1)]
                        nch = len(chunks)

                        def qlo(kc):
                            return max(kc, 8 * qh) * 128

                        def subrange(kc):
                            c0 = qlo(kc) - qoff
                            return [s for s in range(2) if (s + 1) * 512 > c0]

                        first_t = {}
                        last_t = {}
                        for i, (own, kc) in enumerate(chunks):
                            for s in subrange(kc):
                                first_t.setdefault(s, i)
                                last_t[s] = i

                        with nc.Block() as blk:
                            mmb, exb, pvb = mm.v, ex.v, pv.v
                            ld_q = ldS.v

                            @blk.sync
                            def _(s):
                                if h + 1 < 16:
                                    s.dma_start(
                                        qTh[:, ((h + 1) % 2) * 1024:
                                            ((h + 1) % 2 + 1) * 1024],
                                        qT_dram[h + 1, :, qoff:qoff + 1024]
                                        ).then_inc(s_ld, 16)
                            if h + 1 < 16:
                                ldS.inc(16)

                            def emit_S(t, i, h=h, kh=kh, qoff=qoff):
                                own, kc = chunks[i]
                                half = i % 2
                                c0 = qlo(kc) - qoff
                                s0 = kc if own else 16 + kc
                                mi = None
                                for s in subrange(kc):
                                    a = max(c0, s * 512)
                                    b = (s + 1) * 512
                                    mi = t.matmul(
                                        sT[:, half * 1024 + a:half * 1024 + b],
                                        kT[:, kh * 4096 + s0 * 128:
                                           kh * 4096 + (s0 + 1) * 128],
                                        qTh[:, (h % 2) * 1024 + a:
                                            (h % 2) * 1024 + b],
                                        start=True, stop=True)
                                mi.then_inc(s_mm, 1)

                            def emit_ZP(t, i, kh=kh, qoff=qoff):
                                own, kc = chunks[i]
                                half = i % 2
                                c0 = qlo(kc) - qoff
                                s0 = kc if own else 16 + kc
                                for s in subrange(kc):
                                    a = max(c0, s * 512)
                                    b = (s + 1) * 512
                                    stf = (i == first_t[s])
                                    spf = (i == last_t[s])
                                    t.matmul(zps[:, a:b], ones[:, :],
                                             expT[:, half * 1024 + a:
                                                  half * 1024 + b],
                                             start=stf, stop=spf)
                                    mi = t.matmul(
                                        aps[:, a:b],
                                        v_sb[:, s0 * 512 + kh * 128:
                                             s0 * 512 + (kh + 1) * 128],
                                        expT[:, half * 1024 + a:
                                             half * 1024 + b],
                                        start=stf, stop=spf)
                                mi.then_inc(s_pv, 1)

                            @blk.tensor
                            def _(t):
                                t.wait_ge(s_ld, ld_q)
                                emit_S(t, 0)
                                emit_S(t, 1)
                                for i in range(nch):
                                    t.wait_ge(s_ex, exb + i + 1)
                                    emit_ZP(t, i)
                                    if i + 2 < nch:
                                        emit_S(t, i + 2)

                            @blk.scalar
                            def _(s):
                                for i, (own, kc) in enumerate(chunks):
                                    s.wait_ge(s_mm, mmb + i + 1)
                                    if i >= 2:
                                        s.wait_ge(s_pv, pvb + i - 1)
                                    half = i % 2
                                    c0 = qlo(kc) - qoff
                                    diag = (not own) and (kc >= 8 * qh)
                                    if diag:
                                        si = s.activation(
                                            expT[:, half * 1024 + c0:
                                                 half * 1024 + c0 + 128],
                                            sT[:, half * 1024 + c0:
                                               half * 1024 + c0 + 128],
                                            AF.Exp, scale=ISCALE,
                                            bias=pmsk[:, kc:kc + 1])
                                        if c0 + 128 < 1024:
                                            si = s.activation(
                                                expT[:, half * 1024 + c0 + 128:
                                                     half * 1024 + 1024],
                                                sT[:, half * 1024 + c0 + 128:
                                                   half * 1024 + 1024],
                                                AF.Exp, scale=ISCALE)
                                    else:
                                        si = s.activation(
                                            expT[:, half * 1024 + c0:
                                                 half * 1024 + 1024],
                                            sT[:, half * 1024 + c0:
                                               half * 1024 + 1024],
                                            AF.Exp, scale=ISCALE)
                                    si.then_inc(s_ex, 1)

                            evb = ev.v

                            @blk.vector
                            def _(v):
                                v.wait_ge(s_pv, pvb + nch)
                                v.reciprocal(zinv[:, :], zps[:, :]
                                             ).then_inc(s_ev, 1)
                                v.wait_ge(s_ev, evb + 1)
                                v.tensor_tensor(
                                    attnT[:, h * 1024:(h + 1) * 1024],
                                    aps[:, :], zinv[:, :], ALU.mult)
                            ev.inc(1)

                            mm.inc(nch)
                            ex.inc(nch)
                            pv.inc(nch)

                    for h in range(16):
                        _p3_head(h, qh, qoff, jmax, qTh, expT, zinv, sT,
                                 aps, zps)

                with contextlib.ExitStack() as _es4:
                    if L0A_PHASES < 4:
                        continue
                    _e = _es4.enter_context
                    xres = _e(nc.sbuf_tensor(f"xres{qh}", [128, 2 * D], F32, side="right"))
                    otile = _e(nc.sbuf_tensor(f"otile{qh}", [128, 2 * D], F32, side="right"))
                    ops0 = _e(nc.psum_tensor(f"ops0{qh}", [128, 2048], F32))
                    ops1 = _e(nc.psum_tensor(f"ops1{qh}", [128, 2048], F32))
                    with nc.Block() as blk:
                        ld_b = ldS.v
                        mm_b = mm.v
                        ev_b = ev.v
                        st_b = st.v

                        @blk.sync
                        def _(s):
                            s.dma_start(xres[:, 0:D],
                                        x_own[qoff:qoff + 128, :]
                                        ).then_inc(s_ld, 16)
                            s.dma_start(xres[:, D:2 * D],
                                        x_own[qoff + 128:qoff + 256, :]
                                        ).then_inc(s_ld, 16)
                            for tb in range(8):
                                s.wait_ge(s_ev, ev_b + tb + 1)
                                s.dma_start(
                                    x1_out[qoff + tb * 128:
                                           qoff + (tb + 1) * 128, :],
                                    otile[:, (tb % 2) * D:(tb % 2 + 1) * D]
                                    ).then_inc(s_st, 16)
                                if tb + 2 < 8:
                                    s.dma_start(
                                        xres[:, (tb % 2) * D:(tb % 2 + 1) * D],
                                        x_own[qoff + (tb + 2) * 128:
                                              qoff + (tb + 3) * 128, :]
                                        ).then_inc(s_ld, 16)
                            s.wait_ge(s_st, st_b + 16 * 8)
                        ldS.inc(16 * 8)
                        st.inc(16 * 8)

                        @blk.tensor
                        def _(t):
                            for tb in range(8):
                                ps = ops0 if tb % 2 == 0 else ops1
                                if tb >= 2:
                                    t.wait_ge(s_ev, ev_b + tb - 1)
                                mi = None
                                for dc in range(FC):
                                    t.wait_ge(s_lw, lw_wo + 16 * (dc + 1))
                                    for sc in range(4):
                                        mi = t.matmul(
                                            ps[:, sc * 512:(sc + 1) * 512],
                                            attnT[:, dc * 1024 + tb * 128:
                                                  dc * 1024 + (tb + 1) * 128],
                                            Wo_sb[:, dc * D + sc * 512:
                                                  dc * D + (sc + 1) * 512],
                                            start=(dc == 0), stop=(dc == FC - 1))
                                mi.then_inc(s_mm, 1)

                        @blk.vector
                        def _(v):
                            for tb in range(8):
                                ps = ops0 if tb % 2 == 0 else ops1
                                v.wait_ge(s_mm, mm_b + tb + 1)
                                v.wait_ge(s_ld, ld_b + 16 * (tb + 1))
                                if tb >= 2:
                                    v.wait_ge(s_st, st_b + 16 * (tb - 1))
                                v.tensor_tensor(
                                    otile[:, (tb % 2) * D:(tb % 2 + 1) * D],
                                    ps[:, :],
                                    xres[:, (tb % 2) * D:(tb % 2 + 1) * D],
                                    ALU.add).then_inc(s_ev, 1)

                        mm.inc(8)
                        ev.inc(8)

        with nc.Block() as blk:
            @blk.sync
            def _(s):
                s.wait_ge(s_ld, ldS.v)
                s.wait_ge(s_st, st.v)
                s.wait_ge(s_qd, qd.v)

            @blk.gpsimd
            def _(g):
                g.wait_ge(s_lw, ldW.v)
    return p


def l0a_inmaps(x_full, Wq, Wk, Wv, Wo):
    Wq_pre = f32a(pre_headmaj(Wq))
    Wk_pre = f32a(pre_fmaj(Wk))
    Wv_pre = f32a(pre_fmaj(Wv))
    Wo_pre = f32a(pre_fmaj(Wo))
    ident = np.eye(128, dtype=BF)
    onesb = np.ones((128, 128), dtype=BF)
    maps = []
    for c in range(8):
        qcs, qsn, kcs, ksn = l0_rope_tabs(c)
        maps.append({
            "x_own": np.ascontiguousarray(
                x_full[own_positions(c) + (c // 2) * T]),
            "x_oth": np.ascontiguousarray(
                x_full[own_positions(c ^ 1) + (c // 2) * T]),
            "Wq_pre": Wq_pre, "Wk_pre": Wk_pre, "Wv_pre": Wv_pre,
            "Wo_pre": Wo_pre,
            "qcos": qcs, "qsin": qsn, "kcos": kcs, "ksin": ksn,
            "pmask": l0_pmask(c), "ident": ident, "ones_bf": onesb,
        })
    return maps


def scatter_tokens(res_list, key):
    out = np.zeros((B * T, D), dtype=np.float32)
    for c in range(8):
        out[own_positions(c) + (c // 2) * T] = \
            np.asarray(res_list[c][key]).astype(np.float32)
    return out




GU_OT = 88
DN_IC = 44



def build_l0f(p=None, pfx=""):
    if p is None:
        p = Prog()
    p._pfx = pfx
    nc = _NCPfx(p.nc, pfx)
    x1_d = p.din("x1", (TPC, D), F32)
    pksh_d = p.din("gshard_pk", (17, 128, 256), I32)
    scsh_d = p.din("gshard_sc", (17, 128, 16), F16)
    dqc_d = p.din("dqconsts", (128, 4), F32)
    ident_d = p.din("ident", (128, 128), BF16)
    x2_d = p.dout("x2", (TPC, D), F32)
    emit_x2bf = "x2bf" not in p.override
    x2bf_d = p.dout("x2bf", (TPC, D), BF16) if emit_x2bf else None
    agin_w = p.dtmp("agin_w", (17, 128, D), BF16)
    agout_w = p.dtmp("agout_w", (136, 128, D), BF16)

    def agslot(j):
        return (j % 8) * 17 + j // 8

    ldS = Ctr(); ldW = Ctr(); pa = Ctr(); pb = Ctr(); mm = Ctr(); st = Ctr()
    ev = Ctr(); ccc = Ctr()

    with (
        nc.sbuf_tensor("idn", [128, 128], BF16) as idn,
        nc.sbuf_tensor("dqc", [128, 4], F32) as dqc,
        nc.sbuf_tensor("stats", [128, 8], F32) as stats,
        nc.semaphore("s_ld") as s_ld,
        nc.semaphore("s_lw") as s_lw,
        nc.semaphore("s_pa") as s_pa,
        nc.semaphore("s_pb") as s_pb,
        nc.semaphore("s_mm") as s_mm,
        nc.semaphore("s_st") as s_st,
        nc.semaphore("s_ev") as s_ev,
        nc.semaphore("s_cc") as s_cc,
    ):
        with nc.Block() as blk:
            @blk.sync
            def _(s):
                s.dma_start(idn[:, :], ident_d[:, :]).then_inc(s_ld, 16)
                s.dma_start(dqc[:, :], dqc_d[:, :]).then_inc(s_ld, 16)
                s.wait_ge(s_ld, ldS.inc(32))

        with nc.sbuf_tensor("hT", [128, FC * TPC], BF16) as hT:
            jobs = []
            for k in range(17):
                jobs.append((pksh_d[k, :, :], scsh_d[k, :, :], 16,
                             agin_w[k, :, :], D, 0))

            with contextlib.ExitStack() as es:
                _e = es.enter_context
                pk = _e(nc.sbuf_tensor("pk", [128, 2 * 256], I32, side="right"))
                sct = _e(nc.sbuf_tensor("sct", [128, 2 * 16], F16, side="right"))
                bufAi = _e(nc.sbuf_tensor("bufAi", [128, D], I32, side="right"))
                bufB = _e(nc.sbuf_tensor("bufB", [128, D], F32, side="right"))
                bufC = _e(nc.sbuf_tensor("bufC", [128, D], F32, side="right"))
                bufD = _e(nc.sbuf_tensor("bufD", [128, D], F32, side="right"))
                bufE = _e(nc.sbuf_tensor("bufE", [128, D], F32, side="right"))
                scf_t = _e(nc.sbuf_tensor("scf", [128, 32], F32, side="right"))
                deq = _e(nc.sbuf_tensor("deq", [128, 2 * D], BF16, side="right"))
                wT = _e(nc.sbuf_tensor("wT", [128, 2 * D], BF16, side="right"))
                tps0 = _e(nc.psum_tensor("tpsd0", [128, 1024], BF16))
                tps1 = _e(nc.psum_tensor("tpsd1", [128, 1024], BF16))

                with nc.Block() as blk:
                    pa0 = pa.v
                    pb0 = pb.v
                    mm0 = mm.v
                    ev0 = ev.v
                    ld0 = ldS.v
                    st0 = st.v
                    NJ = len(jobs)

                    @blk.sync
                    def _(s):
                        for jb, (pk_ap, sc_ap, ng, dst_ap, w, isdn) \
                                in enumerate(jobs):
                            ob = jb % 2
                            if jb >= 2:
                                s.wait_ge(s_pa, pa0 + 9 * (jb - 2) + 1)
                                s.wait_ge(s_pb, pb0 + jb - 1)
                            s.dma_start(pk[:, ob * 256:ob * 256 + 256],
                                        pk_ap).then_inc(s_ld, 16)
                            s.dma_start(sct[:, ob * 16:ob * 16 + 16],
                                        sc_ap).then_inc(s_ld, 16)
                            s.wait_ge(s_ev, ev0 + 16 * (jb + 1))
                            s.dma_start(dst_ap, wT[:, ob * D:ob * D + D]
                                        ).then_inc(s_st, 16)
                    ldS.inc(32 * NJ)
                    st.inc(16 * NJ)

                    @blk.gpsimd
                    def _(g):
                        for jb in range(NJ):
                            ob = jb % 2
                            so = ob * 16
                            g.wait_ge(s_ld, ld0 + 32 * (jb + 1))
                            if jb >= 2:
                                g.wait_ge(s_pa, pa0 + 9 * (jb - 2) + 9)
                            g.tensor_copy(scf_t[:, so:so + 16],
                                          sct[:, ob * 16:ob * 16 + 16]
                                          ).then_inc(s_pb, 1)
                    pb.v = pb0 + NJ

                    @blk.vector
                    def _(v):
                        v.memset(bufAi[:, :], 0)
                        for jb in range(NJ):
                            ob = jb % 2
                            so = ob * 16

                            def unpack(v, ob=ob):
                                vi = None
                                for k in range(8):
                                    vi = v.tensor_scalar(
                                        bufAi[:, k:D:8],
                                        pk[:, ob * 256:ob * 256 + 256],
                                        4 * k, 15,
                                        ALU.logical_shift_right,
                                        ALU.bitwise_and)
                                return vi

                            def scale(v, ob=ob, so=so, jb=jb):
                                v.wait_ge(s_pb, pb0 + jb + 1)
                                vi = None
                                for g2 in range(16):
                                    vi = v.scalar_tensor_tensor(
                                        deq[:, ob * D + g2 * 128:
                                            ob * D + (g2 + 1) * 128],
                                        bufB[:, g2 * 128:(g2 + 1) * 128],
                                        scf_t[:, so + g2:so + g2 + 1],
                                        bufD[:, g2 * 128:(g2 + 1) * 128],
                                        ALU.mult, ALU.mult)
                                return vi

                            pw = ([(s_mm, mm0 + 16 * (jb - 1))]
                                  if jb >= 2 else [])
                            emit_dq_vec(v, s_pa, pa0 + 9 * jb, s_ld,
                                        ld0 + 32 * (jb + 1), bufAi, bufB,
                                        bufC, bufD, bufE, D, unpack, scale,
                                        prewaits=pw)
                            if jb >= 2:
                                v.wait_ge(s_st, st0 + 16 * (jb - 1))
                            for fc in range(FC):
                                k = 16 * jb + fc
                                ps = tps0 if fc % 2 == 0 else tps1
                                v.wait_ge(s_mm, mm0 + k + 1)
                                v.tensor_copy(wT[:, ob * D + fc * 128:
                                                 ob * D + (fc + 1) * 128],
                                              ps[:, 0:128]).then_inc(s_ev, 1)
                    pa.v = pa0 + 9 * NJ
                    ev.inc(16 * NJ)

                    @blk.scalar
                    def _(s):
                        for jb in range(NJ):
                            emit_dq_sc(s, s_pa, pa0 + 9 * jb, bufAi, bufB,
                                       bufC, bufD, bufE, dqc, D)

                    @blk.tensor
                    def _(t):
                        for jb in range(NJ):
                            ob = jb % 2
                            t.wait_ge(s_pa, pa0 + 9 * jb + 9)
                            for fc in range(FC):
                                k = 16 * jb + fc
                                ps = tps0 if fc % 2 == 0 else tps1
                                if k >= 2:
                                    t.wait_ge(s_ev, ev0 + k - 1)
                                t.transpose(ps[:, 0:128],
                                            deq[:, ob * D + fc * 128:
                                                ob * D + (fc + 1) * 128],
                                            idn[:, :]).then_inc(s_mm, 1)
                    mm.inc(16 * NJ)


            with nc.Block() as blk:
                @blk.gpsimd
                def _(g):
                    g.wait_ge(s_st, st.v)
                    g.collective_compute(
                        "AllGather", ALU.bypass,
                        replica_groups=[[0, 1, 2, 3, 4, 5, 6, 7]],
                        ins=[agin_w.rearrange("k p n -> (k p) n")],
                        outs=[agout_w.rearrange("k p n -> (k p) n")],
                    ).then_inc(s_cc, 1)
                ccc.inc(1)
            if L0F_SERIAL_CC:
                with nc.Block() as blk:
                    @blk.gpsimd
                    def _(g):
                        g.wait_ge(s_cc, ccc.v)
            if L0F_DEBUG:
                dbg_agw = p.dout("dbg_agw", (136, 128, D), BF16)
                with nc.Block() as blk:
                    @blk.sync
                    def _(s):
                        s.wait_ge(s_cc, ccc.v)
                        s.dma_start(dbg_agw, agout_w).then_inc(s_st, 16)
                        s.wait_ge(s_st, st.inc(16))

            with contextlib.ExitStack() as es:
                _e = es.enter_context
                xt = _e(nc.sbuf_tensor("xt", [128, 2 * D], F32, side="right"))
                htile = _e(nc.sbuf_tensor("htile", [128, 2 * D], BF16,
                                          side="right"))
                junk = _e(nc.sbuf_tensor("junk", [128, D], BF16, side="right"))
                lstats = _e(nc.sbuf_tensor("lstats", [128, 16], F32,
                                           side="right"))
                tps0 = _e(nc.psum_tensor("tps0", [128, 1024], BF16))
                tps1 = _e(nc.psum_tensor("tps1", [128, 1024], BF16))
                emit_ln_phase(nc, idn, x1_d, hT, xt, htile, junk, lstats,
                              tps0, tps1, s_ld, s_pa, s_mm, s_pb,
                              ldS, pa, mm, pb)

            with contextlib.ExitStack() as es:
                _e = es.enter_context
                QW = 512
                mT = _e(nc.sbuf_tensor("mT", [128, DN_IC * QW], BF16,
                                       side="right"))
                wg = _e(nc.sbuf_tensor("wg", [128, 2 * D], BF16, side="right"))
                wu = _e(nc.sbuf_tensor("wu", [128, 2 * D], BF16, side="right"))
                wdh = _e(nc.sbuf_tensor("wdh", [128, 2 * 5632], BF16,
                                        side="right"))
                gs = _e(nc.sbuf_tensor("gs", [128, 2 * QW], BF16,
                                       side="right"))
                ybuf = _e(nc.sbuf_tensor("ybuf", [128, 4 * D], BF16,
                                         side="right"))
                ysb = _e(nc.sbuf_tensor("ysb", [128, 2 * QW], BF16,
                                        side="right"))
                x1r = _e(nc.sbuf_tensor("x1r", [128, D], F32, side="right"))
                x2t = _e(nc.sbuf_tensor("x2t", [128, D], F32, side="right"))
                x2b = _e(nc.sbuf_tensor("x2b", [128, D], BF16, side="right"))
                junk2 = _e(nc.sbuf_tensor("junk2", [128, D], BF16,
                                          side="right"))

                for th in range(4):
                    toff = th * QW
                    es1 = contextlib.ExitStack()
                    gps = es1.enter_context(
                        nc.psum_tensor(f"gps{th}", [128, 2 * QW], F32))
                    ups = es1.enter_context(
                        nc.psum_tensor(f"ups{th}", [128, 2 * QW], F32))
                    with nc.Block() as blk:
                        base = pa.v
                        pb0 = pb.v
                        mm0 = mm.v
                        lw0 = ldW.v

                        @blk.gpsimd
                        def _(g):
                            if th == 0:
                                g.wait_ge(s_cc, ccc.v)
                            g.dma_start(wg[:, 0:D],
                                        agout_w[agslot(0), :, :]
                                        ).then_inc(s_lw, 16)
                            g.dma_start(wu[:, 0:D],
                                        agout_w[agslot(DN_IC), :, :]
                                        ).then_inc(s_lw, 16)
                            for gi in range(DN_IC - 1):
                                sl = (gi + 1) % 2
                                if gi >= 1:
                                    g.wait_ge(s_mm, mm0 + gi)
                                g.dma_start(wg[:, sl * D:(sl + 1) * D],
                                            agout_w[agslot(gi + 1), :, :]
                                            ).then_inc(s_lw, 16)
                                g.dma_start(wu[:, sl * D:(sl + 1) * D],
                                            agout_w[agslot(DN_IC + gi + 1),
                                                    :, :]
                                            ).then_inc(s_lw, 16)
                        ldW.inc(32 * DN_IC)

                        @blk.tensor
                        def _(t):
                            for gi in range(DN_IC):
                                sl = gi % 2
                                t.wait_ge(s_lw, lw0 + 32 * (gi + 1))
                                if gi >= 2:
                                    t.wait_ge(s_pb, pb0 + gi - 1)
                                mi = None
                                for fc in range(FC):
                                    t.matmul(
                                        gps[:, sl * QW:(sl + 1) * QW],
                                        wg[:, sl * D + fc * 128:
                                           sl * D + (fc + 1) * 128],
                                        hT[:, fc * TPC + toff:
                                           fc * TPC + toff + QW],
                                        start=(fc == 0), stop=(fc == FC - 1))
                                    mi = t.matmul(
                                        ups[:, sl * QW:(sl + 1) * QW],
                                        wu[:, sl * D + fc * 128:
                                           sl * D + (fc + 1) * 128],
                                        hT[:, fc * TPC + toff:
                                           fc * TPC + toff + QW],
                                        start=(fc == 0), stop=(fc == FC - 1))
                                mi.then_inc(s_mm, 1)
                        mm.inc(DN_IC)

                        @blk.scalar
                        def _(s):
                            for gi in range(DN_IC):
                                sl = gi % 2
                                s.wait_ge(s_mm, mm0 + gi + 1)
                                s.activation(gs[:, sl * QW:(sl + 1) * QW],
                                             gps[:, sl * QW:(sl + 1) * QW],
                                             AF.Silu).then_inc(s_pa, 1)
                        pa.v = base + DN_IC

                        @blk.vector
                        def _(v):
                            for gi in range(DN_IC):
                                sl = gi % 2
                                v.wait_ge(s_pa, base + gi + 1)
                                v.tensor_tensor(mT[:, gi * QW:(gi + 1) * QW],
                                                gs[:, sl * QW:(sl + 1) * QW],
                                                ups[:, sl * QW:(sl + 1) * QW],
                                                ALU.mult).then_inc(s_pb, 1)
                        pb.v = pb0 + DN_IC

                    es1.close()
                    es2 = contextlib.ExitStack()
                    yp0 = es2.enter_context(
                        nc.psum_tensor(f"yp0{th}", [128, QW], F32))
                    yp1 = es2.enter_context(
                        nc.psum_tensor(f"yp1{th}", [128, QW], F32))
                    yt0 = es2.enter_context(
                        nc.psum_tensor(f"yt0{th}", [128, 1024], BF16))
                    yt1 = es2.enter_context(
                        nc.psum_tensor(f"yt1{th}", [128, 1024], BF16))
                    with nc.Block() as blk:
                        pa0 = pa.v
                        pb0 = pb.v
                        mm0 = mm.v
                        lw0 = ldW.v
                        ev0 = ev.v

                        @blk.gpsimd
                        def _(g):
                            def wdh_load(hot, hb):
                                mi = None
                                for sl3 in range(3):
                                    w3 = 2048 if sl3 < 2 else 1536
                                    mi = g.dma_start(
                                        wdh[:, hb * 5632 + sl3 * 2048:
                                            hb * 5632 + sl3 * 2048 + w3],
                                        agout_w[agslot(88 + hot * 3 + sl3),
                                                :, 0:w3])
                                    mi.then_inc(s_lw, 16)
                            wdh_load(0, 0)
                            for hot in range(15):
                                sl = (hot + 1) % 2
                                if hot >= 1:
                                    g.wait_ge(s_mm, mm0 + hot)
                                wdh_load(hot + 1, sl)
                        ldW.inc(48 * 16)

                        @blk.tensor
                        def _(t):
                            for hot in range(16):
                                hb = hot % 2
                                yp = yp0 if hb == 0 else yp1
                                t.wait_ge(s_lw, lw0 + 48 * (hot + 1))
                                if hot >= 2:
                                    t.wait_ge(s_pa, pa0 + hot - 1)
                                mi = None
                                for ic in range(DN_IC):
                                    sl = ic // 16
                                    fcs = ic % 16
                                    stat = wdh[:, hb * 5632 + sl * 2048 +
                                               fcs * 128:
                                               hb * 5632 + sl * 2048 +
                                               (fcs + 1) * 128]
                                    mi = t.matmul(
                                        yp[:, :], stat,
                                        mT[:, ic * QW:(ic + 1) * QW],
                                        start=(ic == 0),
                                        stop=(ic == DN_IC - 1))
                                mi.then_inc(s_mm, 1)
                                t.wait_ge(s_pa, pa0 + hot + 1)
                                for u in range(4):
                                    ps = yt0 if u % 2 == 0 else yt1
                                    k = hot * 4 + u
                                    if k >= 2:
                                        t.wait_ge(s_pb, pb0 + k - 1)
                                    t.transpose(ps[:, 0:128],
                                                ysb[:, hb * QW + u * 128:
                                                    hb * QW + (u + 1) * 128],
                                                idn[:, :]).then_inc(s_ev, 1)
                        mm.inc(16)
                        ev.inc(64)

                        @blk.vector
                        def _(v):
                            for hot in range(16):
                                hb = hot % 2
                                yp = yp0 if hb == 0 else yp1
                                v.wait_ge(s_mm, mm0 + hot + 1)
                                if hot >= 2:
                                    v.wait_ge(s_ev, ev0 + 4 * hot - 4)
                                v.tensor_copy(ysb[:, hb * QW:(hb + 1) * QW],
                                              yp[:, :]).then_inc(s_pa, 1)
                                for u in range(4):
                                    ps = yt0 if u % 2 == 0 else yt1
                                    v.wait_ge(s_ev, ev0 + 4 * hot + u + 1)
                                    v.tensor_copy(
                                        ybuf[:, u * D + hot * 128:
                                             u * D + (hot + 1) * 128],
                                        ps[:, 0:128]).then_inc(s_pb, 1)
                        pa.v = pa0 + 16
                        pb.v = pb0 + 64

                    for u in range(4):
                        with nc.Block() as blk:
                            base = pa.v
                            ev0 = ev.v
                            st0 = st.v
                            ld0 = ldS.v

                            @blk.sync
                            def _(s):
                                s.dma_start(
                                    x1r[:, :],
                                    x1_d[toff + u * 128:toff + (u + 1) * 128, :]
                                    ).then_inc(s_ld, 16)
                                s.wait_ge(s_ev, ev0 + 2)
                                s.dma_start(
                                    x2_d[toff + u * 128:toff + (u + 1) * 128, :],
                                    x2t[:, :]).then_inc(s_st, 16)
                                if emit_x2bf:
                                    s.wait_ge(s_ev, ev0 + 3)
                                    s.dma_start(
                                        x2bf_d[toff + u * 128:
                                               toff + (u + 1) * 128, :],
                                        x2b[:, :]).then_inc(s_st, 16)
                                s.wait_ge(s_st, st0 + (32 if emit_x2bf
                                                       else 16))
                            ldS.inc(16)
                            st.inc(32 if emit_x2bf else 16)

                            @blk.scalar
                            def _(s):
                                s.activation(junk2[:, :],
                                             ybuf[:, u * D:(u + 1) * D],
                                             AF.Square,
                                             accum_out=stats[:, 0:1]
                                             ).then_inc(s_pa, 1)
                                s.wait_ge(s_pa, base + 2)
                                s.activation(stats[:, 2:3], stats[:, 1:2],
                                             AF.Sqrt).then_inc(s_pa, 1)

                            @blk.vector
                            def _(v):
                                v.wait_ge(s_ld, ld0 + 16)
                                v.wait_ge(s_pa, base + 1)
                                v.tensor_scalar(stats[:, 1:2], stats[:, 0:1],
                                                1.0 / D, 1e-6, ALU.mult,
                                                ALU.add).then_inc(s_pa, 1)
                                v.wait_ge(s_pa, base + 3)
                                v.reciprocal(stats[:, 3:4], stats[:, 2:3]
                                             ).then_inc(s_ev, 1)
                                v.wait_ge(s_ev, ev0 + 1)
                                v.tensor_scalar(x2t[:, :],
                                                ybuf[:, u * D:(u + 1) * D],
                                                stats[:, 3:4], None, ALU.mult)
                                v.tensor_tensor(x2t[:, :], x2t[:, :],
                                                x1r[:, :],
                                                ALU.add).then_inc(s_ev, 1)
                                if emit_x2bf:
                                    v.wait_ge(s_ev, ev0 + 2)
                                    v.tensor_copy(x2b[:, :], x2t[:, :]
                                                  ).then_inc(s_ev, 1)
                            pa.v = base + 3
                            ev.inc(3 if emit_x2bf else 2)
                    es2.close()

        with nc.Block() as blk:
            @blk.sync
            def _(s):
                s.wait_ge(s_ld, ldS.v)
                s.wait_ge(s_st, st.v)

            @blk.gpsimd
            def _(g):
                g.wait_ge(s_lw, ldW.v)
    return p


def l0f_inmaps(x1_full, gup, gus, dpk, dsc):
    gusT = np.ascontiguousarray(gus.T)
    dsT = np.ascontiguousarray(dsc.T)
    dqc = np.zeros((128, 4), dtype=np.float32)
    dqc[:, 0] = -4.0
    dqc[:, 1] = -6.0
    dqc[:, 2] = 3.5
    ident = np.eye(128, dtype=BF)
    maps = []
    for c in range(8):
        pk_sh = np.zeros((17, 128, 256), np.int32)
        sc_sh = np.zeros((17, 128, 16), np.float16)
        for k in range(17):
            j = c + 8 * k
            if j < GU_OT:
                pk_sh[k] = gup[j * 128:(j + 1) * 128, :]
                sc_sh[k] = gusT[j * 128:(j + 1) * 128, :]
            else:
                jd = j - GU_OT
                hot, sl = jd // 3, jd % 3
                w = 2048 if sl < 2 else 1536
                pk_sh[k, :, 0:w // 8] = dpk[hot * 128:(hot + 1) * 128,
                                            sl * 256:sl * 256 + w // 8]
                sc_sh[k, :, 0:w // 128] = dsT[hot * 128:(hot + 1) * 128,
                                              sl * 16:sl * 16 + w // 128]
        maps.append({
            "x1": np.ascontiguousarray(
                x1_full[own_positions(c) + (c // 2) * T]),
            "gshard_pk": pk_sh, "gshard_sc": sc_sh,
            "dqconsts": dqc, "ident": ident,
        })
    return maps



def build_fused():
    p = Prog()
    x1h = p.dtmp("x1h", (TPC, D), F32)
    x2h = p.dtmp("x2h", (TPC, D), F32)
    x2bfh = p.dtmp("x2bfh", (TPC, D), BF16)
    x3h = p.dtmp("x3h", (TPC, D), F32)
    p.override = {"x1": x1h, "x2": x2h, "x2bf": x2bfh, "x3": x3h}
    build_l0a(p, "A_")
    build_l0f(p, "B_")
    build_l1a(p, "C_")
    build_l1f(p, "D_")
    return p


def fused_inmaps(inputs):
    x = np.asarray(inputs["L0_attn_x"], dtype=np.float32).reshape(B * T, D)
    zeros = np.zeros((B * T, D), np.float32)
    stage_maps = [
        l0a_inmaps(x, f32a(inputs["L0_attn_Wq"]),
                   f32a(inputs["L0_attn_Wk"]),
                   f32a(inputs["L0_attn_Wv"]),
                   f32a(inputs["L0_attn_Wo"])),
        l0f_inmaps(zeros, inputs["L0_ffn_gate_up_packed"],
                   inputs["L0_ffn_gate_up_scales"],
                   inputs["L0_ffn_down_packed"],
                   inputs["L0_ffn_down_scales"]),
        l1a_inmaps(zeros, inputs),
        l1f_inmaps(zeros, f32a(inputs["L1_ffn_gate_w"]),
                   f32a(inputs["L1_ffn_gate_b"]),
                   f32a(inputs["L1_ffn_w1_shared"]),
                   f32a(inputs["L1_ffn_w2_shared"]),
                   f32a(inputs["L1_ffn_w3_shared"]),
                   f32a(inputs["L1_ffn_w1_routed"]),
                   f32a(inputs["L1_ffn_w2_routed"]),
                   f32a(inputs["L1_ffn_w3_routed"])),
    ]
    maps = []
    for c in range(8):
        d = {}
        for m in stage_maps:
            for k, v in m[c].items():
                if k in ("x1", "x2", "x3"):
                    continue
                if k not in d:
                    d[k] = v
        maps.append(d)
    return maps


def kernel(**inputs):
    p = build_fused()
    maps = fused_inmaps(inputs)
    res = run_bass_kernel_spmd(p.nc, maps, core_ids=list(range(8)))
    x4 = scatter_tokens(res.results, "x4")
    return x4.reshape(B, T, D)



GU_OT = 88
DN_IC = 44



def emit_dq_vec(v, sem, base, s_ld, ld_target, bufAi, bufB, bufC, bufD,
                bufE, width, unpack_fn, scale_fn, prewaits=()):
    v.wait_ge(s_ld, ld_target)
    for ps_, pv_ in prewaits:
        v.wait_ge(ps_, pv_)
    unpack_fn(v).then_inc(sem, 1)
    v.wait_ge(sem, base + 3)
    v.scalar_tensor_tensor(bufC[:, 0:width], bufD[:, 0:width], 4.0,
                           bufB[:, 0:width], ALU.mult,
                           ALU.add).then_inc(sem, 1)
    v.wait_ge(sem, base + 6)
    v.scalar_tensor_tensor(bufB[:, 0:width], bufC[:, 0:width], 0.5,
                           bufB[:, 0:width], ALU.mult,
                           ALU.add).then_inc(sem, 1)
    v.wait_ge(sem, base + 7)
    v.scalar_tensor_tensor(bufB[:, 0:width], bufE[:, 0:width], 0.5,
                           bufB[:, 0:width], ALU.mult,
                           ALU.add).then_inc(sem, 1)
    v.wait_ge(sem, base + 8)
    scale_fn(v).then_inc(sem, 1)


def emit_dq_sc(s, sem, base, bufAi, bufB, bufC, bufD, bufE, dqc, width):
    s.wait_ge(sem, base + 1)
    s.activation(bufB[:, 0:width], bufAi[:, 0:width], AF.Identity,
                 bias=dqc[:, 0:1]).then_inc(sem, 1)
    s.activation(bufD[:, 0:width], bufB[:, 0:width], AF.Sign,
                 scale=-1.0, bias=dqc[:, 2:3]).then_inc(sem, 1)
    s.wait_ge(sem, base + 4)
    s.activation(bufE[:, 0:width], bufC[:, 0:width], AF.Relu,
                 bias=dqc[:, 0:1]).then_inc(sem, 1)
    s.activation(bufB[:, 0:width], bufC[:, 0:width], AF.Relu,
                 bias=dqc[:, 1:2]).then_inc(sem, 1)


def emit_dq(blk, sem, c, s_ld, s_st, st_war, bufAi, bufB, bufC, bufD, bufE,
            dqc, out_ap, width, unpack_fn, scale_fn):
    base = c.v

    @blk.vector
    def _(v):
        v.wait_ge(s_ld, emit_dq.ld_target)
        unpack_fn(v).then_inc(sem, 1)
        v.wait_ge(sem, base + 3)
        v.scalar_tensor_tensor(bufC[:, 0:width], bufD[:, 0:width], 4.0,
                               bufB[:, 0:width], ALU.mult,
                               ALU.add).then_inc(sem, 1)
        v.wait_ge(sem, base + 6)
        v.scalar_tensor_tensor(bufB[:, 0:width], bufC[:, 0:width], 0.5,
                               bufB[:, 0:width], ALU.mult,
                               ALU.add).then_inc(sem, 1)
        v.wait_ge(sem, base + 7)
        v.scalar_tensor_tensor(bufB[:, 0:width], bufE[:, 0:width], 0.5,
                               bufB[:, 0:width], ALU.mult,
                               ALU.add).then_inc(sem, 1)
        v.wait_ge(sem, base + 8)
        if st_war is not None:
            v.wait_ge(s_st, st_war)
        scale_fn(v).then_inc(sem, 1)

    @blk.scalar
    def _(s):
        s.wait_ge(sem, base + 1)
        s.activation(bufB[:, 0:width], bufAi[:, 0:width], AF.Identity,
                     bias=dqc[:, 0:1]).then_inc(sem, 1)
        s.activation(bufD[:, 0:width], bufB[:, 0:width], AF.Sign,
                     scale=-1.0, bias=dqc[:, 2:3]).then_inc(sem, 1)
        s.wait_ge(sem, base + 4)
        s.activation(bufE[:, 0:width], bufC[:, 0:width], AF.Relu,
                     bias=dqc[:, 0:1]).then_inc(sem, 1)
        s.activation(bufB[:, 0:width], bufC[:, 0:width], AF.Relu,
                     bias=dqc[:, 1:2]).then_inc(sem, 1)
    c.v = base + 9




def build_l1f(p=None, pfx=""):
    if p is None:
        p = Prog()
    p._pfx = pfx
    nc = _NCPfx(p.nc, pfx)
    x3_d = p.din("x3", (TPC, D), F32)
    gwT_d = p.din("gwT_in", (128, FC * 16), F32)
    gb_d = p.din("gb_rep", (128, 16), F32)
    w1_d = p.din("w1_pre", (17, 11, 128, D), F32)
    w3_d = p.din("w3_pre", (17, 11, 128, D), F32)
    w2_d = p.din("w2_cat", (17, 1408, D), F32)
    ident_d = p.din("ident", (128, 128), BF16)
    x4_d = p.dout("x4", (TPC, D), F32)

    CI = 11
    ldS = Ctr(); ldW = Ctr(); pa = Ctr(); pb = Ctr(); mm = Ctr(); st = Ctr()
    ev = Ctr(); ac = Ctr()

    with (
        nc.sbuf_tensor("idn", [128, 128], BF16) as idn,
        nc.sbuf_tensor("gwT", [128, FC * 16], BF16) as gwT,
        nc.sbuf_tensor("gb", [128, 16], F32) as gb,
        nc.sbuf_tensor("stats", [128, 8], F32) as stats,
        nc.sbuf_tensor("t8", [128, 8], F32) as t8,
        nc.sbuf_tensor("t16", [128, 16], F32) as t16,
        nc.sbuf_tensor("rp", [128, 16 * 16], F32) as rp,
        nc.semaphore("s_ld") as s_ld,
        nc.semaphore("s_lw") as s_lw,
        nc.semaphore("s_pa") as s_pa,
        nc.semaphore("s_pb") as s_pb,
        nc.semaphore("s_mm") as s_mm,
        nc.semaphore("s_st") as s_st,
        nc.semaphore("s_ac") as s_ac,
    ):
        with nc.Block() as blk:
            @blk.sync
            def _(s):
                s.dma_start(idn[:, :], ident_d[:, :]).then_inc(s_ld, 16)
                s.dma_start(gb[:, :], gb_d[:, :]).then_inc(s_ld, 16)
                s.wait_ge(s_ld, ldS.inc(32))

            @blk.gpsimd
            def _(g):
                g.dma_start(gwT[:, :], gwT_d[:, :]).then_inc(s_lw, 16)
                g.dma_start(x4_d[:, :], x3_d[:, :]).then_inc(s_ac, 16)
                g.wait_ge(s_lw, ldW.inc(16))
                g.wait_ge(s_ac, ac.inc(16))

        with nc.sbuf_tensor("hT", [128, FC * TPC], BF16) as hT:
            with contextlib.ExitStack() as es:
                _e = es.enter_context
                xt = _e(nc.sbuf_tensor("xt", [128, 2 * D], F32, side="right"))
                htile = _e(nc.sbuf_tensor("htile", [128, 2 * D], BF16,
                                          side="right"))
                junk = _e(nc.sbuf_tensor("junk", [128, D], BF16, side="right"))
                lstats = _e(nc.sbuf_tensor("lstats", [128, 16], F32,
                                           side="right"))
                tps0 = _e(nc.psum_tensor("tps0", [128, 1024], BF16))
                tps1 = _e(nc.psum_tensor("tps1", [128, 1024], BF16))
                rps = _e(nc.psum_tensor("rps", [128, 512], F32))
                emit_ln_phase(nc, idn, x3_d, hT, xt, htile, junk, lstats,
                              tps0, tps1, s_ld, s_pa, s_mm, s_pb,
                              ldS, pa, mm, pb)

                for tb in range(16):
                    with nc.Block() as blk:
                        base = pa.v

                        @blk.tensor
                        def _(t):
                            mi = None
                            for fc in range(FC):
                                mi = t.matmul(rps[:, 0:16],
                                              hT[:, fc * TPC + tb * 128:
                                                 fc * TPC + (tb + 1) * 128],
                                              gwT[:, fc * 16:(fc + 1) * 16],
                                              start=(fc == 0),
                                              stop=(fc == FC - 1))
                            mi.then_inc(s_mm, 1)
                        mm.inc(1)

                        @blk.scalar
                        def _(s):
                            s.wait_ge(s_mm, mm.v)
                            s.activation(rp[:, tb * 16:(tb + 1) * 16],
                                         rps[:, 0:16],
                                         AF.Sigmoid).then_inc(s_pa, 1)

                        @blk.vector
                        def _(v):
                            R = rp[:, tb * 16:(tb + 1) * 16]
                            v.wait_ge(s_pa, base + 1)
                            v.tensor_tensor(R, R, gb[:, :],
                                            ALU.add).then_inc(s_pa, 1)
                            v.wait_ge(s_pa, base + 2)
                            v.max(out=t8[:, 0:8], in_=R).then_inc(s_pa, 1)
                            v.wait_ge(s_pa, base + 3)
                            v.memset(t8[:, 4:8], -1e30).then_inc(s_pa, 1)
                            v.wait_ge(s_pa, base + 4)
                            v.match_replace(out=t16[:, 0:16],
                                            in_to_replace=t8[:, 0:8],
                                            in_values=R,
                                            imm_value=-1e30).then_inc(s_pa, 1)
                            v.wait_ge(s_pa, base + 5)
                            v.tensor_scalar(t16[:, 0:16], t16[:, 0:16],
                                            -1e30, None,
                                            ALU.is_equal).then_inc(s_pa, 1)
                            v.wait_ge(s_pa, base + 6)
                            v.tensor_tensor(R, R, t16[:, 0:16],
                                            ALU.mult).then_inc(s_pa, 1)
                            v.wait_ge(s_pa, base + 7)
                            v.reduce_sum(stats[:, 0:1], R,
                                         axis=AX.X).then_inc(s_pa, 1)
                            v.wait_ge(s_pa, base + 8)
                            v.reciprocal(stats[:, 1:2], stats[:, 0:1]
                                         ).then_inc(s_pa, 1)
                            v.wait_ge(s_pa, base + 9)
                            v.tensor_scalar(R, R, stats[:, 1:2], None,
                                            ALU.mult).then_inc(s_pa, 1)
                        pa.v = base + 10

            with contextlib.ExitStack() as es:
                _e = es.enter_context
                mT = _e(nc.sbuf_tensor("mT", [128, CI * TPC], BF16,
                                       side="right"))
                w1c = _e(nc.sbuf_tensor("w1c", [128, 2 * D], BF16, side="right"))
                w3c = _e(nc.sbuf_tensor("w3c", [128, 2 * D], BF16, side="right"))
                w2sb = _e(nc.sbuf_tensor("w2sb", [128, CI * D], BF16,
                                         side="right"))
                gsb = _e(nc.sbuf_tensor("gsb", [128, 2048], BF16, side="right"))
                gsb2 = _e(nc.sbuf_tensor("gsb2", [128, 2048], BF16, side="right"))
                tmp0 = _e(nc.sbuf_tensor("tmp0", [128, D], F32, side="right"))
                tmp1 = _e(nc.sbuf_tensor("tmp1", [128, D], F32, side="right"))

                with nc.Block() as blk:
                    @blk.gpsimd
                    def _(g):
                        g.dma_start(w1c[:, 0:D], w1_d[0, 0, :, :]
                                    ).then_inc(s_lw, 16)
                        g.dma_start(w3c[:, 0:D], w3_d[0, 0, :, :]
                                    ).then_inc(s_lw, 16)
                    ldW.inc(32)

                for e in range(17):
                    esm = contextlib.ExitStack()
                    g1p = esm.enter_context(
                        nc.psum_tensor(f"g1p{e}", [128, 2048], F32))
                    g3p = esm.enter_context(
                        nc.psum_tensor(f"g3p{e}", [128, 2048], F32))
                    with nc.Block() as blk:
                        base = pa.v
                        pb0 = pb.v
                        mm0 = mm.v
                        lw0 = ldW.v

                        @blk.gpsimd
                        def _(g):
                            for cc in range(1, CI):
                                ob = cc % 2
                                if cc >= 2:
                                    g.wait_ge(s_mm, mm0 + 2 * cc - 2)
                                g.dma_start(w1c[:, ob * D:(ob + 1) * D],
                                            w1_d[e, cc, :, :]
                                            ).then_inc(s_lw, 16)
                                g.dma_start(w3c[:, ob * D:(ob + 1) * D],
                                            w3_d[e, cc, :, :]
                                            ).then_inc(s_lw, 16)
                            for c2 in range(CI):
                                g.dma_start(
                                    w2sb[:, c2 * D:(c2 + 1) * D],
                                    w2_d[e, c2 * 128:(c2 + 1) * 128, :]
                                    ).then_inc(s_lw, 16)
                        ldW.inc(32 * (CI - 1) + 16 * CI)

                        @blk.tensor
                        def _(t):
                            for j in range(2 * CI):
                                cc = j // 2
                                qh = j % 2
                                ob = cc % 2
                                if qh == 0:
                                    t.wait_ge(s_lw, lw0 + 32 * cc)
                                if j >= 2:
                                    t.wait_ge(s_pb, pb0 + j - 1)
                                mi = None
                                for fc in range(FC):
                                    for q2 in range(2):
                                        co = qh * 1024 + q2 * 512
                                        t.matmul(
                                            g1p[:, co:co + 512],
                                            w1c[:, ob * D + fc * 128:
                                                ob * D + (fc + 1) * 128],
                                            hT[:, fc * TPC + co:
                                               fc * TPC + co + 512],
                                            start=(fc == 0),
                                            stop=(fc == FC - 1))
                                        mi = t.matmul(
                                            g3p[:, co:co + 512],
                                            w3c[:, ob * D + fc * 128:
                                                ob * D + (fc + 1) * 128],
                                            hT[:, fc * TPC + co:
                                               fc * TPC + co + 512],
                                            start=(fc == 0),
                                            stop=(fc == FC - 1))
                                mi.then_inc(s_mm, 1)
                        mm.inc(2 * CI)

                        if e == 0:
                            @blk.scalar
                            def _(s):
                                for j in range(2 * CI):
                                    qh = j % 2
                                    s.wait_ge(s_mm, mm0 + j + 1)
                                    s.activation(
                                        gsb[:, qh * 1024:(qh + 1) * 1024],
                                        g1p[:, qh * 1024:(qh + 1) * 1024],
                                        AF.Silu).then_inc(s_pa, 1)

                            @blk.vector
                            def _(v):
                                for j in range(2 * CI):
                                    cc = j // 2
                                    qh = j % 2
                                    v.wait_ge(s_pa, base + j + 1)
                                    v.tensor_tensor(
                                        mT[:, cc * TPC + qh * 1024:
                                           cc * TPC + (qh + 1) * 1024],
                                        gsb[:, qh * 1024:(qh + 1) * 1024],
                                        g3p[:, qh * 1024:(qh + 1) * 1024],
                                        ALU.mult).then_inc(s_pb, 1)
                            pa.v = base + 2 * CI
                            pb.v = pb0 + 2 * CI
                        else:
                            @blk.scalar
                            def _(s):
                                for j in range(2 * CI):
                                    cc = j // 2
                                    qh = j % 2
                                    s.wait_ge(s_mm, mm0 + j + 1)
                                    s.activation(
                                        gsb[:, qh * 1024:(qh + 1) * 1024],
                                        g1p[:, qh * 1024:(qh + 1) * 1024],
                                        AF.Identity).then_inc(s_pa, 1)
                                    s.wait_ge(s_pa, base + 2 * j + 2)
                                    s.activation(
                                        mT[:, cc * TPC + qh * 1024:
                                           cc * TPC + (qh + 1) * 1024],
                                        gsb2[:, qh * 1024:(qh + 1) * 1024],
                                        AF.Silu).then_inc(s_pb, 1)

                            @blk.vector
                            def _(v):
                                for j in range(2 * CI):
                                    qh = j % 2
                                    v.wait_ge(s_pa, base + 2 * j + 1)
                                    v.tensor_tensor(
                                        gsb2[:, qh * 1024:(qh + 1) * 1024],
                                        gsb[:, qh * 1024:(qh + 1) * 1024],
                                        g3p[:, qh * 1024:(qh + 1) * 1024],
                                        ALU.mult).then_inc(s_pa, 1)
                            pa.v = base + 4 * CI
                            pb.v = pb0 + 2 * CI

                    esm.close()
                    eso = contextlib.ExitStack()
                    op0 = eso.enter_context(
                        nc.psum_tensor(f"op0{e}", [128, 2048], F32))
                    op1 = eso.enter_context(
                        nc.psum_tensor(f"op1{e}", [128, 2048], F32))
                    with nc.Block() as blk:
                        base = pa.v
                        ac0 = ac.v
                        lw_w2 = ldW.v

                        @blk.tensor
                        def _(t):
                            t.wait_ge(s_lw, lw_w2)
                            for tb in range(16):
                                ps = op0 if tb % 2 == 0 else op1
                                if tb >= 2:
                                    t.wait_ge(s_pa, base + tb - 1)
                                mi = None
                                for cc in range(CI):
                                    for q in range(4):
                                        mi = t.matmul(
                                            ps[:, q * 512:(q + 1) * 512],
                                            mT[:, cc * TPC + tb * 128:
                                               cc * TPC + (tb + 1) * 128],
                                            w2sb[:, cc * D + q * 512:
                                                 cc * D + (q + 1) * 512],
                                            start=(cc == 0), stop=(cc == CI - 1))
                                mi.then_inc(s_mm, 1)
                        mm.inc(16)

                        @blk.vector
                        def _(v):
                            for tb in range(16):
                                ps = op0 if tb % 2 == 0 else op1
                                tm = tmp0 if tb % 2 == 0 else tmp1
                                v.wait_ge(s_mm, mm.v - 16 + tb + 1)
                                if tb >= 2:
                                    v.wait_ge(s_ac, ac0 + 16 * (tb - 1))
                                if e == 0:
                                    v.tensor_copy(tm[:, :], ps[:, :]
                                                  ).then_inc(s_pa, 1)
                                else:
                                    v.tensor_scalar(
                                        tm[:, :], ps[:, :],
                                        rp[:, tb * 16 + e - 1:tb * 16 + e],
                                        None, ALU.mult).then_inc(s_pa, 1)

                        @blk.gpsimd
                        def _(g):
                            if e + 1 < 17:
                                g.dma_start(w1c[:, 0:D], w1_d[e + 1, 0, :, :]
                                            ).then_inc(s_lw, 16)
                                g.dma_start(w3c[:, 0:D], w3_d[e + 1, 0, :, :]
                                            ).then_inc(s_lw, 16)
                            for tb in range(16):
                                tm = tmp0 if tb % 2 == 0 else tmp1
                                g.wait_ge(s_pa, base + tb + 1)
                                g.dma_start(
                                    x4_d[tb * 128:(tb + 1) * 128, :],
                                    tm[:, :],
                                    accum_op=ALU.add).then_inc(s_ac, 16)
                        pa.v = base + 16
                        ac.inc(16 * 16)
                        if e + 1 < 17:
                            ldW.inc(32)
                    eso.close()

        with nc.Block() as blk:
            @blk.sync
            def _(s):
                s.wait_ge(s_ld, ldS.v)

            @blk.gpsimd
            def _(g):
                g.wait_ge(s_lw, ldW.v)
                g.wait_ge(s_ac, ac.v)
    return p


def l1f_inmaps(x3_full, gate_w, gate_b, w1s, w2s, w3s, w1r, w2r, w3r):
    gwT = f32a(pre_fmaj(np.ascontiguousarray(gate_w.T)))
    gb_rep = np.ascontiguousarray(
        np.broadcast_to(gate_b[None, :], (128, 16)).astype(np.float32))
    w1_pre = np.zeros((17, 11, 128, D), dtype=np.float32)
    w3_pre = np.zeros((17, 11, 128, D), dtype=np.float32)
    w2_cat = np.zeros((17, 1408, D), dtype=np.float32)
    w1_pre[0] = pre_headmaj(f32a(w1s), hd=128).transpose(0, 1, 2)
    w3_pre[0] = pre_headmaj(f32a(w3s), hd=128)
    w2_cat[0] = w2s
    for e in range(16):
        w1_pre[e + 1] = pre_headmaj(f32a(w1r[e]), hd=128)
        w3_pre[e + 1] = pre_headmaj(f32a(w3r[e]), hd=128)
        w2_cat[e + 1] = w2r[e]
    ident = np.eye(128, dtype=BF)
    maps = []
    for c in range(8):
        maps.append({
            "x3": np.ascontiguousarray(
                x3_full[own_positions(c) + (c // 2) * T]),
            "gwT_in": gwT, "gb_rep": gb_rep, "w1_pre": w1_pre,
            "w3_pre": w3_pre, "w2_cat": w2_cat, "ident": ident,
        })
    return maps



def emit_rmsnorm_row(blk, sem, c, x_ap, out_ap, stats, scol, junk, width,
                     eps=1e-6, mul_ap=None):
    base = c.v

    @blk.scalar
    def _(s):
        s.activation(junk, x_ap, AF.Square,
                     accum_out=stats[:, scol:scol + 1]).then_inc(sem, 1)
        s.wait_ge(sem, base + 2)
        s.activation(stats[:, scol + 2:scol + 3],
                     stats[:, scol + 1:scol + 2], AF.Sqrt).then_inc(sem, 1)

    @blk.vector
    def _(v):
        v.wait_ge(sem, base + 1)
        v.tensor_scalar(stats[:, scol + 1:scol + 2],
                        stats[:, scol:scol + 1], 1.0 / width, eps,
                        ALU.mult, ALU.add).then_inc(sem, 1)
        v.wait_ge(sem, base + 3)
        v.reciprocal(stats[:, scol + 3:scol + 4],
                     stats[:, scol + 2:scol + 3]).then_inc(sem, 1)
        v.wait_ge(sem, base + 4)
        if mul_ap is not None:
            v.tensor_scalar(out_ap, x_ap, stats[:, scol + 3:scol + 4],
                            None, ALU.mult).then_inc(sem, 1)
            v.wait_ge(sem, base + 5)
            v.tensor_tensor(out_ap, out_ap, mul_ap,
                            ALU.mult).then_inc(sem, 1)
            c.v = base + 6
        else:
            v.tensor_scalar(out_ap, x_ap, stats[:, scol + 3:scol + 4],
                            None, ALU.mult).then_inc(sem, 1)
            c.v = base + 5



def build_l1a(p=None, pfx=""):
    if p is None:
        p = Prog()
    p._pfx = pfx
    nc = _NCPfx(p.nc, pfx)
    x2_d = p.din("x2", (TPC, D), F32)
    wqc_d = p.din("wqc_pre", (128, FC * 512), F32)
    wqup_d = p.din("wqup_pre", (128, 4 * D), F32)
    kvw_d = p.din("kvcw_pre", (128, FC * 128), F32)
    kvz_d = p.din("kvcz_pre", (128, FC * 128), F32)
    kpw_d = p.din("kproj_pre", (128, FC * 128), F32)
    vpw_d = p.din("vproj_pre", (128, FC * 128), F32)
    icw_d = p.din("idxcw_pre", (128, FC * 64), F32)
    icz_d = p.din("idxcz_pre", (128, FC * 64), F32)
    wdq_d = p.din("wdq_pre", (128, FC * 64), F32)
    wiuq_d = p.din("wiuq", (64, 256), F32)
    ww_d = p.din("ww_pre", (128, FC * 4), F32)
    gw_d = p.din("gw_pre", (4, 128, 4 * 512), F32)
    fw_d = p.din("final_w", (D, D), F32)
    kvb_d = p.din("kvb_rep", (128, 128), F32)
    icb_d = p.din("icb_rep", (128, 64), F32)
    knw_d = p.din("knw_rep", (128, 128), F32)
    qnw_d = p.din("qnw_col", (128, 1), F32)
    qcs_d = p.din("l1qcos", (32, TPC), BF16)
    qsn_d = p.din("l1qsin", (32, TPC), BF16)
    scs_d = p.din("swcos", (TPC, 32), F32)
    ssn_d = p.din("swsin", (TPC, 32), F32)
    cmA_d = p.din("cmaskA", (TPC, 256), F32)
    cm1_d = p.din("cmask01", (TPC, 256), BF16)
    sw0_d = p.din("swmask0", (128, 2048), BF16)
    sw1_d = p.din("swmask1", (128, 2048), BF16)
    g8_d = p.din("g8_in", (128, 8), BF16)
    ident_d = p.din("ident", (128, 128), BF16)
    ones_d = p.din("ones_bf", (128, 128), BF16)
    x3_d = p.dout("x3", (TPC, D), F32)
    agin = p.dtmp("agin", (4352, 128), BF16)
    agout = p.dtmp("agout", (8704, 128), BF16)
    qT_dram = (p.dout if L1A_DEBUG else p.dtmp)("qT_l1", (16, 128, TPC), BF16)
    qT2_dram = p.dtmp("qT2_l1", (16, 128, 16, 128), BF16)
    if L1A_DEBUG:
        dbg_isc_d = p.dout("dbg_isc", (TPC, 256), F32)
        dbg_selm_d = p.dout("dbg_selm", (TPC, 256), BF16)
        dbg_ckv_d = p.dout("dbg_ckv", (128, 256), BF16)
        dbg_kc_d = p.dout("dbg_kc", (64, 256), BF16)
        dbg_wih_d = p.dout("dbg_wih", (128, 64), F32)
        dbg_dqT_d = p.dout("dbg_dqT", (64, TPC), BF16)
        dbg_swk_d = p.dout("dbg_swk", (128, 34 * 128), BF16)
        dbg_swv_d = p.dout("dbg_swv", (128, 34 * 128), BF16)

    ldS = Ctr(); ldW = Ctr(); pa = Ctr(); pb = Ctr(); mm = Ctr(); st = Ctr()
    ev = Ctr(); ac = Ctr(); ccc = Ctr(); dbc = Ctr()

    with contextlib.ExitStack() as TOP:
        _t = TOP.enter_context
        idn = _t(nc.sbuf_tensor("idn", [128, 128], BF16))
        ones = _t(nc.sbuf_tensor("ones", [128, 128], BF16))
        stats = _t(nc.sbuf_tensor("stats", [128, 12], F32))
        knw = _t(nc.sbuf_tensor("knw", [128, 128], F32))
        qnw = _t(nc.sbuf_tensor("qnw", [128, 1], F32))
        kvb = _t(nc.sbuf_tensor("kvb", [128, 128], F32))
        icb = _t(nc.sbuf_tensor("icb", [128, 64], F32))
        g8 = _t(nc.sbuf_tensor("g8", [128, 8], BF16))
        sw0 = _t(nc.sbuf_tensor("sw0", [128, 2048], BF16))
        sw1 = _t(nc.sbuf_tensor("sw1", [128, 2048], BF16))
        swkT = _t(nc.sbuf_tensor("swkT", [128, 34 * 128], BF16))
        swv = _t(nc.sbuf_tensor("swv", [128, 34 * 128], BF16))
        ckvn = _t(nc.sbuf_tensor("ckvn", [128, 256], BF16))
        ckvnT = _t(nc.sbuf_tensor("ckvnT", [128, 256], BF16))
        kcompT = _t(nc.sbuf_tensor("kcompT", [64, 256], BF16))
        selmT = _t(nc.sbuf_tensor("selmT", [128, 2 * TPC], BF16))
        qcT = _t(nc.sbuf_tensor("qcT", [128, 4 * TPC], BF16))
        dqT = _t(nc.sbuf_tensor("dqT", [64, TPC], BF16))
        wih = _t(nc.sbuf_tensor("wih", [128, 64], F32))
        s_ld = _t(nc.semaphore("s_ld"))
        s_lw = _t(nc.semaphore("s_lw"))
        s_pa = _t(nc.semaphore("s_pa"))
        s_pb = _t(nc.semaphore("s_pb"))
        s_mm = _t(nc.semaphore("s_mm"))
        s_st = _t(nc.semaphore("s_st"))
        s_ev = _t(nc.semaphore("s_ev"))
        s_ac = _t(nc.semaphore("s_ac"))
        s_cc = _t(nc.semaphore("s_cc"))
        s_db = _t(nc.semaphore("s_db"))

        with nc.Block() as blk:
            @blk.sync
            def _(s):
                for ap, dap in [(idn, ident_d), (ones, ones_d), (knw, knw_d),
                                (qnw, qnw_d), (kvb, kvb_d), (icb, icb_d),
                                (g8, g8_d), (sw0, sw0_d), (sw1, sw1_d)]:
                    s.dma_start(ap[:, :], dap[:, :]).then_inc(s_ld, 16)
                s.wait_ge(s_ld, ldS.inc(16 * 9))

            @blk.gpsimd
            def _(g):
                g.dma_start(x3_d[:, :], x2_d[:, :]).then_inc(s_ac, 16)
                g.wait_ge(s_ac, ac.inc(16))

        with nc.sbuf_tensor("hT", [128, FC * TPC], BF16, side="right") as hT:
            with contextlib.ExitStack() as es:
                _e = es.enter_context
                xt = _e(nc.sbuf_tensor("xt", [128, 2 * D], F32, side="right"))
                htile = _e(nc.sbuf_tensor("htile", [128, 2 * D], BF16,
                                          side="right"))
                junk = _e(nc.sbuf_tensor("junk", [128, D], BF16, side="right"))
                lstats = _e(nc.sbuf_tensor("lstats", [128, 16], F32,
                                           side="right"))
                tps0 = _e(nc.psum_tensor("tps0", [128, 1024], BF16))
                tps1 = _e(nc.psum_tensor("tps1", [128, 1024], BF16))
                emit_ln_phase(nc, idn, x2_d, hT, xt, htile, junk, lstats,
                              tps0, tps1, s_ld, s_pa, s_mm, s_pb,
                              ldS, pa, mm, pb)

            with contextlib.ExitStack() as es:
                _e = es.enter_context
                wqc = _e(nc.sbuf_tensor("wqc", [128, FC * 512], BF16, side="right"))
                wkv = _e(nc.sbuf_tensor("wkv", [128, FC * 128], BF16, side="right"))
                wkz = _e(nc.sbuf_tensor("wkz", [128, FC * 128], BF16, side="right"))
                wkp = _e(nc.sbuf_tensor("wkp", [128, FC * 128], BF16, side="right"))
                wvp = _e(nc.sbuf_tensor("wvp", [128, FC * 128], BF16, side="right"))
                wic = _e(nc.sbuf_tensor("wic", [128, FC * 64], BF16, side="right"))
                wiz = _e(nc.sbuf_tensor("wiz", [128, FC * 64], BF16, side="right"))
                wdq = _e(nc.sbuf_tensor("wdq", [128, FC * 64], BF16, side="right"))
                www = _e(nc.sbuf_tensor("www", [128, FC * 4], BF16, side="right"))
                scst = _e(nc.sbuf_tensor("scst", [128, 64], F32, side="right"))
                ssnt = _e(nc.sbuf_tensor("ssnt", [128, 64], F32, side="right"))
                zbuf = _e(nc.sbuf_tensor("zbuf", [128, 128], F32, side="right"))
                ebuf = _e(nc.sbuf_tensor("ebuf", [128, 384], BF16, side="right"))
                pbuf = _e(nc.sbuf_tensor("pbuf", [128, 384], BF16, side="right"))
                sk1 = _e(nc.sbuf_tensor("sk1", [128, 128], F32, side="right"))
                rp1 = _e(nc.sbuf_tensor("rp1", [128, 128], F32, side="right"))
                sk2 = _e(nc.sbuf_tensor("sk2", [128, 256], BF16, side="right"))
                svb = _e(nc.sbuf_tensor("svb", [128, 256], BF16, side="right"))
                stg = _e(nc.sbuf_tensor("stg", [128, 384], BF16, side="right"))
                rcb = _e(nc.sbuf_tensor("rcb", [128, 192], F32, side="right"))
                junk3 = _e(nc.sbuf_tensor("junk3", [128, 128], BF16, side="right"))
                prj = _e(nc.psum_tensor("prj", [128, 2048], F32))
                cps = _e(nc.psum_tensor("cps", [128, 1024], F32))
                qcp = _e(nc.psum_tensor("qcp", [128, 1024], F32))

                with nc.Block() as blk:
                    @blk.gpsimd
                    def _(g):
                        for sb_t, d_t in [(wqc, wqc_d), (wkv, kvw_d),
                                          (wkz, kvz_d), (wkp, kpw_d),
                                          (wvp, vpw_d), (wic, icw_d),
                                          (wiz, icz_d), (wdq, wdq_d),
                                          (www, ww_d)]:
                            g.dma_start(sb_t[:, :], d_t[:, :]).then_inc(s_lw, 16)
                        g.wait_ge(s_lw, ldW.inc(16 * 9))

                with nc.Block() as blk:
                    pa0 = pa.v
                    st00 = st.v
                    mm0 = mm.v
                    ld0 = ldS.v

                    @blk.sync
                    def _(s):
                        for tb in range(16):
                            so = (tb % 2) * 32
                            base = pa0 + 20 * tb
                            if tb >= 2:
                                s.wait_ge(s_pa, base - 40 + 7)
                            s.dma_start(scst[:, so:so + 32],
                                        scs_d[tb * 128:(tb + 1) * 128, :]
                                        ).then_inc(s_ld, 16)
                            s.dma_start(ssnt[:, so:so + 32],
                                        ssn_d[tb * 128:(tb + 1) * 128, :]
                                        ).then_inc(s_ld, 16)
                            ko = (tb % 2) * 128
                            go = (tb % 2) * 192
                            s.wait_ge(s_pa, base + 8)
                            s.dma_start(agin[tb * 128:(tb + 1) * 128, :],
                                        sk2[:, ko:ko + 128]
                                        ).then_inc(s_st, 16)
                            s.wait_ge(s_pa, base + 9)
                            s.dma_start(agin[2048 + tb * 128:
                                             2048 + (tb + 1) * 128, :],
                                        svb[:, ko:ko + 128]
                                        ).then_inc(s_st, 16)
                            s.wait_ge(s_pa, base + 18)
                            s.dma_start(agin[4096 + tb * 8:
                                             4096 + (tb + 1) * 8, :],
                                        stg[0:8, go:go + 128]
                                        ).then_inc(s_st, 16)
                            s.wait_ge(s_pa, base + 20)
                            s.dma_start(agin[4224 + tb * 8:
                                             4224 + (tb + 1) * 8, 0:64],
                                        stg[0:8, go + 128:go + 192]
                                        ).then_inc(s_st, 16)
                    ldS.inc(32 * 16)
                    st.inc(64 * 16)

                    @blk.tensor
                    def _(t):
                        def g_mms(tb):
                            co = (tb % 2) * 512
                            eo = (tb % 2) * 192
                            t.wait_ge(s_pa, pa0 + 20 * tb + 16)
                            t.matmul(cps[0:8, co:co + 128], g8[:, :],
                                     pbuf[:, eo:eo + 128],
                                     start=True, stop=True)
                            t.matmul(cps[0:8, co + 128:co + 256], g8[:, :],
                                     ebuf[:, eo:eo + 128],
                                     start=True, stop=True)
                            t.matmul(cps[0:8, co + 256:co + 320], g8[:, :],
                                     pbuf[:, eo + 128:eo + 192],
                                     start=True, stop=True)
                            t.matmul(cps[0:8, co + 320:co + 384], g8[:, :],
                                     ebuf[:, eo + 128:eo + 192],
                                     start=True,
                                     stop=True).then_inc(s_mm, 1)
                        for tb in range(16):
                            po = (tb % 2) * 1024
                            qo = (tb % 2) * 512
                            co = (tb % 2) * 512
                            if tb >= 1:
                                g_mms(tb - 1)
                                t.wait_ge(s_pa, pa0 + 20 * (tb - 1) + 20)
                            mi = None
                            for fc in range(FC):
                                hs = hT[:, fc * TPC + tb * 128:
                                        fc * TPC + (tb + 1) * 128]
                                for (lo, hi, wt, ww_) in [
                                        (0, 128, wkp, 128),
                                        (128, 256, wvp, 128),
                                        (256, 384, wkv, 128),
                                        (384, 512, wkz, 128),
                                        (512, 576, wic, 64),
                                        (576, 640, wiz, 64),
                                        (640, 644, www, 4)]:
                                    t.matmul(prj[:, po + lo:po + hi], hs,
                                             wt[:, fc * ww_:(fc + 1) * ww_],
                                             start=(fc == 0 and
                                                    lo in (0, 512)),
                                             stop=(fc == FC - 1))
                                t.matmul(qcp[:, qo:qo + 128],
                                         wqc[:, fc * 512:fc * 512 + 128], hs,
                                         start=(fc == 0),
                                         stop=(fc == FC - 1))
                                t.matmul(cps[0:64, co + 384:co + 512],
                                         wdq[:, fc * 64:(fc + 1) * 64], hs,
                                         start=(fc == 0),
                                         stop=(fc == FC - 1))
                                for ob in range(1, 4):
                                    mi = t.matmul(
                                        qcp[:, qo + ob * 128:
                                            qo + (ob + 1) * 128],
                                        wqc[:, fc * 512 + ob * 128:
                                            fc * 512 + (ob + 1) * 128], hs,
                                        start=False, stop=(fc == FC - 1))
                            mi.then_inc(s_mm, 1)
                        g_mms(15)
                    mm.inc(32)

                    @blk.scalar
                    def _(s):
                        for tb in range(16):
                            base = pa0 + 20 * tb
                            p4 = (tb % 2) * 4
                            po = (tb % 2) * 1024
                            eo = (tb % 2) * 192
                            s.wait_ge(s_mm, mm0 + 2 * tb + 1)
                            s.activation(junk3[:, :], prj[:, po:po + 128],
                                         AF.Square,
                                         accum_out=stats[:, p4:p4 + 1]
                                         ).then_inc(s_pa, 1)
                            s.wait_ge(s_pa, base + 2)
                            s.activation(stats[:, p4 + 2:p4 + 3],
                                         stats[:, p4 + 1:p4 + 2],
                                         AF.Sqrt).then_inc(s_pa, 1)
                            s.wait_ge(s_pa, base + 11)
                            s.activation(ebuf[:, eo:eo + 128], zbuf[:, :],
                                         AF.Exp).then_inc(s_pa, 1)
                            s.wait_ge(s_pa, base + 13)
                            s.activation(ebuf[:, eo + 128:eo + 192],
                                         rp1[:, 0:64],
                                         AF.Exp).then_inc(s_pa, 1)

                    @blk.vector
                    def _(v):
                        for tb in range(16):
                            base = pa0 + 20 * tb
                            p4 = (tb % 2) * 4
                            po = (tb % 2) * 1024
                            qo = (tb % 2) * 512
                            co = (tb % 2) * 512
                            so = (tb % 2) * 32
                            ko = (tb % 2) * 128
                            go = (tb % 2) * 192
                            eo = (tb % 2) * 192
                            if tb >= 2:
                                v.wait_ge(s_st, st00 + 64 * (tb - 1))
                            v.wait_ge(s_ld, ld0 + 32 * (tb + 1))
                            v.wait_ge(s_pa, base + 1)
                            v.tensor_scalar(stats[:, p4 + 1:p4 + 2],
                                            stats[:, p4:p4 + 1],
                                            1.0 / 128, 1e-6, ALU.mult,
                                            ALU.add).then_inc(s_pa, 1)
                            v.wait_ge(s_pa, base + 3)
                            v.reciprocal(stats[:, p4 + 3:p4 + 4],
                                         stats[:, p4 + 2:p4 + 3]
                                         ).then_inc(s_pa, 1)
                            v.wait_ge(s_pa, base + 4)
                            v.tensor_scalar(sk1[:, :], prj[:, po:po + 128],
                                            stats[:, p4 + 3:p4 + 4], None,
                                            ALU.mult).then_inc(s_pa, 1)
                            v.wait_ge(s_pa, base + 5)
                            v.tensor_tensor(sk1[:, :], sk1[:, :], knw[:, :],
                                            ALU.mult).then_inc(s_pa, 1)
                            v.wait_ge(s_pa, base + 6)
                            v.tensor_tensor(rp1[:, 0:32], sk1[:, 0:32],
                                            scst[:, so:so + 32], ALU.mult)
                            v.tensor_tensor(rp1[:, 32:64], sk1[:, 0:32],
                                            ssnt[:, so:so + 32], ALU.mult)
                            v.tensor_tensor(rp1[:, 64:96], sk1[:, 32:64],
                                            ssnt[:, so:so + 32], ALU.mult)
                            v.tensor_tensor(rp1[:, 96:128], sk1[:, 32:64],
                                            scst[:, so:so + 32], ALU.mult
                                            ).then_inc(s_pa, 1)
                            v.wait_ge(s_pa, base + 7)
                            v.tensor_tensor(sk2[:, ko:ko + 32],
                                            rp1[:, 0:32],
                                            rp1[:, 64:96], ALU.subtract)
                            v.tensor_tensor(sk2[:, ko + 32:ko + 64],
                                            rp1[:, 32:64],
                                            rp1[:, 96:128], ALU.add)
                            v.tensor_copy(sk2[:, ko + 64:ko + 128],
                                          sk1[:, 64:128]
                                          ).then_inc(s_pa, 1)
                            v.tensor_copy(svb[:, ko:ko + 128],
                                          prj[:, po + 128:po + 256]
                                          ).then_inc(s_pa, 1)
                            v.tensor_tensor(zbuf[:, :],
                                            prj[:, po + 384:po + 512],
                                            kvb[:, :], ALU.add
                                            ).then_inc(s_pa, 1)
                            v.wait_ge(s_pa, base + 10)
                            v.nop().then_inc(s_pa, 1)
                            v.wait_ge(s_pa, base + 12)
                            v.tensor_tensor(pbuf[:, eo:eo + 128],
                                            ebuf[:, eo:eo + 128],
                                            prj[:, po + 256:po + 384],
                                            ALU.mult)
                            v.tensor_tensor(rp1[:, 0:64],
                                            prj[:, po + 576:po + 640],
                                            icb[:, :], ALU.add
                                            ).then_inc(s_pa, 1)
                            v.wait_ge(s_pa, base + 14)
                            v.tensor_tensor(pbuf[:, eo + 128:eo + 192],
                                            ebuf[:, eo + 128:eo + 192],
                                            prj[:, po + 512:po + 576],
                                            ALU.mult).then_inc(s_pa, 1)
                            v.tensor_copy(dqT[:, tb * 128:(tb + 1) * 128],
                                          cps[0:64, co + 384:co + 512])
                            for ob in range(4):
                                v.tensor_copy(
                                    qcT[:, ob * TPC + tb * 128:
                                        ob * TPC + (tb + 1) * 128],
                                    qcp[:, qo + ob * 128:qo + (ob + 1) * 128])
                            v.tensor_copy(wih[:, tb * 4:(tb + 1) * 4],
                                          prj[:, po + 640:po + 644]
                                          ).then_inc(s_pa, 1)
                            v.wait_ge(s_mm, mm0 + 2 * tb + 2)
                            v.reciprocal(rcb[0:8, 0:128],
                                         cps[0:8, co + 128:co + 256]
                                         ).then_inc(s_pa, 1)
                            v.wait_ge(s_pa, base + 17)
                            v.tensor_tensor(stg[0:8, go:go + 128],
                                            cps[0:8, co:co + 128],
                                            rcb[0:8, 0:128], ALU.mult
                                            ).then_inc(s_pa, 1)
                            v.reciprocal(rcb[0:8, 128:192],
                                         cps[0:8, co + 320:co + 384]
                                         ).then_inc(s_pa, 1)
                            v.wait_ge(s_pa, base + 19)
                            v.tensor_tensor(stg[0:8, go + 128:go + 192],
                                            cps[0:8, co + 256:co + 320],
                                            rcb[0:8, 128:192], ALU.mult
                                            ).then_inc(s_pa, 1)
                    pa.v = pa0 + 20 * 16


            with nc.Block() as blk:
                @blk.gpsimd
                def _(g):
                    g.wait_ge(s_st, st.v)
                    g.collective_compute(
                        "AllGather", ALU.bypass,
                        replica_groups=[[0, 1], [2, 3], [4, 5], [6, 7]],
                        ins=[agin[:, :]],
                        outs=[agout[:, :]],
                    ).then_inc(s_cc, 1)
                ccc.inc(1)

            with contextlib.ExitStack() as es:
                _e = es.enter_context
                wqup = _e(nc.sbuf_tensor("wqup", [128, 4 * D], BF16,
                                         side="right"))
                qcs = _e(nc.sbuf_tensor("qcs", [32, TPC], BF16, side="right"))
                qsn = _e(nc.sbuf_tensor("qsn", [32, TPC], BF16, side="right"))
                qraw = _e(nc.sbuf_tensor("qraw", [128, 2 * TPC], BF16,
                                         side="right"))
                qs32 = _e(nc.sbuf_tensor("qs32", [32, 4 * TPC], BF16,
                                         side="right"))
                qrope = _e(nc.sbuf_tensor("qrope", [128, TPC], BF16,
                                          side="right"))
                qsq = _e(nc.sbuf_tensor("qsq", [128, 2 * TPC], BF16,
                                        side="right"))
                rstd = _e(nc.sbuf_tensor("rstd", [128, TPC], F32, side="right"))
                qps = _e(nc.psum_tensor("qps", [128, 2048], F32))
                sps = _e(nc.psum_tensor("sps", [128, 2048], F32))

                with nc.Block() as blk:
                    @blk.gpsimd
                    def _(g):
                        g.dma_start(wqup[:, :], wqup_d[:, :]).then_inc(s_lw, 16)
                        g.wait_ge(s_lw, ldW.inc(16))

                    @blk.sync
                    def _(s):
                        s.dma_start(qcs[:, :], qcs_d[:, :]).then_inc(s_ld, 16)
                        s.dma_start(qsn[:, :], qsn_d[:, :]).then_inc(s_ld, 16)
                        s.wait_ge(s_ld, ldS.inc(32))

                with nc.Block() as blk:
                    pa0 = pa.v
                    mm0 = mm.v
                    st0 = st.v

                    @blk.tensor
                    def _(t):
                        def sps_mm(hp):
                            t.wait_ge(s_pa, pa0 + 11 * hp + 7)
                            if hp >= 1:
                                t.wait_ge(s_pa, pa0 + 11 * (hp - 1) + 8)
                            mi = None
                            for q in range(4):
                                mi = t.matmul(
                                    sps[:, q * 512:(q + 1) * 512], ones[:, :],
                                    qsq[:, (hp % 2) * TPC + q * 512:
                                        (hp % 2) * TPC + (q + 1) * 512],
                                    start=True, stop=True)
                            mi.then_inc(s_ev, 1)
                        for h in range(16):
                            if h >= 1:
                                t.wait_ge(s_pa, pa0 + 11 * (h - 1) + 1)
                            mi = None
                            for cc in range(4):
                                for q in range(4):
                                    mi = t.matmul(
                                        qps[:, q * 512:(q + 1) * 512],
                                        wqup[:, cc * D + h * 128:
                                             cc * D + (h + 1) * 128],
                                        qcT[:, cc * TPC + q * 512:
                                            cc * TPC + (q + 1) * 512],
                                        start=(cc == 0), stop=(cc == 3))
                            mi.then_inc(s_mm, 1)
                            if h >= 1:
                                sps_mm(h - 1)
                        sps_mm(15)
                    mm.inc(16)
                    ev0_p6 = ev.v
                    ev.inc(16)

                    @blk.vector
                    def _(v):
                        for h in range(16):
                            base = pa0 + 11 * h
                            hb = (h % 2) * TPC
                            qrw = qraw[:, hb:hb + TPC]
                            v.wait_ge(s_mm, mm0 + h + 1)
                            if h >= 2:
                                v.wait_ge(s_st, st0 + 32 * (h - 1))
                            v.tensor_copy(qrw, qps[:, :]
                                          ).then_inc(s_pa, 1)
                            v.wait_ge(s_pa, base + 1)
                            v.tensor_copy(qs32[:, 0:TPC], qrw[32:64, :]
                                          ).then_inc(s_pa, 1)
                            v.wait_ge(s_pa, base + 2)
                            v.tensor_tensor(qs32[:, TPC:2 * TPC],
                                            qs32[:, 0:TPC], qsn[:, :],
                                            ALU.mult)
                            v.tensor_tensor(qs32[:, 2 * TPC:3 * TPC],
                                            qrw[0:32, :], qsn[:, :],
                                            ALU.mult)
                            v.tensor_tensor(qs32[:, 3 * TPC:4 * TPC],
                                            qs32[:, 0:TPC], qcs[:, :],
                                            ALU.mult).then_inc(s_pa, 1)
                            v.wait_ge(s_pa, base + 3)
                            v.tensor_tensor(qrope[0:32, :], qrw[0:32, :],
                                            qcs[:, :], ALU.mult
                                            ).then_inc(s_pa, 1)
                            v.wait_ge(s_pa, base + 4)
                            v.tensor_tensor(qrope[0:32, :], qrope[0:32, :],
                                            qs32[:, TPC:2 * TPC],
                                            ALU.subtract)
                            v.tensor_tensor(qs32[:, 0:TPC],
                                            qs32[:, 2 * TPC:3 * TPC],
                                            qs32[:, 3 * TPC:4 * TPC],
                                            ALU.add).then_inc(s_pa, 1)
                            v.wait_ge(s_pa, base + 5)
                            v.tensor_copy(qrope[32:64, :], qs32[:, 0:TPC])
                            v.tensor_copy(qrope[64:128, :], qrw[64:128, :]
                                          ).then_inc(s_pa, 1)
                            v.wait_ge(s_ev, ev0_p6 + h + 1)
                            v.tensor_scalar(rstd[:, :], sps[:, :], 1.0 / 128,
                                            1e-6, ALU.mult,
                                            ALU.add).then_inc(s_pa, 1)
                            v.wait_ge(s_pa, base + 10)
                            v.scalar_tensor_tensor(qrw, qrope[:, :],
                                                   qnw[:, 0:1], rstd[:, :],
                                                   ALU.mult, ALU.mult
                                                   ).then_inc(s_pa, 1)

                    @blk.scalar
                    def _(s):
                        for h in range(16):
                            base = pa0 + 11 * h
                            s.wait_ge(s_pa, base + 6)
                            s.activation(qsq[:, (h % 2) * TPC:
                                              (h % 2 + 1) * TPC],
                                         qrope[:, :],
                                         AF.Square).then_inc(s_pa, 1)
                            s.wait_ge(s_pa, base + 8)
                            s.activation(rstd[:, :], rstd[:, :],
                                         AF.Ln).then_inc(s_pa, 1)
                            s.wait_ge(s_pa, base + 9)
                            s.activation(rstd[:, :], rstd[:, :], AF.Exp,
                                         scale=-0.5).then_inc(s_pa, 1)

                    @blk.sync
                    def _(s):
                        for h in range(16):
                            hb = (h % 2) * TPC
                            s.wait_ge(s_pa, pa0 + 11 * h + 11)
                            s.dma_start(qT_dram[h, :, :],
                                        qraw[:, hb:hb + TPC]
                                        ).then_inc(s_st, 16)
                            s.dma_start(
                                qT2_dram[:, :, h, :].rearrange(
                                    "j p t -> p j t"),
                                qraw[:, hb:hb + TPC].rearrange(
                                    "p (j t) -> p j t",
                                    j=16)).then_inc(s_st, 16)
                    pa.v = pa0 + 11 * 16
                    st.inc(32 * 16)

        with contextlib.ExitStack() as es:
            _e = es.enter_context
            kst = _e(nc.sbuf_tensor("kst", [128, 256], BF16, side="right"))
            junk4 = _e(nc.sbuf_tensor("junk4", [128, 128], BF16, side="right"))
            tpsk = _e(nc.psum_tensor("tpsk", [128, 1024], BF16))
            tpsk2 = _e(nc.psum_tensor("tpsk2", [128, 1024], BF16))

            with nc.Block() as blk:
                @blk.gpsimd
                def _(g):
                    g.wait_ge(s_cc, ccc.v)

            with nc.Block() as blk:
                @blk.vector
                def _(v):
                    v.memset(swkT[:, 0:128], 0)
                    v.memset(swkT[:, 17 * 128:18 * 128], 0)
                    v.memset(swv[:, 0:128], 0)
                    v.memset(swv[:, 17 * 128:18 * 128], 0)

                @blk.sync
                def _(s):
                    for hf in range(2):
                        for j in range(16):
                            s.dma_start(
                                swv[:, (1 + hf * 17 + j) * 128:
                                    (2 + hf * 17 + j) * 128],
                                agout[hf * 4352 + 2048 + j * 128:
                                      hf * 4352 + 2048 + (j + 1) * 128, :]
                                ).then_inc(s_ld, 16)
                    s.wait_ge(s_ld, ldS.inc(32 * 16))

            for hf in range(2):
                for j in range(16):
                    with nc.Block() as blk:
                        base = pa.v
                        ld0 = ldS.v

                        @blk.sync
                        def _(s):
                            s.dma_start(kst[:, (j % 2) * 128:(j % 2 + 1) * 128],
                                        agout[hf * 4352 + j * 128:
                                              hf * 4352 + (j + 1) * 128, :]
                                        ).then_inc(s_ld, 16)
                        ldS.inc(16)

                        @blk.tensor
                        def _(t):
                            t.wait_ge(s_ld, ldS.v)
                            t.transpose(tpsk[:, 0:128],
                                        kst[:, (j % 2) * 128:(j % 2 + 1) * 128],
                                        idn[:, :]).then_inc(s_mm, 1)
                        mm.inc(1)

                        @blk.vector
                        def _(v):
                            v.wait_ge(s_mm, mm.v)
                            v.tensor_copy(swkT[:, (1 + hf * 17 + j) * 128:
                                              (2 + hf * 17 + j) * 128],
                                          tpsk[:, 0:128])

            for nb in range(2):
                with nc.Block() as blk:
                    base = pa.v
                    ld0 = ldS.v

                    @blk.sync
                    def _(s):
                        s.dma_start(kst[:, 0:128],
                                    agout[nb * 4352 + 4096:
                                          nb * 4352 + 4224, :]
                                    ).then_inc(s_ld, 16)
                        s.dma_start(kst[:, 128:192],
                                    agout[nb * 4352 + 4224:
                                          nb * 4352 + 4352, 0:64]
                                    ).then_inc(s_ld, 16)
                    ldS.inc(32)

                    @blk.scalar
                    def _(s):
                        s.wait_ge(s_ld, ldS.v)
                        s.activation(junk4[:, 0:128], kst[:, 0:128], AF.Square,
                                     accum_out=stats[:, 4:5]).then_inc(s_pa, 1)
                        s.wait_ge(s_pa, base + 2)
                        s.activation(stats[:, 6:7], stats[:, 5:6],
                                     AF.Sqrt).then_inc(s_pa, 1)

                    @blk.vector
                    def _(v):
                        v.wait_ge(s_pa, base + 1)
                        v.tensor_scalar(stats[:, 5:6], stats[:, 4:5],
                                        1.0 / 128, 1e-6, ALU.mult,
                                        ALU.add).then_inc(s_pa, 1)
                        v.wait_ge(s_pa, base + 3)
                        v.reciprocal(stats[:, 7:8], stats[:, 6:7]
                                     ).then_inc(s_pa, 1)
                        v.wait_ge(s_pa, base + 4)
                        v.tensor_scalar(ckvn[:, nb * 128:(nb + 1) * 128],
                                        kst[:, 0:128], stats[:, 7:8], None,
                                        ALU.mult).then_inc(s_pa, 1)
                        v.wait_ge(s_pa, base + 5)
                        v.tensor_tensor(ckvn[:, nb * 128:(nb + 1) * 128],
                                        ckvn[:, nb * 128:(nb + 1) * 128],
                                        knw[:, :], ALU.mult).then_inc(s_pa, 1)
                    pa.v = base + 6

                with nc.Block() as blk:
                    mm0 = mm.v

                    @blk.tensor
                    def _(t):
                        t.transpose(tpsk[:, 0:128],
                                    ckvn[:, nb * 128:(nb + 1) * 128],
                                    idn[:, :]).then_inc(s_mm, 1)
                        t.transpose(tpsk2[0:64, 0:128], kst[:, 128:192],
                                    idn[:, :]).then_inc(s_mm, 1)
                    mm.inc(2)

                    @blk.vector
                    def _(v):
                        v.wait_ge(s_mm, mm0 + 1)
                        v.tensor_copy(ckvnT[:, nb * 128:(nb + 1) * 128],
                                      tpsk[:, 0:128])
                        v.wait_ge(s_mm, mm0 + 2)
                        v.tensor_copy(kcompT[:, nb * 128:(nb + 1) * 128],
                                      tpsk2[0:64, 0:128])

        with contextlib.ExitStack() as es:
            _e = es.enter_context
            qiT = _e(nc.sbuf_tensor("qiT", [64, 4 * TPC], BF16, side="right"))
            wiuq = _e(nc.sbuf_tensor("wiuq_sb", [64, 256], BF16, side="right"))
            cmA = _e(nc.sbuf_tensor("cmA", [128, 16 * 256], F32, side="right"))
            cm1 = _e(nc.sbuf_tensor("cm1", [128, 16 * 256], BF16, side="right"))
            hsb = _e(nc.sbuf_tensor("hsb", [128, 256], BF16, side="right"))
            isc = _e(nc.sbuf_tensor("isc", [128, 256], F32, side="right"))
            isc2 = _e(nc.sbuf_tensor("isc2", [128, 256], F32, side="right"))
            t8b = _e(nc.sbuf_tensor("t8b", [128, 8], F32, side="right"))
            selm = _e(nc.sbuf_tensor("selm", [128, 256], BF16, side="right"))
            isp = _e(nc.psum_tensor("isp", [128, 512], F32))
            qip = _e(nc.psum_tensor("qip", [128, 2048], F32))
            tpsm = _e(nc.psum_tensor("tpsm", [128, 1024], BF16))

            with nc.Block() as blk:
                ld0 = ldS.v

                @blk.sync
                def _(s):
                    s.dma_start(cmA[:, :].rearrange("p (a n) -> p a n", n=256),
                                cmA_d.rearrange("(a p) n -> p a n", p=128)
                                ).then_inc(s_ld, 16)
                    s.dma_start(cm1[:, :].rearrange("p (a n) -> p a n", n=256),
                                cm1_d.rearrange("(a p) n -> p a n", p=128)
                                ).then_inc(s_ld, 16)
                    s.wait_ge(s_ld, ldS.inc(32))

                @blk.gpsimd
                def _(g):
                    g.dma_start(wiuq[:, :], wiuq_d[:, :]).then_inc(s_lw, 16)
                    g.wait_ge(s_lw, ldW.inc(16))

            with nc.Block() as blk:
                mm0 = mm.v

                @blk.tensor
                def _(t):
                    qb0 = pa.v
                    mi = None
                    for ob in range(2):
                        for q in range(4):
                            mi = t.matmul(
                                qip[:, q * 512:(q + 1) * 512],
                                wiuq[:, ob * 128:(ob + 1) * 128],
                                dqT[:, q * 512:(q + 1) * 512],
                                start=True, stop=True)
                        mi.then_inc(s_mm, 1)
                        t.wait_ge(s_pa, qb0 + ob + 1)
                    mm.inc(2)

                @blk.vector
                def _(v):
                    for ob in range(2):
                        v.wait_ge(s_mm, mm.v - 2 + ob + 1)
                        v.tensor_copy(qiT[:, (2 * ob) * TPC:
                                          (2 * ob + 1) * TPC],
                                      qip[0:64, :])
                        v.tensor_copy(qiT[:, (2 * ob + 1) * TPC:
                                          (2 * ob + 2) * TPC],
                                      qip[64:128, :]).then_inc(s_pa, 1)
                pa.v = pa.v + 2

            for tb in range(16):
                with nc.Block() as blk:
                    base = pa.v
                    mm0 = mm.v

                    @blk.tensor
                    def _(t):
                        for h in range(4):
                            t.matmul(isp[:, 0:256],
                                     qiT[:, h * TPC + tb * 128:
                                         h * TPC + (tb + 1) * 128],
                                     kcompT[:, :],
                                     start=True, stop=True).then_inc(s_mm, 1)
                            t.wait_ge(s_pa, base + 2 * h + 1)
                    mm.inc(4)

                    @blk.scalar
                    def _(s):
                        for h in range(4):
                            s.wait_ge(s_mm, mm0 + h + 1)
                            if h >= 1:
                                s.wait_ge(s_pa, base + 2 * h)
                            s.activation(hsb[:, :], isp[:, 0:256],
                                         AF.Relu).then_inc(s_pa, 1)

                    @blk.vector
                    def _(v):
                        if L1A_DEBUG and tb >= 1:
                            v.wait_ge(s_db, 32 * tb)
                        for h in range(4):
                            v.wait_ge(s_pa, base + 2 * h + 1)
                            if h == 0:
                                v.tensor_scalar(
                                    isc[:, :], hsb[:, :],
                                    wih[:, tb * 4:tb * 4 + 1], None,
                                    ALU.mult).then_inc(s_pa, 1)
                            else:
                                v.scalar_tensor_tensor(
                                    isc[:, :], hsb[:, :],
                                    wih[:, tb * 4 + h:tb * 4 + h + 1],
                                    isc[:, :], ALU.mult,
                                    ALU.add).then_inc(s_pa, 1)
                        v.wait_ge(s_pa, base + 8)
                        v.tensor_tensor(isc[:, :], isc[:, :],
                                        cmA[:, tb * 256:(tb + 1) * 256],
                                        ALU.add).then_inc(s_pa, 1)
                        cur = isc
                        nxt = isc2
                        w = 9
                        for it in range(8):
                            v.wait_ge(s_pa, base + w)
                            v.max(out=t8b[:, :], in_=cur[:, :]
                                  ).then_inc(s_pa, 1)
                            w += 1
                            v.wait_ge(s_pa, base + w)
                            if L1A_DEBUG and it == 1:
                                v.wait_ge(s_db, 32 * tb + 16)
                            v.match_replace(out=nxt[:, :],
                                            in_to_replace=t8b[:, :],
                                            in_values=cur[:, :],
                                            imm_value=-1e30).then_inc(s_pa, 1)
                            w += 1
                            cur, nxt = nxt, cur
                        v.wait_ge(s_pa, base + w)
                        v.tensor_scalar(cur[:, :], cur[:, :], -1e30, None,
                                        ALU.is_equal).then_inc(s_pa, 1)
                        w += 1
                        v.wait_ge(s_pa, base + w)
                        v.tensor_tensor(selm[:, :], cur[:, :],
                                        cm1[:, tb * 256:(tb + 1) * 256],
                                        ALU.mult).then_inc(s_pa, 1)
                        w += 1
                        pa.v = base + w

                    if L1A_DEBUG:
                        @blk.sync
                        def _(s):
                            s.wait_ge(s_pa, base + 9)
                            s.dma_start(dbg_isc_d[tb * 128:(tb + 1) * 128, :],
                                        isc[:, :]).then_inc(s_db, 16)
                            s.wait_ge(s_pa, base + 27)
                            s.dma_start(dbg_selm_d[tb * 128:(tb + 1) * 128, :],
                                        selm[:, :]).then_inc(s_db, 16)
                        dbc.inc(32)
                with nc.Block() as blk:
                    mm0 = mm.v

                    @blk.tensor
                    def _(t):
                        t.transpose(tpsm[:, 0:128], selm[:, 0:128],
                                    idn[:, :]).then_inc(s_mm, 1)
                        t.transpose(tpsm[:, 128:256], selm[:, 128:256],
                                    idn[:, :]).then_inc(s_mm, 1)
                    mm.inc(2)

                    @blk.vector
                    def _(v):
                        v.wait_ge(s_mm, mm0 + 1)
                        v.tensor_copy(selmT[:, tb * 128:(tb + 1) * 128],
                                      tpsm[:, 0:128])
                        v.wait_ge(s_mm, mm0 + 2)
                        v.tensor_copy(selmT[:, TPC + tb * 128:
                                          TPC + (tb + 1) * 128],
                                      tpsm[:, 128:256])

        if L1A_DEBUG:
            with nc.Block() as blk:
                @blk.sync
                def _(s):
                    s.dma_start(dbg_ckv_d[:, :], ckvnT[:, :]).then_inc(s_db, 16)
                    s.dma_start(dbg_kc_d[:, :], kcompT[:, :]).then_inc(s_db, 16)
                    s.dma_start(dbg_wih_d[:, :], wih[:, :]).then_inc(s_db, 16)
                    s.dma_start(dbg_dqT_d[:, :], dqT[:, :]).then_inc(s_db, 16)
                    s.dma_start(dbg_swk_d[:, :], swkT[:, :]).then_inc(s_db, 16)
                    s.dma_start(dbg_swv_d[:, :], swv[:, :]).then_inc(s_db, 16)
                    s.wait_ge(s_db, dbc.inc(96))

        st_b4_p7 = st.v
        with contextlib.ExitStack() as es:
            _e = es.enter_context
            qTh = _e(nc.sbuf_tensor("qTh", [128, 4 * TPC], BF16, side="right"))
            qjg = _e(nc.sbuf_tensor("qjg", [128, 16 * 512], BF16,
                                    side="right"))
            swo = _e(nc.sbuf_tensor("swo", [128, 1024], F32, side="right"))
            exps = _e(nc.sbuf_tensor("exps", [128, 2048], BF16, side="right"))
            zit = _e(nc.sbuf_tensor("zit", [128, 1024], F32, side="right"))
            attg = _e(nc.sbuf_tensor("attg", [128, 4 * TPC], BF16,
                                     side="right"))
            o2T = _e(nc.sbuf_tensor("o2T", [128, 4 * TPC], BF16, side="right"))
            fwc = _e(nc.sbuf_tensor("fwc", [128, 4 * D], BF16, side="right"))
            gwsb = _e(nc.sbuf_tensor("gwsb", [128, 4 * 512], BF16,
                                     side="right"))
            tmp0 = _e(nc.sbuf_tensor("tmp0l", [128, D], F32, side="right"))
            tmp1 = _e(nc.sbuf_tensor("tmp1l", [128, D], F32, side="right"))
            swe = _e(nc.sbuf_tensor("swe", [128, 2048], BF16, side="right"))
            zsw = _e(nc.sbuf_tensor("zsw", [128, 1024], F32, side="right"))
            pA0 = _e(nc.psum_tensor("pA0", [128, 1024], F32))
            pA1 = _e(nc.psum_tensor("pA1", [128, 1024], F32))
            pB = _e(nc.psum_tensor("pB", [128, 1024], F32))
            pC = _e(nc.psum_tensor("pC", [128, 1024], F32))

            for gidx in range(4):
                with nc.Block() as blk:
                    @blk.sync
                    def _(s):
                        s.dma_start(
                            qTh[:, :].rearrange("p (h t) -> p h t", h=4),
                            qT_dram[4 * gidx:4 * gidx + 4, :, :]
                            .rearrange("h p t -> p h t")).then_inc(s_ld, 16)
                        s.wait_ge(s_st, st_b4_p7)
                    ldS.inc(16)

                with nc.Block() as blk:
                    pa0 = pa.v
                    pb0 = pb.v
                    mm0 = mm.v
                    ev0 = ev.v
                    ld0 = ldS.v

                    @blk.tensor
                    def _(t):
                        t.wait_ge(s_ld, ld0)
                        for p8 in range(8):
                            hh = p8 // 2
                            qh = p8 % 2
                            qsb = qTh[:, hh * TPC + qh * 1024:
                                      hh * TPC + qh * 1024 + 1024]
                            if p8 >= 1:
                                t.wait_ge(s_pb, pb0 + 2 * p8 - 1)
                            mi = None
                            for nb2 in range(2):
                                ps = pA0 if nb2 == 0 else pA1
                                for q in range(2):
                                    mi = t.matmul(
                                        ps[:, q * 512:(q + 1) * 512],
                                        ckvnT[:, nb2 * 128:(nb2 + 1) * 128],
                                        qsb[:, q * 512:(q + 1) * 512],
                                        start=True, stop=True)
                            mi.then_inc(s_mm, 1)
                            t.wait_ge(s_pa, pa0 + 2 * p8 + 1)
                            if p8 >= 1:
                                t.wait_ge(s_ev, ev0 + p8)
                            for nb2 in range(2):
                                for q in range(2):
                                    t.matmul(pB[:, q * 512:(q + 1) * 512],
                                             ones[:, :],
                                             exps[:, nb2 * 1024 + q * 512:
                                                  nb2 * 1024 + (q + 1) * 512],
                                             start=(nb2 == 0),
                                             stop=(nb2 == 1))
                                    mi = t.matmul(
                                        pC[:, q * 512:(q + 1) * 512],
                                        ckvn[:, nb2 * 128:(nb2 + 1) * 128],
                                        exps[:, nb2 * 1024 + q * 512:
                                             nb2 * 1024 + (q + 1) * 512],
                                        start=(nb2 == 0), stop=(nb2 == 1))
                            mi.then_inc(s_mm, 1)
                    mm.inc(16)

                    @blk.scalar
                    def _(s):
                        for p8 in range(8):
                            s.wait_ge(s_mm, mm0 + 2 * p8 + 1)
                            s.activation(exps[:, 0:1024], pA0[:, :],
                                         AF.Exp, scale=ISCALE)
                            s.activation(exps[:, 1024:2048], pA1[:, :],
                                         AF.Exp,
                                         scale=ISCALE).then_inc(s_pb, 1)
                            s.wait_ge(s_pa, pa0 + 2 * p8 + 2)
                            s.activation(zit[:, :], zit[:, :], AF.Ln)
                            s.activation(zit[:, :], zit[:, :], AF.Exp,
                                         scale=-1.0).then_inc(s_pb, 1)
                    pb.v = pb0 + 16

                    @blk.vector
                    def _(v):
                        for p8 in range(8):
                            hh = p8 // 2
                            qh = p8 % 2
                            v.wait_ge(s_pb, pb0 + 2 * p8 + 1)
                            v.tensor_tensor(
                                exps[:, 0:1024], exps[:, 0:1024],
                                selmT[:, qh * 1024:qh * 1024 + 1024],
                                ALU.mult)
                            v.tensor_tensor(
                                exps[:, 1024:2048], exps[:, 1024:2048],
                                selmT[:, TPC + qh * 1024:
                                      TPC + qh * 1024 + 1024],
                                ALU.mult).then_inc(s_pa, 1)
                            v.wait_ge(s_mm, mm0 + 2 * p8 + 2)
                            v.tensor_scalar(zit[:, :], pB[:, :], 1.0, 1e-30,
                                            ALU.mult,
                                            ALU.add).then_inc(s_pa, 1)
                            v.wait_ge(s_pb, pb0 + 2 * p8 + 2)
                            v.tensor_tensor(
                                attg[:, hh * TPC + qh * 1024:
                                     hh * TPC + qh * 1024 + 1024],
                                pC[:, :], zit[:, :],
                                ALU.mult).then_inc(s_ev, 1)
                    pa.v = pa0 + 16
                    ev.inc(8)

                if True:
                    with nc.Block() as blk:
                        mm0 = mm.v
                        pa0 = pa.v
                        pb0 = pb.v
                        ev0 = ev.v
                        ld0 = ldS.v

                        @blk.sync
                        def _(s):
                            s.wait_ge(s_st, st_b4_p7)
                            s.dma_start(
                                qjg[:, :].rearrange("p (j n) -> p j n", j=16),
                                qT2_dram[:, :, 4 * gidx:4 * gidx + 4, :]
                                .rearrange("j p h t -> p j (h t)")
                                ).then_inc(s_ld, 16)
                        ldS.inc(16)

                        @blk.tensor
                        def _(t):
                            t.wait_ge(s_ld, ld0 + 16)
                            for j in range(16):
                                slots = [j, 1 + j, 17 + j, 18 + j]
                                if j >= 1:
                                    t.wait_ge(s_pb, pb0 + 2 * j - 1)
                                mi = None
                                for c4, sl in enumerate(slots):
                                    ps = pA0 if c4 < 2 else pA1
                                    mi = t.matmul(
                                        ps[:, (c4 % 2) * 512:
                                           (c4 % 2 + 1) * 512],
                                        swkT[:, sl * 128:(sl + 1) * 128],
                                        qjg[:, j * 512:(j + 1) * 512],
                                        start=True, stop=True)
                                mi.then_inc(s_mm, 1)
                                t.wait_ge(s_pa, pa0 + 2 * j + 1)
                                if j >= 2:
                                    t.wait_ge(s_ev, ev0 + j - 1)
                                for c4, sl in enumerate(slots):
                                    t.matmul(pB[:, (j % 2) * 512:
                                                (j % 2 + 1) * 512],
                                             ones[:, :],
                                             swe[:, c4 * 512:(c4 + 1) * 512],
                                             start=(c4 == 0), stop=(c4 == 3))
                                    mi = t.matmul(
                                        pC[:, (j % 2) * 512:(j % 2 + 1) * 512],
                                        swv[:, sl * 128:(sl + 1) * 128],
                                        swe[:, c4 * 512:(c4 + 1) * 512],
                                        start=(c4 == 0), stop=(c4 == 3))
                                mi.then_inc(s_mm, 1)
                        mm.inc(32)

                        @blk.scalar
                        def _(s):
                            for j in range(16):
                                s.wait_ge(s_mm, mm0 + 2 * j + 1)
                                s.activation(swe[:, 0:1024], pA0[:, :],
                                             AF.Exp, scale=ISCALE)
                                s.activation(swe[:, 1024:2048], pA1[:, :],
                                             AF.Exp,
                                             scale=ISCALE).then_inc(s_pb, 1)
                                s.wait_ge(s_pa, pa0 + 2 * j + 2)
                                zh = zsw[:, (j % 2) * 512:(j % 2 + 1) * 512]
                                s.activation(zh, zh, AF.Ln)
                                s.activation(zh, zh, AF.Exp,
                                             scale=-1.0).then_inc(s_pb, 1)
                        pb.v = pb0 + 32

                        @blk.vector
                        def _(v):
                            for j in range(16):
                                v.wait_ge(s_pb, pb0 + 2 * j + 1)
                                v.tensor_tensor(swe[:, :], swe[:, :],
                                                sw0[:, :] if j == 0
                                                else sw1[:, :],
                                                ALU.mult).then_inc(s_pa, 1)
                                v.wait_ge(s_mm, mm0 + 2 * j + 2)
                                v.tensor_copy(
                                    zsw[:, (j % 2) * 512:(j % 2 + 1) * 512],
                                    pB[:, (j % 2) * 512:(j % 2 + 1) * 512]
                                    ).then_inc(s_pa, 1)
                                v.wait_ge(s_pb, pb0 + 2 * j + 2)
                                v.tensor_tensor(
                                    swo[:, (j % 2) * 512:(j % 2 + 1) * 512],
                                    pC[:, (j % 2) * 512:(j % 2 + 1) * 512],
                                    zsw[:, (j % 2) * 512:(j % 2 + 1) * 512],
                                    ALU.mult)
                                vi = None
                                for hh in range(4):
                                    vi = v.tensor_tensor(
                                        attg[:, hh * TPC + j * 128:
                                             hh * TPC + (j + 1) * 128],
                                        attg[:, hh * TPC + j * 128:
                                             hh * TPC + (j + 1) * 128],
                                        swo[:, (j % 2) * 512 + hh * 128:
                                            (j % 2) * 512 + (hh + 1) * 128],
                                        ALU.add)
                                vi.then_inc(s_ev, 1)
                        pa.v = pa0 + 32
                        ev.inc(16)

                if True:
                    with nc.Block() as blk:
                        base = pa.v
                        mm0 = mm.v
                        lw0 = ldW.v

                        @blk.gpsimd
                        def _(g):
                            g.dma_start(gwsb[:, :], gw_d[gidx, :, :]
                                        ).then_inc(s_lw, 16)
                        ldW.inc(16)

                        @blk.tensor
                        def _(t):
                            t.wait_ge(s_lw, ldW.v)
                            for ob in range(4):
                                mi = None
                                for ic in range(4):
                                    for q in range(4):
                                        mi = t.matmul(
                                            pA0[:, q * 512:(q + 1) * 512]
                                            if q < 2 else
                                            pA1[:, (q - 2) * 512:(q - 1) * 512],
                                            gwsb[:, ic * 512 + ob * 128:
                                                 ic * 512 + (ob + 1) * 128],
                                            attg[:, ic * TPC + q * 512:
                                                 ic * TPC + (q + 1) * 512],
                                            start=(ic == 0), stop=(ic == 3))
                                mi.then_inc(s_mm, 1)
                                t.wait_ge(s_pa, base + ob + 1)
                        mm.inc(4)

                        @blk.vector
                        def _(v):
                            for ob in range(4):
                                v.wait_ge(s_mm, mm0 + ob + 1)
                                v.tensor_copy(
                                    o2T[:, ob * TPC:ob * TPC + 1024],
                                    pA0[:, :])
                                v.tensor_copy(
                                    o2T[:, ob * TPC + 1024:(ob + 1) * TPC],
                                    pA1[:, :]).then_inc(s_pa, 1)
                        pa.v = base + 4

                    with nc.Block() as blk:
                        base = pa.v
                        mm0 = mm.v
                        lw0 = ldW.v
                        ac0 = ac.v
                        ev0 = ev.v

                        @blk.gpsimd
                        def _(g):
                            for ob in range(4):
                                g.dma_start(
                                    fwc[:, ob * D:(ob + 1) * D],
                                    fw_d[(gidx * 4 + ob) * 128:
                                         (gidx * 4 + ob + 1) * 128, :]
                                    ).then_inc(s_lw, 16)
                            for tb in range(16):
                                tm = tmp0 if tb % 2 == 0 else tmp1
                                g.wait_ge(s_ev, ev0 + 2 * tb + 2)
                                g.dma_start(
                                    x3_d[tb * 128:(tb + 1) * 128, :],
                                    tm[:, :],
                                    accum_op=ALU.add).then_inc(s_ac, 16)
                        ldW.inc(64)
                        ac.inc(256)

                        @blk.tensor
                        def _(t):
                            t.wait_ge(s_lw, lw0 + 64)
                            for tb in range(16):
                                ps = pB if tb % 2 == 0 else pC
                                if tb >= 2:
                                    t.wait_ge(s_ev, ev0 + 2 * tb - 2)
                                mi = None
                                for dh in range(2):
                                    for ob in range(4):
                                        for q in range(2):
                                            mi = t.matmul(
                                                ps[:, q * 512:(q + 1) * 512],
                                                o2T[:, ob * TPC + tb * 128:
                                                    ob * TPC + (tb + 1) * 128],
                                                fwc[:, ob * D + dh * 1024 +
                                                    q * 512:
                                                    ob * D + dh * 1024 +
                                                    (q + 1) * 512],
                                                start=(ob == 0),
                                                stop=(ob == 3))
                                    mi.then_inc(s_mm, 1)
                                    if dh == 0:
                                        t.wait_ge(s_ev, ev0 + 2 * tb + 1)
                        mm.inc(32)

                        @blk.vector
                        def _(v):
                            for tb in range(16):
                                ps = pB if tb % 2 == 0 else pC
                                tm = tmp0 if tb % 2 == 0 else tmp1
                                if tb >= 2:
                                    v.wait_ge(s_ac, ac0 + 16 * (tb - 1))
                                for dh in range(2):
                                    v.wait_ge(s_mm, mm0 + 2 * tb + dh + 1)
                                    ci = v.tensor_copy(
                                        tm[:, dh * 1024:(dh + 1) * 1024],
                                        ps[:, :])
                                    ci.then_inc(s_ev, 1)
                        ev.inc(32)

        with nc.Block() as blk:
            @blk.sync
            def _(s):
                s.wait_ge(s_ld, ldS.v)
                s.wait_ge(s_st, st.v)

            @blk.gpsimd
            def _(g):
                g.wait_ge(s_lw, ldW.v)
                g.wait_ge(s_ac, ac.v)
                g.wait_ge(s_cc, ccc.v)
    return p


def l1a_masks(core):
    q = core % 2
    pos = own_positions(core)
    mblock_end = np.zeros(256, dtype=np.int64)
    for n in range(256):
        hf, jj = divmod(n, 128)
        j8, k8 = divmod(jj, 8)
        gblk = 2 * j8 + hf
        mstart = gblk * 8 + k8
        mblock_end[n] = mstart * 16 + 15
    tglob = pos
    cm01 = (mblock_end[None, :] < tglob[:, None]).astype(np.float32)
    cmA = (cm01 - 1.0) * 30000.0
    kk = np.arange(128)[:, None]
    qq = np.arange(128)[None, :]
    tri_up = (kk > qq).astype(np.float32)
    low_in = (kk <= qq).astype(np.float32)
    full = np.ones((128, 128), np.float32)
    zero = np.zeros((128, 128), np.float32)
    if q == 0:
        c1 = [tri_up, low_in, full, zero]
        c0 = [zero, low_in, zero, zero]
    else:
        c1 = [zero, full, tri_up, low_in]
        c0 = [zero, full, zero, low_in]
    m1 = np.concatenate([np.tile(ch, (1, 4)) for ch in c1], axis=1)
    m0 = np.concatenate([np.tile(ch, (1, 4)) for ch in c0], axis=1)
    return (np.ascontiguousarray(cmA.astype(np.float32)),
            np.ascontiguousarray(cm01.astype(BF)),
            np.ascontiguousarray(m0.astype(BF)),
            np.ascontiguousarray(m1.astype(BF)))


def l1a_inmaps(x2_full, z):
    wqc = f32a(z["L1_attn_w_qc"]); wqup = f32a(z["L1_attn_w_qup"])
    kvw = f32a(z["L1_attn_kvc_w"]); kvz = f32a(z["L1_attn_kvc_wz"])
    kvbias = f32a(z["L1_attn_kvc_bias"])
    kpw = f32a(z["L1_attn_k_proj_w"]); vpw = f32a(z["L1_attn_v_proj_w"])
    icw = f32a(z["L1_attn_idx_c_w"]); icz = f32a(z["L1_attn_idx_c_wz"])
    icbias = f32a(z["L1_attn_idx_c_bias"])
    wdq = f32a(z["L1_attn_w_dq"]); wiuq = f32a(z["L1_attn_w_iuq"])
    ww = f32a(z["L1_attn_w_w"])
    qnwv = f32a(z["L1_attn_q_norm_w"]); knwv = f32a(z["L1_attn_k_norm_w"])
    gweight = f32a(z["L1_attn_group_w"]); fw = f32a(z["L1_attn_final_w"])
    rd = 64
    inv = 1.0 / (10000.0 ** (np.arange(0, rd, 2, dtype=np.float64) / rd))
    ang = np.arange(T, dtype=np.float64)[:, None] * inv
    cosT = np.cos(ang).astype(np.float32)
    sinT = np.sin(ang).astype(np.float32)
    g8m = np.zeros((128, 8), dtype=np.float32)
    for p_ in range(128):
        g8m[p_, p_ // 16] = 1.0
    ident = np.eye(128, dtype=BF)
    onesb = np.ones((128, 128), dtype=BF)
    gw_pre = np.stack([pre_fmaj(gweight[g_]) for g_ in range(4)])
    maps = []
    for c in range(8):
        pos = own_positions(c)
        cmA, cm01, m0, m1 = l1a_masks(c)
        maps.append({
            "x2": np.ascontiguousarray(x2_full[pos + (c // 2) * T]),
            "wqc_pre": f32a(pre_fmaj(wqc)),
            "wqup_pre": f32a(pre_fmaj(wqup)),
            "kvcw_pre": f32a(pre_fmaj(kvw)),
            "kvcz_pre": f32a(pre_fmaj(kvz)),
            "kproj_pre": f32a(pre_fmaj(kpw)),
            "vproj_pre": f32a(pre_fmaj(vpw)),
            "idxcw_pre": f32a(pre_fmaj(icw)),
            "idxcz_pre": f32a(pre_fmaj(icz)),
            "wdq_pre": f32a(pre_fmaj(wdq)),
            "wiuq": wiuq,
            "ww_pre": f32a(pre_fmaj(ww)),
            "gw_pre": f32a(gw_pre),
            "final_w": fw,
            "kvb_rep": np.ascontiguousarray(np.tile(kvbias, (8, 1))),
            "icb_rep": np.ascontiguousarray(np.tile(icbias, (8, 1))),
            "knw_rep": np.ascontiguousarray(
                np.broadcast_to(knwv[None, :], (128, 128)).astype(np.float32)),
            "qnw_col": np.ascontiguousarray(qnwv.reshape(128, 1)),
            "l1qcos": np.ascontiguousarray(cosT[pos].T.astype(BF)),
            "l1qsin": np.ascontiguousarray(sinT[pos].T.astype(BF)),
            "swcos": np.ascontiguousarray(cosT[pos]),
            "swsin": np.ascontiguousarray(sinT[pos]),
            "cmaskA": cmA, "cmask01": cm01,
            "swmask0": m0, "swmask1": m1,
            "g8_in": np.ascontiguousarray(g8m.astype(BF)),
            "ident": ident, "ones_bf": onesb,
        })
    return maps


# version 17 (final)
# speedup vs baseline: 1.0059x; 1.0059x over previous
import contextlib
import numpy as np
import ml_dtypes
import concourse.bass as bass
import concourse.mybir as mybir
from concourse.bass_utils import run_bass_kernel_spmd

F32 = mybir.dt.float32
BF16 = mybir.dt.bfloat16
F16 = mybir.dt.float16
I32 = mybir.dt.int32
AF = mybir.ActivationFunctionType
ALU = mybir.AluOpType
AX = mybir.AxisListType
BF = ml_dtypes.bfloat16

B, T, D = 4, 4096, 2048
TPC = 2048
NSLOT = 16
FC = D // 128
LN_EPS = 1e-5
NEGB = -30000.0
ISCALE = 0.08838834764831845
L1A_DEBUG = False
L0F_DEBUG = False
L0F_SERIAL_CC = True
L0A_PHASES = 4
L0A_P2 = 0
L0A_QT_OUT = False



def own_positions(core):
    q = core % 2
    pos = []
    for j in range(NSLOT):
        base = (2 * j + q) * 128
        pos.extend(range(base, base + 128))
    return np.array(pos, dtype=np.int64)


def pre_fmaj(W):
    Fdim, O = W.shape
    return np.ascontiguousarray(
        W.reshape(Fdim // 128, 128, O).transpose(1, 0, 2).reshape(128, -1))


def pre_headmaj(W, hd=128):
    Fdim, O = W.shape
    w = W.reshape(Fdim // 128, 128, O // hd, hd).transpose(2, 1, 0, 3)
    return np.ascontiguousarray(w.reshape(O // hd, 128, (Fdim // 128) * hd))


def l0_rope_tabs(core):
    hd = 128
    freqs = 1.0 / (10000.0 ** (np.arange(0, hd, 2, dtype=np.float64) / hd))
    emb = np.outer(np.arange(T, dtype=np.float64), freqs)
    cos = np.concatenate([np.cos(emb), np.cos(emb)], -1).astype(np.float32)
    sin = np.concatenate([np.sin(emb), np.sin(emb)], -1).astype(np.float32)
    op = own_positions(core)
    kp = np.concatenate([op, own_positions(core ^ 1)])
    return (np.ascontiguousarray(cos[op].T.astype(BF)),
            np.ascontiguousarray(sin[op].T.astype(BF)),
            np.ascontiguousarray(cos[kp].T.astype(BF)),
            np.ascontiguousarray(sin[kp].T.astype(BF)))


def l0_pmask(core):
    m = np.zeros((128, NSLOT + 1), dtype=np.float32)
    if core % 2 == 0:
        m[:, :NSLOT] = NEGB
    return m


def f32a(x):
    return np.ascontiguousarray(x, dtype=np.float32)



class Ctr:
    def __init__(self):
        self.v = 0

    def inc(self, n=1):
        self.v += n
        return self.v


def emit_ln_sc(s, sem, base, x_ap, h_ap, stats, junk, prewait=None):
    if prewait is not None:
        s.wait_ge(prewait[0], prewait[1])
    s.activation(junk, x_ap, AF.Copy, accum_out=stats[:, 0:1])
    s.activation(h_ap, x_ap, AF.Square,
                 accum_out=stats[:, 1:2]).then_inc(sem, 1)
    s.wait_ge(sem, base + 4)
    s.activation(stats[:, 5:6], stats[:, 4:5],
                 AF.Sqrt).then_inc(sem, 1)


def emit_ln_tile(blk, sem, c, x_ap, h_ap, stats, junk, width=D, eps=LN_EPS,
                 prewait=None):
    base = c.v

    @blk.scalar
    def _(s):
        emit_ln_sc(s, sem, base, x_ap, h_ap, stats, junk, prewait)

    @blk.vector
    def _(v):
        emit_ln_vec(v, sem, base, x_ap, h_ap, stats, width, eps)
    c.v = base + 8


def emit_ln_vec(v, sem, base, x_ap, h_ap, stats, width=D, eps=LN_EPS):
    v.wait_ge(sem, base + 1)
    v.tensor_tensor(stats[:, 2:3], stats[:, 0:1], stats[:, 0:1],
                    ALU.mult)
    v.tensor_scalar_mul(stats[:, 7:8], stats[:, 0:1],
                        -1.0 / width).then_inc(sem, 1)
    v.wait_ge(sem, base + 2)
    v.scalar_tensor_tensor(stats[:, 3:4], stats[:, 2:3], -1.0 / width,
                           stats[:, 1:2], ALU.mult,
                           ALU.add).then_inc(sem, 1)
    v.wait_ge(sem, base + 3)
    v.tensor_scalar(stats[:, 4:5], stats[:, 3:4], 1.0 / width, eps,
                    ALU.mult, ALU.add).then_inc(sem, 1)
    v.wait_ge(sem, base + 5)
    v.reciprocal(stats[:, 5:6], stats[:, 5:6]).then_inc(sem, 1)
    v.wait_ge(sem, base + 6)
    v.tensor_tensor(stats[:, 6:7], stats[:, 7:8], stats[:, 5:6],
                    ALU.mult).then_inc(sem, 1)
    v.wait_ge(sem, base + 7)
    v.tensor_scalar(h_ap, x_ap, stats[:, 5:6], stats[:, 6:7],
                    ALU.mult, ALU.add).then_inc(sem, 1)


def emit_ln_phase(nc, idn, x_d, hT, xt, htile, junk, stats, tps0, tps1,
                  s_ld, s_pa, s_mm, s_pb, ldS, pa, mm, pb, ntiles=16,
                  src_fn=None, dst_fn=None):
    if src_fn is None:
        def src_fn(g):
            return x_d[g * 128:(g + 1) * 128, :]
    if dst_fn is None:
        def dst_fn(g, fc):
            return hT[:, fc * TPC + g * 128:fc * TPC + (g + 1) * 128]
    with nc.Block() as blk:
        ld0 = ldS.v
        pa0 = pa.v
        mm0 = mm.v
        pb0 = pb.v

        @blk.sync
        def _(s):
            s.dma_start(xt[:, 0:D], src_fn(0)).then_inc(s_ld, 16)
            for g in range(1, ntiles):
                if g >= 2:
                    s.wait_ge(s_pa, pa0 + 8 * (g - 2) + 8)
                s.dma_start(xt[:, (g % 2) * D:(g % 2 + 1) * D],
                            src_fn(g)).then_inc(s_ld, 16)
        ldS.inc(16 * ntiles)

        @blk.scalar
        def _(s):
            for g in range(ntiles):
                off = (g % 2) * 8
                if g >= 2:
                    s.wait_ge(s_mm, mm0 + 16 * (g - 1))
                emit_ln_sc(s, s_pa, pa0 + 8 * g,
                           xt[:, (g % 2) * D:(g % 2 + 1) * D],
                           htile[:, (g % 2) * D:(g % 2 + 1) * D],
                           stats[:, off:off + 8], junk[:, :],
                           prewait=(s_ld, ld0 + 16 * (g + 1)))

        @blk.vector
        def _(v):
            for g in range(ntiles):
                off = (g % 2) * 8
                emit_ln_vec(v, s_pa, pa0 + 8 * g,
                            xt[:, (g % 2) * D:(g % 2 + 1) * D],
                            htile[:, (g % 2) * D:(g % 2 + 1) * D],
                            stats[:, off:off + 8])
                for fc in range(FC):
                    ps = tps0 if fc % 2 == 0 else tps1
                    v.wait_ge(s_mm, mm0 + 16 * g + fc + 1)
                    v.tensor_copy(dst_fn(g, fc),
                                  ps[:, 0:128]).then_inc(s_pb, 1)

        @blk.tensor
        def _(t):
            for g in range(ntiles):
                t.wait_ge(s_pa, pa0 + 8 * g + 8)
                for fc in range(FC):
                    k = 16 * g + fc
                    ps = tps0 if fc % 2 == 0 else tps1
                    if k >= 2:
                        t.wait_ge(s_pb, pb0 + k - 1)
                    t.transpose(ps[:, 0:128],
                                htile[:, (g % 2) * D + fc * 128:
                                       (g % 2) * D + (fc + 1) * 128],
                                idn[:, :]).then_inc(s_mm, 1)
        pa.v = pa0 + 8 * ntiles
        mm.inc(16 * ntiles)
        pb.v = pb0 + 16 * ntiles


def emit_rope128(v, dst, src, cos, sin, scr, width):
    v.tensor_copy(scr[0:64, 0:width], src[64:128, :])
    v.tensor_copy(scr[64:128, 0:width], src[0:64, :])
    v.tensor_tensor(scr[:, 0:width], scr[:, 0:width], sin[:, :], ALU.mult)
    v.tensor_tensor(dst, src, cos, ALU.mult)
    v.tensor_tensor(dst[0:64, :], dst[0:64, :], scr[0:64, 0:width],
                    ALU.subtract)
    return v.tensor_tensor(dst[64:128, :], dst[64:128, :],
                           scr[64:128, 0:width], ALU.add)


class Prog:
    def __init__(self):
        self.nc = bass.Bass("TRN2", target_bir_lowering=False)
        self._cache = {}
        self.override = {}
        self._pfx = ""

    def din(self, name, shape, dtype):
        if name in self.override:
            return self.override[name]
        if name in self._cache:
            return self._cache[name]
        ap = self.nc.dram_tensor(name, list(shape), dtype,
                                 kind="ExternalInput").ap()
        self._cache[name] = ap
        return ap

    def dout(self, name, shape, dtype):
        if name in self.override:
            return self.override[name]
        return self.nc.dram_tensor(name, list(shape), dtype,
                                   kind="ExternalOutput").ap()

    def dtmp(self, name, shape, dtype, addr_space="Local"):
        return self.nc.dram_tensor(self._pfx + name, list(shape), dtype,
                                   kind="Internal",
                                   addr_space=addr_space).ap()


class _NCPfx:

    def __init__(self, nc, pfx):
        self._nc = nc
        self._pfx = pfx

    def sbuf_tensor(self, name, *a, **k):
        return self._nc.sbuf_tensor(self._pfx + name, *a, **k)

    def psum_tensor(self, name, *a, **k):
        return self._nc.psum_tensor(self._pfx + name, *a, **k)

    def semaphore(self, name):
        return self._nc.semaphore(self._pfx + name)

    def __getattr__(self, n):
        return getattr(self._nc, n)



class _StopEmit(Exception):
    pass


def build_l0a(p=None, pfx=""):
    try:
        return _build_l0a(p, pfx)
    except _StopEmit:
        return _BUILD_HOLDER[0]


_BUILD_HOLDER = [None]


def _build_l0a(p=None, pfx=""):
    if p is None:
        p = Prog()
    p._pfx = pfx
    _BUILD_HOLDER[0] = p
    nc = _NCPfx(p.nc, pfx)
    x_own = p.din("x_own", (TPC, D), F32)
    x_oth = p.din("x_oth", (TPC, D), F32)
    Wq_d = p.din("Wq_pre", (16, 128, FC * 128), F32)
    Wk_d = p.din("Wk_pre", (128, FC * 512), F32)
    Wv_d = p.din("Wv_pre", (128, FC * 512), F32)
    Wo_d = p.din("Wo_pre", (128, FC * D), F32)
    qcos_d = p.din("qcos", (128, TPC), BF16)
    qsin_d = p.din("qsin", (128, TPC), BF16)
    kcos_d = p.din("kcos", (128, 2 * TPC), BF16)
    ksin_d = p.din("ksin", (128, 2 * TPC), BF16)
    pmask_d = p.din("pmask", (128, NSLOT + 1), F32)
    ident_d = p.din("ident", (128, 128), BF16)
    ones_d = p.din("ones_bf", (128, 128), BF16)
    x1_out = p.dout("x1", (TPC, D), F32)
    if L0A_QT_OUT:
        qT_dram = p.dout("qT_scr", (16, 128, TPC), BF16)
    else:
        qT_dram = p.dtmp("qT_scr", (16, 128, TPC), BF16)

    ldS = Ctr(); ldW = Ctr(); pa = Ctr(); pb = Ctr(); mm = Ctr(); ex = Ctr(); pv = Ctr()
    qd = Ctr(); st = Ctr(); ev = Ctr()

    with (
        nc.sbuf_tensor("kT", [128, 4 * 4096], BF16) as kT,
        nc.sbuf_tensor("v_sb", [128, 32 * 512], BF16) as v_sb,
        nc.sbuf_tensor("idn", [128, 128], BF16) as idn,
        nc.sbuf_tensor("ones", [128, 128], BF16) as ones,
        nc.sbuf_tensor("pmsk", [128, NSLOT + 1], F32) as pmsk,
        nc.sbuf_tensor("stats", [128, 8], F32) as stats,
        nc.semaphore("s_ld") as s_ld,
        nc.semaphore("s_lw") as s_lw,
        nc.semaphore("s_pa") as s_pa,
        nc.semaphore("s_pb") as s_pb,
        nc.semaphore("s_mm") as s_mm,
        nc.semaphore("s_ex") as s_ex,
        nc.semaphore("s_pv") as s_pv,
        nc.semaphore("s_qd") as s_qd,
        nc.semaphore("s_st") as s_st,
        nc.semaphore("s_ev") as s_ev,
    ):
        with nc.Block() as blk:
            @blk.sync
            def _(s):
                s.dma_start(idn[:, :], ident_d[:, :]).then_inc(s_ld, 16)
                s.dma_start(ones[:, :], ones_d[:, :]).then_inc(s_ld, 16)
                s.dma_start(pmsk[:, :], pmask_d[:, :]).then_inc(s_ld, 16)
                s.wait_ge(s_ld, ldS.inc(48))

        with nc.sbuf_tensor("hT", [128, FC * TPC], BF16, side="right") as hT:
            with contextlib.ExitStack() as _es1:
                _e = _es1.enter_context
                xt = _e(nc.sbuf_tensor("xt", [128, 2 * D], F32, side="right"))
                htile = _e(nc.sbuf_tensor("htile", [128, 2 * D], BF16,
                                          side="right"))
                junk = _e(nc.sbuf_tensor("junk", [128, D], BF16, side="right"))
                lstats = _e(nc.sbuf_tensor("lstats", [128, 16], F32,
                                           side="right"))
                kraw = _e(nc.sbuf_tensor("kraw", [128, 2048], BF16,
                                         side="right"))
                kcs = _e(nc.sbuf_tensor("kcs", [128, 512], BF16, side="right"))
                ksn = _e(nc.sbuf_tensor("ksn", [128, 512], BF16, side="right"))
                rsc = _e(nc.sbuf_tensor("rsc", [128, 256], BF16, side="right"))
                Wk_sb = _e(nc.sbuf_tensor("Wk_sb", [128, FC * 512], BF16, side="right"))
                Wv_sb = _e(nc.sbuf_tensor("Wv_sb", [128, FC * 512], BF16, side="right"))
                with nc.Block() as blk:
                    @blk.gpsimd
                    def _(g):
                        for i in range(4):
                            g.dma_start(Wk_sb[:, i * 2048:(i + 1) * 2048],
                                        Wk_d[:, i * 2048:(i + 1) * 2048]
                                        ).then_inc(s_lw, 16)
                            g.dma_start(Wv_sb[:, i * 2048:(i + 1) * 2048],
                                        Wv_d[:, i * 2048:(i + 1) * 2048]
                                        ).then_inc(s_lw, 16)
                        g.wait_ge(s_lw, ldW.inc(128))

                def _p1_kv(tag, prs):
                    with contextlib.ExitStack() as _eps:
                        kps = _eps.enter_context(
                            nc.psum_tensor(f"kps{tag}", [128, 2048], F32))
                        vps = _eps.enter_context(
                            nc.psum_tensor(f"vps{tag}", [128, 2048], F32))
                        with nc.Block() as blk:
                            pa0 = pa.v
                            ld0 = ldS.v

                            @blk.sync
                            def _(s):
                                for j, pr in enumerate(prs):
                                    sl = (j % 2) * 256
                                    if j >= 2:
                                        s.wait_ge(s_pa, pa0 + 4 * (j - 2) + 4)
                                    s.dma_start(
                                        kcs[:, sl:sl + 256],
                                        kcos_d[:, pr * 256:(pr + 1) * 256]
                                        ).then_inc(s_ld, 16)
                                    s.dma_start(
                                        ksn[:, sl:sl + 256],
                                        ksin_d[:, pr * 256:(pr + 1) * 256]
                                        ).then_inc(s_ld, 16)
                            ldS.inc(32 * len(prs))

                            @blk.tensor
                            def _(t):
                                for j, pr in enumerate(prs):
                                    o = (j % 2) * 1024
                                    if j >= 2:
                                        t.wait_ge(s_pa, pa0 + 4 * (j - 2) + 4)
                                    bt = (2 * pr) % 16
                                    mi = None
                                    for kb in range(4):
                                        for fc in range(FC):
                                            mi = t.matmul(
                                                kps[:, o + kb * 256:
                                                    o + (kb + 1) * 256],
                                                Wk_sb[:, fc * 512 + kb * 128:
                                                      fc * 512 + (kb + 1) * 128],
                                                hT[:, fc * TPC + bt * 128:
                                                   fc * TPC + (bt + 2) * 128],
                                                start=(fc == 0),
                                                stop=(fc == FC - 1))
                                    mi.then_inc(s_pa, 1)
                                    for i in range(2):
                                        for fc in range(FC):
                                            mi = t.matmul(
                                                vps[:, o + i * 512:
                                                    o + (i + 1) * 512],
                                                hT[:, fc * TPC + (bt + i) * 128:
                                                   fc * TPC + (bt + i + 1) * 128],
                                                Wv_sb[:, fc * 512:
                                                      (fc + 1) * 512],
                                                start=(fc == 0),
                                                stop=(fc == FC - 1))
                                    mi.then_inc(s_pa, 1)

                            @blk.scalar
                            def _(s):
                                for j, pr in enumerate(prs):
                                    o = (j % 2) * 1024
                                    s.wait_ge(s_pa, pa0 + 4 * j + 1)
                                    si = None
                                    for kb in range(4):
                                        si = s.activation(
                                            kraw[:, o + kb * 256:
                                                 o + (kb + 1) * 256],
                                            kps[:, o + kb * 256:
                                                o + (kb + 1) * 256],
                                            AF.Copy)
                                    si.then_inc(s_pa, 1)

                            @blk.vector
                            def _(v):
                                for j, pr in enumerate(prs):
                                    o = (j % 2) * 1024
                                    tb = (j % 2) * 256
                                    s0 = 2 * pr
                                    v.wait_ge(s_pa, pa0 + 4 * j + 3)
                                    v.wait_ge(s_ld, ld0 + 32 * (j + 1))
                                    for kb in range(4):
                                        emit_rope128(
                                            v,
                                            kT[:, kb * 4096 + s0 * 128:
                                               kb * 4096 + (s0 + 2) * 128],
                                            kraw[:, o + kb * 256:
                                                 o + (kb + 1) * 256],
                                            kcs[:, tb:tb + 256],
                                            ksn[:, tb:tb + 256],
                                            rsc[:, :], 256)
                                    vi = None
                                    for i in range(2):
                                        vi = v.tensor_copy(
                                            v_sb[:, (s0 + i) * 512:
                                                 (s0 + i + 1) * 512],
                                            vps[:, o + i * 512:
                                                o + (i + 1) * 512])
                                    vi.then_inc(s_pa, 1)
                            pa.v = pa0 + 4 * len(prs)

                for half, x_src in ((1, x_oth), (0, x_own)):
                    with contextlib.ExitStack() as _eps:
                        tps0 = _eps.enter_context(
                            nc.psum_tensor(f"tps0h{half}", [128, 1024], BF16))
                        tps1 = _eps.enter_context(
                            nc.psum_tensor(f"tps1h{half}", [128, 1024], BF16))
                        emit_ln_phase(nc, idn, x_src, hT, xt, htile, junk,
                                      lstats, tps0, tps1, s_ld, s_pa, s_mm,
                                      s_pb, ldS, pa, mm, pb, ntiles=16)
                    _p1_kv(half, list(range(8 * half, 8 * half + 8)))

            with contextlib.ExitStack() as _es2:
                if L0A_PHASES < 2:
                    raise _StopEmit()
                _e = _es2.enter_context
                Wq_sb = _e(nc.sbuf_tensor("Wq_sb", [128, 2 * FC * 128], BF16, side="right"))
                qcs = _e(nc.sbuf_tensor("qcs", [128, TPC], BF16, side="right"))
                qsn = _e(nc.sbuf_tensor("qsn", [128, TPC], BF16, side="right"))
                qraw = _e(nc.sbuf_tensor("qraw", [128, TPC], BF16,
                                         side="right"))
                qrope = _e(nc.sbuf_tensor("qrope", [128, TPC], BF16, side="right"))
                qsc = _e(nc.sbuf_tensor("qsc", [128, TPC], BF16, side="right"))
                qps = _e(nc.psum_tensor("qps", [128, 2048], F32))
                with nc.Block() as blk:
                    ld_tab = ldS.v + 32
                    lw_b = ldW.v
                    mm_b = mm.v
                    qd_b = qd.v
                    st_b = st.v

                    @blk.sync
                    def _(s):
                        s.dma_start(qcs[:, :], qcos_d[:, :]).then_inc(s_ld, 16)
                        s.dma_start(qsn[:, :], qsin_d[:, :]).then_inc(s_ld, 16)
                        if L0A_P2 < 1:
                            for h in range(16):
                                s.wait_ge(s_qd, qd_b + h + 1)
                                s.dma_start(qT_dram[h, :, :], qrope[:, :]
                                            ).then_inc(s_st, 16)
                            s.wait_ge(s_st, st_b + 16 * 16)
                    ldS.inc(32)

                    @blk.gpsimd
                    def _(g):
                        for h in range(16):
                            if h >= 2:
                                g.wait_ge(s_mm, mm_b + h - 1)
                            g.dma_start(Wq_sb[:, (h % 2) * FC * 128:
                                              (h % 2 + 1) * FC * 128],
                                        Wq_d[h, :, :]).then_inc(s_lw, 16)

                    @blk.tensor
                    def _(t):
                        for h in range(16):
                            t.wait_ge(s_lw, lw_b + 16 * (h + 1))
                            if h >= 1:
                                t.wait_ge(s_qd, qd_b + h)
                            mi = None
                            for cc in range(4):
                                for fc in range(FC):
                                    mi = t.matmul(
                                        qps[:, cc * 512:(cc + 1) * 512],
                                        Wq_sb[:, (h % 2) * FC * 128 + fc * 128:
                                              (h % 2) * FC * 128 + (fc + 1) * 128],
                                        hT[:, fc * TPC + cc * 512:
                                           fc * TPC + (cc + 1) * 512],
                                        start=(fc == 0), stop=(fc == FC - 1))
                            mi.then_inc(s_mm, 1)

                    @blk.vector
                    def _(v):
                        v.wait_ge(s_ld, ld_tab)
                        for h in range(16):
                            v.wait_ge(s_mm, mm_b + h + 1)
                            if h >= 1 and L0A_P2 < 1:
                                v.wait_ge(s_st, st_b + 16 * h)
                            v.tensor_copy(qraw[:, :], qps[:, :]
                                          ).then_inc(s_ev, 1)
                            v.wait_ge(s_ev, ev.v + h + 1)
                            if L0A_P2 < 2:
                                vi = emit_rope128(v, qrope[:, :], qraw[:, :],
                                                  qcs[:, :], qsn[:, :],
                                                  qsc[:, :], TPC)
                            else:
                                vi = v.tensor_copy(qrope[:, :], qraw[:, :])
                            vi.then_inc(s_qd, 1)

                    ldW.inc(16 * 16)
                    mm.inc(16)
                    qd.inc(16)
                    ev.inc(16)
                    if L0A_P2 < 1:
                        st.inc(16 * 16)

        if L0A_PHASES < 3:
            raise _StopEmit()
        with (
            nc.sbuf_tensor("attnT", [128, 16 * 1024], BF16, side="right") as attnT,
            nc.sbuf_tensor("Wo_sb", [128, FC * D], BF16, side="right") as Wo_sb,
        ):
            with nc.Block() as blk:
                lw_wo = ldW.v

                @blk.gpsimd
                def _(g):
                    for dc in range(FC):
                        g.dma_start(Wo_sb[:, dc * D:(dc + 1) * D],
                                    Wo_d[:, dc * D:(dc + 1) * D]
                                    ).then_inc(s_lw, 16)
                ldW.inc(16 * 16)

            for qh in range(2):
                jmax = 8 * qh + 7
                qoff = 1024 * qh

                with contextlib.ExitStack() as _es3:
                    _e = _es3.enter_context
                    qTh = _e(nc.sbuf_tensor(f"qTh{qh}", [128, 2 * 1024], BF16, side="right"))
                    expT = _e(nc.sbuf_tensor(f"expT{qh}", [128, 2 * 1024], BF16, side="right"))
                    zinv = _e(nc.sbuf_tensor(f"zinv{qh}", [128, 1024], F32, side="right"))
                    sT = _e(nc.psum_tensor(f"sT{qh}", [128, 2048], F32))
                    aps = _e(nc.psum_tensor(f"aps{qh}", [128, 1024], F32))
                    zps = _e(nc.psum_tensor(f"zps{qh}", [128, 1024], F32))
                    with nc.Block() as blk:
                        @blk.sync
                        def _(s):
                            s.dma_start(qTh[:, 0:1024],
                                        qT_dram[0, :, qoff:qoff + 1024]
                                        ).then_inc(s_ld, 16)
                            s.wait_ge(s_ld, ldS.inc(16))

                    def _p3_head(h, qh, qoff, jmax, qTh, expT, zinv, sT,
                                 aps, zps):
                        kh = h // 4
                        chunks = [(True, kc) for kc in range(jmax + 1)] + \
                                 [(False, kc) for kc in range(jmax + 1)]
                        nch = len(chunks)

                        def qlo(kc):
                            return max(kc, 8 * qh) * 128

                        def subrange(kc):
                            c0 = qlo(kc) - qoff
                            return [s for s in range(2) if (s + 1) * 512 > c0]

                        first_t = {}
                        last_t = {}
                        for i, (own, kc) in enumerate(chunks):
                            for s in subrange(kc):
                                first_t.setdefault(s, i)
                                last_t[s] = i

                        with nc.Block() as blk:
                            mmb, exb, pvb = mm.v, ex.v, pv.v
                            ld_q = ldS.v

                            @blk.sync
                            def _(s):
                                if h + 1 < 16:
                                    s.dma_start(
                                        qTh[:, ((h + 1) % 2) * 1024:
                                            ((h + 1) % 2 + 1) * 1024],
                                        qT_dram[h + 1, :, qoff:qoff + 1024]
                                        ).then_inc(s_ld, 16)
                            if h + 1 < 16:
                                ldS.inc(16)

                            def emit_S(t, i, h=h, kh=kh, qoff=qoff):
                                own, kc = chunks[i]
                                half = i % 2
                                c0 = qlo(kc) - qoff
                                s0 = kc if own else 16 + kc
                                mi = None
                                for s in subrange(kc):
                                    a = max(c0, s * 512)
                                    b = (s + 1) * 512
                                    mi = t.matmul(
                                        sT[:, half * 1024 + a:half * 1024 + b],
                                        kT[:, kh * 4096 + s0 * 128:
                                           kh * 4096 + (s0 + 1) * 128],
                                        qTh[:, (h % 2) * 1024 + a:
                                            (h % 2) * 1024 + b],
                                        start=True, stop=True)
                                mi.then_inc(s_mm, 1)

                            def emit_ZP(t, i, kh=kh, qoff=qoff):
                                own, kc = chunks[i]
                                half = i % 2
                                c0 = qlo(kc) - qoff
                                s0 = kc if own else 16 + kc
                                for s in subrange(kc):
                                    a = max(c0, s * 512)
                                    b = (s + 1) * 512
                                    stf = (i == first_t[s])
                                    spf = (i == last_t[s])
                                    t.matmul(zps[:, a:b], ones[:, :],
                                             expT[:, half * 1024 + a:
                                                  half * 1024 + b],
                                             start=stf, stop=spf)
                                    mi = t.matmul(
                                        aps[:, a:b],
                                        v_sb[:, s0 * 512 + kh * 128:
                                             s0 * 512 + (kh + 1) * 128],
                                        expT[:, half * 1024 + a:
                                             half * 1024 + b],
                                        start=stf, stop=spf)
                                mi.then_inc(s_pv, 1)

                            @blk.tensor
                            def _(t):
                                t.wait_ge(s_ld, ld_q)
                                emit_S(t, 0)
                                emit_S(t, 1)
                                for i in range(nch):
                                    t.wait_ge(s_ex, exb + i + 1)
                                    emit_ZP(t, i)
                                    if i + 2 < nch:
                                        emit_S(t, i + 2)

                            @blk.scalar
                            def _(s):
                                for i, (own, kc) in enumerate(chunks):
                                    s.wait_ge(s_mm, mmb + i + 1)
                                    if i >= 2:
                                        s.wait_ge(s_pv, pvb + i - 1)
                                    half = i % 2
                                    c0 = qlo(kc) - qoff
                                    diag = (not own) and (kc >= 8 * qh)
                                    if diag:
                                        si = s.activation(
                                            expT[:, half * 1024 + c0:
                                                 half * 1024 + c0 + 128],
                                            sT[:, half * 1024 + c0:
                                               half * 1024 + c0 + 128],
                                            AF.Exp, scale=ISCALE,
                                            bias=pmsk[:, kc:kc + 1])
                                        if c0 + 128 < 1024:
                                            si = s.activation(
                                                expT[:, half * 1024 + c0 + 128:
                                                     half * 1024 + 1024],
                                                sT[:, half * 1024 + c0 + 128:
                                                   half * 1024 + 1024],
                                                AF.Exp, scale=ISCALE)
                                    else:
                                        si = s.activation(
                                            expT[:, half * 1024 + c0:
                                                 half * 1024 + 1024],
                                            sT[:, half * 1024 + c0:
                                               half * 1024 + 1024],
                                            AF.Exp, scale=ISCALE)
                                    si.then_inc(s_ex, 1)

                            evb = ev.v

                            @blk.vector
                            def _(v):
                                v.wait_ge(s_pv, pvb + nch)
                                v.reciprocal(zinv[:, :], zps[:, :]
                                             ).then_inc(s_ev, 1)
                                v.wait_ge(s_ev, evb + 1)
                                v.tensor_tensor(
                                    attnT[:, h * 1024:(h + 1) * 1024],
                                    aps[:, :], zinv[:, :], ALU.mult)
                            ev.inc(1)

                            mm.inc(nch)
                            ex.inc(nch)
                            pv.inc(nch)

                    for h in range(16):
                        _p3_head(h, qh, qoff, jmax, qTh, expT, zinv, sT,
                                 aps, zps)

                with contextlib.ExitStack() as _es4:
                    if L0A_PHASES < 4:
                        continue
                    _e = _es4.enter_context
                    xres = _e(nc.sbuf_tensor(f"xres{qh}", [128, 2 * D], F32, side="right"))
                    otile = _e(nc.sbuf_tensor(f"otile{qh}", [128, 2 * D], F32, side="right"))
                    ops0 = _e(nc.psum_tensor(f"ops0{qh}", [128, 2048], F32))
                    ops1 = _e(nc.psum_tensor(f"ops1{qh}", [128, 2048], F32))
                    with nc.Block() as blk:
                        ld_b = ldS.v
                        mm_b = mm.v
                        ev_b = ev.v
                        st_b = st.v

                        @blk.sync
                        def _(s):
                            s.dma_start(xres[:, 0:D],
                                        x_own[qoff:qoff + 128, :]
                                        ).then_inc(s_ld, 16)
                            s.dma_start(xres[:, D:2 * D],
                                        x_own[qoff + 128:qoff + 256, :]
                                        ).then_inc(s_ld, 16)
                            for tb in range(8):
                                s.wait_ge(s_ev, ev_b + tb + 1)
                                s.dma_start(
                                    x1_out[qoff + tb * 128:
                                           qoff + (tb + 1) * 128, :],
                                    otile[:, (tb % 2) * D:(tb % 2 + 1) * D]
                                    ).then_inc(s_st, 16)
                                if tb + 2 < 8:
                                    s.dma_start(
                                        xres[:, (tb % 2) * D:(tb % 2 + 1) * D],
                                        x_own[qoff + (tb + 2) * 128:
                                              qoff + (tb + 3) * 128, :]
                                        ).then_inc(s_ld, 16)
                            s.wait_ge(s_st, st_b + 16 * 8)
                        ldS.inc(16 * 8)
                        st.inc(16 * 8)

                        @blk.tensor
                        def _(t):
                            for tb in range(8):
                                ps = ops0 if tb % 2 == 0 else ops1
                                if tb >= 2:
                                    t.wait_ge(s_ev, ev_b + tb - 1)
                                mi = None
                                for dc in range(FC):
                                    t.wait_ge(s_lw, lw_wo + 16 * (dc + 1))
                                    for sc in range(4):
                                        mi = t.matmul(
                                            ps[:, sc * 512:(sc + 1) * 512],
                                            attnT[:, dc * 1024 + tb * 128:
                                                  dc * 1024 + (tb + 1) * 128],
                                            Wo_sb[:, dc * D + sc * 512:
                                                  dc * D + (sc + 1) * 512],
                                            start=(dc == 0), stop=(dc == FC - 1))
                                mi.then_inc(s_mm, 1)

                        @blk.vector
                        def _(v):
                            for tb in range(8):
                                ps = ops0 if tb % 2 == 0 else ops1
                                v.wait_ge(s_mm, mm_b + tb + 1)
                                v.wait_ge(s_ld, ld_b + 16 * (tb + 1))
                                if tb >= 2:
                                    v.wait_ge(s_st, st_b + 16 * (tb - 1))
                                v.tensor_tensor(
                                    otile[:, (tb % 2) * D:(tb % 2 + 1) * D],
                                    ps[:, :],
                                    xres[:, (tb % 2) * D:(tb % 2 + 1) * D],
                                    ALU.add).then_inc(s_ev, 1)

                        mm.inc(8)
                        ev.inc(8)

        with nc.Block() as blk:
            @blk.sync
            def _(s):
                s.wait_ge(s_ld, ldS.v)
                s.wait_ge(s_st, st.v)
                s.wait_ge(s_qd, qd.v)

            @blk.gpsimd
            def _(g):
                g.wait_ge(s_lw, ldW.v)
    return p


def l0a_inmaps(x_full, Wq, Wk, Wv, Wo):
    Wq_pre = f32a(pre_headmaj(Wq))
    Wk_pre = f32a(pre_fmaj(Wk))
    Wv_pre = f32a(pre_fmaj(Wv))
    Wo_pre = f32a(pre_fmaj(Wo))
    ident = np.eye(128, dtype=BF)
    onesb = np.ones((128, 128), dtype=BF)
    maps = []
    for c in range(8):
        qcs, qsn, kcs, ksn = l0_rope_tabs(c)
        maps.append({
            "x_own": np.ascontiguousarray(
                x_full[own_positions(c) + (c // 2) * T]),
            "x_oth": np.ascontiguousarray(
                x_full[own_positions(c ^ 1) + (c // 2) * T]),
            "Wq_pre": Wq_pre, "Wk_pre": Wk_pre, "Wv_pre": Wv_pre,
            "Wo_pre": Wo_pre,
            "qcos": qcs, "qsin": qsn, "kcos": kcs, "ksin": ksn,
            "pmask": l0_pmask(c), "ident": ident, "ones_bf": onesb,
        })
    return maps


def scatter_tokens(res_list, key):
    out = np.zeros((B * T, D), dtype=np.float32)
    for c in range(8):
        out[own_positions(c) + (c // 2) * T] = \
            np.asarray(res_list[c][key]).astype(np.float32)
    return out




GU_OT = 88
DN_IC = 44



def build_l0f(p=None, pfx=""):
    if p is None:
        p = Prog()
    p._pfx = pfx
    nc = _NCPfx(p.nc, pfx)
    x1_d = p.din("x1", (TPC, D), F32)
    pksh_d = p.din("gshard_pk", (17, 128, 256), I32)
    scsh_d = p.din("gshard_sc", (17, 128, 16), F16)
    dqc_d = p.din("dqconsts", (128, 4), F32)
    ident_d = p.din("ident", (128, 128), BF16)
    x2_d = p.dout("x2", (TPC, D), F32)
    emit_x2bf = "x2bf" not in p.override
    x2bf_d = p.dout("x2bf", (TPC, D), BF16) if emit_x2bf else None
    agin_w = p.dtmp("agin_w", (17, 128, D), BF16)
    agout_w = p.dtmp("agout_w", (136, 128, D), BF16)

    def agslot(j):
        return (j % 8) * 17 + j // 8

    ldS = Ctr(); ldW = Ctr(); pa = Ctr(); pb = Ctr(); mm = Ctr(); st = Ctr()
    ev = Ctr(); ccc = Ctr()

    with (
        nc.sbuf_tensor("idn", [128, 128], BF16) as idn,
        nc.sbuf_tensor("dqc", [128, 4], F32) as dqc,
        nc.sbuf_tensor("stats", [128, 8], F32) as stats,
        nc.semaphore("s_ld") as s_ld,
        nc.semaphore("s_lw") as s_lw,
        nc.semaphore("s_pa") as s_pa,
        nc.semaphore("s_pb") as s_pb,
        nc.semaphore("s_mm") as s_mm,
        nc.semaphore("s_st") as s_st,
        nc.semaphore("s_ev") as s_ev,
        nc.semaphore("s_cc") as s_cc,
    ):
        with nc.Block() as blk:
            @blk.sync
            def _(s):
                s.dma_start(idn[:, :], ident_d[:, :]).then_inc(s_ld, 16)
                s.dma_start(dqc[:, :], dqc_d[:, :]).then_inc(s_ld, 16)
                s.wait_ge(s_ld, ldS.inc(32))

        with nc.sbuf_tensor("hT", [128, FC * TPC], BF16) as hT:
            jobs = []
            for k in range(17):
                jobs.append((pksh_d[k, :, :], scsh_d[k, :, :], 16,
                             agin_w[k, :, :], D, 0))

            with contextlib.ExitStack() as es:
                _e = es.enter_context
                pk = _e(nc.sbuf_tensor("pk", [128, 2 * 256], I32, side="right"))
                sct = _e(nc.sbuf_tensor("sct", [128, 2 * 16], F16, side="right"))
                bufAi = _e(nc.sbuf_tensor("bufAi", [128, D], I32, side="right"))
                bufB = _e(nc.sbuf_tensor("bufB", [128, D], F32, side="right"))
                bufC = _e(nc.sbuf_tensor("bufC", [128, D], F32, side="right"))
                bufD = _e(nc.sbuf_tensor("bufD", [128, D], F32, side="right"))
                bufE = _e(nc.sbuf_tensor("bufE", [128, D], F32, side="right"))
                scf_t = _e(nc.sbuf_tensor("scf", [128, 32], F32, side="right"))
                deq = _e(nc.sbuf_tensor("deq", [128, 2 * D], BF16, side="right"))
                wT = _e(nc.sbuf_tensor("wT", [128, 2 * D], BF16, side="right"))
                tps0 = _e(nc.psum_tensor("tpsd0", [128, 1024], BF16))
                tps1 = _e(nc.psum_tensor("tpsd1", [128, 1024], BF16))

                with nc.Block() as blk:
                    pa0 = pa.v
                    pb0 = pb.v
                    mm0 = mm.v
                    ev0 = ev.v
                    ld0 = ldS.v
                    st0 = st.v
                    NJ = len(jobs)

                    @blk.sync
                    def _(s):
                        for jb, (pk_ap, sc_ap, ng, dst_ap, w, isdn) \
                                in enumerate(jobs):
                            ob = jb % 2
                            if jb >= 2:
                                s.wait_ge(s_pa, pa0 + 9 * (jb - 2) + 1)
                                s.wait_ge(s_pb, pb0 + jb - 1)
                            s.dma_start(pk[:, ob * 256:ob * 256 + 256],
                                        pk_ap).then_inc(s_ld, 16)
                            s.dma_start(sct[:, ob * 16:ob * 16 + 16],
                                        sc_ap).then_inc(s_ld, 16)
                            s.wait_ge(s_ev, ev0 + 16 * (jb + 1))
                            s.dma_start(dst_ap, wT[:, ob * D:ob * D + D]
                                        ).then_inc(s_st, 16)
                    ldS.inc(32 * NJ)
                    st.inc(16 * NJ)

                    @blk.gpsimd
                    def _(g):
                        for jb in range(NJ):
                            ob = jb % 2
                            so = ob * 16
                            g.wait_ge(s_ld, ld0 + 32 * (jb + 1))
                            if jb >= 2:
                                g.wait_ge(s_pa, pa0 + 9 * (jb - 2) + 9)
                            g.tensor_copy(scf_t[:, so:so + 16],
                                          sct[:, ob * 16:ob * 16 + 16]
                                          ).then_inc(s_pb, 1)
                    pb.v = pb0 + NJ

                    @blk.vector
                    def _(v):
                        v.memset(bufAi[:, :], 0)
                        for jb in range(NJ):
                            ob = jb % 2
                            so = ob * 16

                            def unpack(v, ob=ob):
                                vi = None
                                for k in range(8):
                                    vi = v.tensor_scalar(
                                        bufAi[:, k:D:8],
                                        pk[:, ob * 256:ob * 256 + 256],
                                        4 * k, 15,
                                        ALU.logical_shift_right,
                                        ALU.bitwise_and)
                                return vi

                            def scale(v, ob=ob, so=so, jb=jb):
                                v.wait_ge(s_pb, pb0 + jb + 1)
                                vi = None
                                for g2 in range(16):
                                    vi = v.scalar_tensor_tensor(
                                        deq[:, ob * D + g2 * 128:
                                            ob * D + (g2 + 1) * 128],
                                        bufB[:, g2 * 128:(g2 + 1) * 128],
                                        scf_t[:, so + g2:so + g2 + 1],
                                        bufD[:, g2 * 128:(g2 + 1) * 128],
                                        ALU.mult, ALU.mult)
                                return vi

                            pw = ([(s_mm, mm0 + 16 * (jb - 1))]
                                  if jb >= 2 else [])
                            emit_dq_vec(v, s_pa, pa0 + 9 * jb, s_ld,
                                        ld0 + 32 * (jb + 1), bufAi, bufB,
                                        bufC, bufD, bufE, D, unpack, scale,
                                        prewaits=pw)
                            if jb >= 2:
                                v.wait_ge(s_st, st0 + 16 * (jb - 1))
                            for fc in range(FC):
                                k = 16 * jb + fc
                                ps = tps0 if fc % 2 == 0 else tps1
                                v.wait_ge(s_mm, mm0 + k + 1)
                                v.tensor_copy(wT[:, ob * D + fc * 128:
                                                 ob * D + (fc + 1) * 128],
                                              ps[:, 0:128]).then_inc(s_ev, 1)
                    pa.v = pa0 + 9 * NJ
                    ev.inc(16 * NJ)

                    @blk.scalar
                    def _(s):
                        for jb in range(NJ):
                            emit_dq_sc(s, s_pa, pa0 + 9 * jb, bufAi, bufB,
                                       bufC, bufD, bufE, dqc, D)

                    @blk.tensor
                    def _(t):
                        for jb in range(NJ):
                            ob = jb % 2
                            t.wait_ge(s_pa, pa0 + 9 * jb + 9)
                            for fc in range(FC):
                                k = 16 * jb + fc
                                ps = tps0 if fc % 2 == 0 else tps1
                                if k >= 2:
                                    t.wait_ge(s_ev, ev0 + k - 1)
                                t.transpose(ps[:, 0:128],
                                            deq[:, ob * D + fc * 128:
                                                ob * D + (fc + 1) * 128],
                                            idn[:, :]).then_inc(s_mm, 1)
                    mm.inc(16 * NJ)


            with nc.Block() as blk:
                @blk.gpsimd
                def _(g):
                    g.wait_ge(s_st, st.v)
                    g.collective_compute(
                        "AllGather", ALU.bypass,
                        replica_groups=[[0, 1, 2, 3, 4, 5, 6, 7]],
                        ins=[agin_w.rearrange("k p n -> (k p) n")],
                        outs=[agout_w.rearrange("k p n -> (k p) n")],
                    ).then_inc(s_cc, 1)
                ccc.inc(1)
            if L0F_SERIAL_CC:
                with nc.Block() as blk:
                    @blk.gpsimd
                    def _(g):
                        g.wait_ge(s_cc, ccc.v)
            if L0F_DEBUG:
                dbg_agw = p.dout("dbg_agw", (136, 128, D), BF16)
                with nc.Block() as blk:
                    @blk.sync
                    def _(s):
                        s.wait_ge(s_cc, ccc.v)
                        s.dma_start(dbg_agw, agout_w).then_inc(s_st, 16)
                        s.wait_ge(s_st, st.inc(16))

            with contextlib.ExitStack() as es:
                _e = es.enter_context
                xt = _e(nc.sbuf_tensor("xt", [128, 2 * D], F32, side="right"))
                htile = _e(nc.sbuf_tensor("htile", [128, 2 * D], BF16,
                                          side="right"))
                junk = _e(nc.sbuf_tensor("junk", [128, D], BF16, side="right"))
                lstats = _e(nc.sbuf_tensor("lstats", [128, 16], F32,
                                           side="right"))
                tps0 = _e(nc.psum_tensor("tps0", [128, 1024], BF16))
                tps1 = _e(nc.psum_tensor("tps1", [128, 1024], BF16))
                emit_ln_phase(nc, idn, x1_d, hT, xt, htile, junk, lstats,
                              tps0, tps1, s_ld, s_pa, s_mm, s_pb,
                              ldS, pa, mm, pb)

            with contextlib.ExitStack() as es:
                _e = es.enter_context
                QW = 512
                mT = _e(nc.sbuf_tensor("mT", [128, DN_IC * QW], BF16,
                                       side="right"))
                wg = _e(nc.sbuf_tensor("wg", [128, 2 * D], BF16, side="right"))
                wu = _e(nc.sbuf_tensor("wu", [128, 2 * D], BF16, side="right"))
                wdh = _e(nc.sbuf_tensor("wdh", [128, 2 * 5632], BF16,
                                        side="right"))
                gs = _e(nc.sbuf_tensor("gs", [128, 2 * QW], BF16,
                                       side="right"))
                ybuf = _e(nc.sbuf_tensor("ybuf", [128, 4 * D], BF16,
                                         side="right"))
                ysb = _e(nc.sbuf_tensor("ysb", [128, 2 * QW], BF16,
                                        side="right"))
                x1r = _e(nc.sbuf_tensor("x1r", [128, D], F32, side="right"))
                x2t = _e(nc.sbuf_tensor("x2t", [128, D], F32, side="right"))
                x2b = _e(nc.sbuf_tensor("x2b", [128, D], BF16, side="right"))
                junk2 = _e(nc.sbuf_tensor("junk2", [128, D], BF16,
                                          side="right"))

                for th in range(4):
                    toff = th * QW
                    es1 = contextlib.ExitStack()
                    gps = es1.enter_context(
                        nc.psum_tensor(f"gps{th}", [128, 2 * QW], F32))
                    ups = es1.enter_context(
                        nc.psum_tensor(f"ups{th}", [128, 2 * QW], F32))
                    with nc.Block() as blk:
                        base = pa.v
                        pb0 = pb.v
                        mm0 = mm.v
                        lw0 = ldW.v

                        @blk.gpsimd
                        def _(g):
                            if th == 0:
                                g.wait_ge(s_cc, ccc.v)
                            g.dma_start(wg[:, 0:D],
                                        agout_w[agslot(0), :, :]
                                        ).then_inc(s_lw, 16)
                            g.dma_start(wu[:, 0:D],
                                        agout_w[agslot(DN_IC), :, :]
                                        ).then_inc(s_lw, 16)
                            for gi in range(DN_IC - 1):
                                sl = (gi + 1) % 2
                                if gi >= 1:
                                    g.wait_ge(s_mm, mm0 + gi)
                                g.dma_start(wg[:, sl * D:(sl + 1) * D],
                                            agout_w[agslot(gi + 1), :, :]
                                            ).then_inc(s_lw, 16)
                                g.dma_start(wu[:, sl * D:(sl + 1) * D],
                                            agout_w[agslot(DN_IC + gi + 1),
                                                    :, :]
                                            ).then_inc(s_lw, 16)
                        ldW.inc(32 * DN_IC)

                        @blk.tensor
                        def _(t):
                            for gi in range(DN_IC):
                                sl = gi % 2
                                t.wait_ge(s_lw, lw0 + 32 * (gi + 1))
                                if gi >= 2:
                                    t.wait_ge(s_pb, pb0 + gi - 1)
                                mi = None
                                for fc in range(FC):
                                    t.matmul(
                                        gps[:, sl * QW:(sl + 1) * QW],
                                        wg[:, sl * D + fc * 128:
                                           sl * D + (fc + 1) * 128],
                                        hT[:, fc * TPC + toff:
                                           fc * TPC + toff + QW],
                                        start=(fc == 0), stop=(fc == FC - 1))
                                    mi = t.matmul(
                                        ups[:, sl * QW:(sl + 1) * QW],
                                        wu[:, sl * D + fc * 128:
                                           sl * D + (fc + 1) * 128],
                                        hT[:, fc * TPC + toff:
                                           fc * TPC + toff + QW],
                                        start=(fc == 0), stop=(fc == FC - 1))
                                mi.then_inc(s_mm, 1)
                        mm.inc(DN_IC)

                        @blk.scalar
                        def _(s):
                            for gi in range(DN_IC):
                                sl = gi % 2
                                s.wait_ge(s_mm, mm0 + gi + 1)
                                s.activation(gs[:, sl * QW:(sl + 1) * QW],
                                             gps[:, sl * QW:(sl + 1) * QW],
                                             AF.Silu).then_inc(s_pa, 1)
                        pa.v = base + DN_IC

                        @blk.vector
                        def _(v):
                            for gi in range(DN_IC):
                                sl = gi % 2
                                v.wait_ge(s_pa, base + gi + 1)
                                v.tensor_tensor(mT[:, gi * QW:(gi + 1) * QW],
                                                gs[:, sl * QW:(sl + 1) * QW],
                                                ups[:, sl * QW:(sl + 1) * QW],
                                                ALU.mult).then_inc(s_pb, 1)
                        pb.v = pb0 + DN_IC

                    es1.close()
                    es2 = contextlib.ExitStack()
                    yp0 = es2.enter_context(
                        nc.psum_tensor(f"yp0{th}", [128, QW], F32))
                    yp1 = es2.enter_context(
                        nc.psum_tensor(f"yp1{th}", [128, QW], F32))
                    yt0 = es2.enter_context(
                        nc.psum_tensor(f"yt0{th}", [128, 1024], BF16))
                    yt1 = es2.enter_context(
                        nc.psum_tensor(f"yt1{th}", [128, 1024], BF16))
                    with nc.Block() as blk:
                        pa0 = pa.v
                        pb0 = pb.v
                        mm0 = mm.v
                        lw0 = ldW.v
                        ev0 = ev.v

                        @blk.gpsimd
                        def _(g):
                            def wdh_load(hot, hb):
                                mi = None
                                for sl3 in range(3):
                                    w3 = 2048 if sl3 < 2 else 1536
                                    mi = g.dma_start(
                                        wdh[:, hb * 5632 + sl3 * 2048:
                                            hb * 5632 + sl3 * 2048 + w3],
                                        agout_w[agslot(88 + hot * 3 + sl3),
                                                :, 0:w3])
                                    mi.then_inc(s_lw, 16)
                            wdh_load(0, 0)
                            for hot in range(15):
                                sl = (hot + 1) % 2
                                if hot >= 1:
                                    g.wait_ge(s_mm, mm0 + hot)
                                wdh_load(hot + 1, sl)
                        ldW.inc(48 * 16)

                        @blk.tensor
                        def _(t):
                            for hot in range(16):
                                hb = hot % 2
                                yp = yp0 if hb == 0 else yp1
                                t.wait_ge(s_lw, lw0 + 48 * (hot + 1))
                                if hot >= 2:
                                    t.wait_ge(s_pa, pa0 + hot - 1)
                                mi = None
                                for ic in range(DN_IC):
                                    sl = ic // 16
                                    fcs = ic % 16
                                    stat = wdh[:, hb * 5632 + sl * 2048 +
                                               fcs * 128:
                                               hb * 5632 + sl * 2048 +
                                               (fcs + 1) * 128]
                                    mi = t.matmul(
                                        yp[:, :], stat,
                                        mT[:, ic * QW:(ic + 1) * QW],
                                        start=(ic == 0),
                                        stop=(ic == DN_IC - 1))
                                mi.then_inc(s_mm, 1)
                                t.wait_ge(s_pa, pa0 + hot + 1)
                                for u in range(4):
                                    ps = yt0 if u % 2 == 0 else yt1
                                    k = hot * 4 + u
                                    if k >= 2:
                                        t.wait_ge(s_pb, pb0 + k - 1)
                                    t.transpose(ps[:, 0:128],
                                                ysb[:, hb * QW + u * 128:
                                                    hb * QW + (u + 1) * 128],
                                                idn[:, :]).then_inc(s_ev, 1)
                        mm.inc(16)
                        ev.inc(64)

                        @blk.vector
                        def _(v):
                            for hot in range(16):
                                hb = hot % 2
                                yp = yp0 if hb == 0 else yp1
                                v.wait_ge(s_mm, mm0 + hot + 1)
                                if hot >= 2:
                                    v.wait_ge(s_ev, ev0 + 4 * hot - 4)
                                v.tensor_copy(ysb[:, hb * QW:(hb + 1) * QW],
                                              yp[:, :]).then_inc(s_pa, 1)
                                for u in range(4):
                                    ps = yt0 if u % 2 == 0 else yt1
                                    v.wait_ge(s_ev, ev0 + 4 * hot + u + 1)
                                    v.tensor_copy(
                                        ybuf[:, u * D + hot * 128:
                                             u * D + (hot + 1) * 128],
                                        ps[:, 0:128]).then_inc(s_pb, 1)
                        pa.v = pa0 + 16
                        pb.v = pb0 + 64

                    for u in range(4):
                        with nc.Block() as blk:
                            base = pa.v
                            ev0 = ev.v
                            st0 = st.v
                            ld0 = ldS.v

                            @blk.sync
                            def _(s):
                                s.dma_start(
                                    x1r[:, :],
                                    x1_d[toff + u * 128:toff + (u + 1) * 128, :]
                                    ).then_inc(s_ld, 16)
                                s.wait_ge(s_ev, ev0 + 2)
                                s.dma_start(
                                    x2_d[toff + u * 128:toff + (u + 1) * 128, :],
                                    x2t[:, :]).then_inc(s_st, 16)
                                if emit_x2bf:
                                    s.wait_ge(s_ev, ev0 + 3)
                                    s.dma_start(
                                        x2bf_d[toff + u * 128:
                                               toff + (u + 1) * 128, :],
                                        x2b[:, :]).then_inc(s_st, 16)
                                s.wait_ge(s_st, st0 + (32 if emit_x2bf
                                                       else 16))
                            ldS.inc(16)
                            st.inc(32 if emit_x2bf else 16)

                            @blk.scalar
                            def _(s):
                                s.activation(junk2[:, :],
                                             ybuf[:, u * D:(u + 1) * D],
                                             AF.Square,
                                             accum_out=stats[:, 0:1]
                                             ).then_inc(s_pa, 1)
                                s.wait_ge(s_pa, base + 2)
                                s.activation(stats[:, 2:3], stats[:, 1:2],
                                             AF.Sqrt).then_inc(s_pa, 1)

                            @blk.vector
                            def _(v):
                                v.wait_ge(s_ld, ld0 + 16)
                                v.wait_ge(s_pa, base + 1)
                                v.tensor_scalar(stats[:, 1:2], stats[:, 0:1],
                                                1.0 / D, 1e-6, ALU.mult,
                                                ALU.add).then_inc(s_pa, 1)
                                v.wait_ge(s_pa, base + 3)
                                v.reciprocal(stats[:, 3:4], stats[:, 2:3]
                                             ).then_inc(s_ev, 1)
                                v.wait_ge(s_ev, ev0 + 1)
                                v.tensor_scalar(x2t[:, :],
                                                ybuf[:, u * D:(u + 1) * D],
                                                stats[:, 3:4], None, ALU.mult)
                                v.tensor_tensor(x2t[:, :], x2t[:, :],
                                                x1r[:, :],
                                                ALU.add).then_inc(s_ev, 1)
                                if emit_x2bf:
                                    v.wait_ge(s_ev, ev0 + 2)
                                    v.tensor_copy(x2b[:, :], x2t[:, :]
                                                  ).then_inc(s_ev, 1)
                            pa.v = base + 3
                            ev.inc(3 if emit_x2bf else 2)
                    es2.close()

        with nc.Block() as blk:
            @blk.sync
            def _(s):
                s.wait_ge(s_ld, ldS.v)
                s.wait_ge(s_st, st.v)

            @blk.gpsimd
            def _(g):
                g.wait_ge(s_lw, ldW.v)
    return p


def l0f_inmaps(x1_full, gup, gus, dpk, dsc):
    gusT = np.ascontiguousarray(gus.T)
    dsT = np.ascontiguousarray(dsc.T)
    dqc = np.zeros((128, 4), dtype=np.float32)
    dqc[:, 0] = -4.0
    dqc[:, 1] = -6.0
    dqc[:, 2] = 3.5
    ident = np.eye(128, dtype=BF)
    maps = []
    for c in range(8):
        pk_sh = np.zeros((17, 128, 256), np.int32)
        sc_sh = np.zeros((17, 128, 16), np.float16)
        for k in range(17):
            j = c + 8 * k
            if j < GU_OT:
                pk_sh[k] = gup[j * 128:(j + 1) * 128, :]
                sc_sh[k] = gusT[j * 128:(j + 1) * 128, :]
            else:
                jd = j - GU_OT
                hot, sl = jd // 3, jd % 3
                w = 2048 if sl < 2 else 1536
                pk_sh[k, :, 0:w // 8] = dpk[hot * 128:(hot + 1) * 128,
                                            sl * 256:sl * 256 + w // 8]
                sc_sh[k, :, 0:w // 128] = dsT[hot * 128:(hot + 1) * 128,
                                              sl * 16:sl * 16 + w // 128]
        maps.append({
            "x1": np.ascontiguousarray(
                x1_full[own_positions(c) + (c // 2) * T]),
            "gshard_pk": pk_sh, "gshard_sc": sc_sh,
            "dqconsts": dqc, "ident": ident,
        })
    return maps



def build_fused():
    p = Prog()
    x1h = p.dtmp("x1h", (TPC, D), F32)
    x2h = p.dtmp("x2h", (TPC, D), F32)
    x2bfh = p.dtmp("x2bfh", (TPC, D), BF16)
    x3h = p.dtmp("x3h", (TPC, D), F32)
    p.override = {"x1": x1h, "x2": x2h, "x2bf": x2bfh, "x3": x3h}
    build_l0a(p, "A_")
    build_l0f(p, "B_")
    build_l1a(p, "C_")
    build_l1f(p, "D_")
    return p


def fused_inmaps(inputs):
    x = np.asarray(inputs["L0_attn_x"], dtype=np.float32).reshape(B * T, D)
    zeros = np.zeros((B * T, D), np.float32)
    stage_maps = [
        l0a_inmaps(x, f32a(inputs["L0_attn_Wq"]),
                   f32a(inputs["L0_attn_Wk"]),
                   f32a(inputs["L0_attn_Wv"]),
                   f32a(inputs["L0_attn_Wo"])),
        l0f_inmaps(zeros, inputs["L0_ffn_gate_up_packed"],
                   inputs["L0_ffn_gate_up_scales"],
                   inputs["L0_ffn_down_packed"],
                   inputs["L0_ffn_down_scales"]),
        l1a_inmaps(zeros, inputs),
        l1f_inmaps(zeros, f32a(inputs["L1_ffn_gate_w"]),
                   f32a(inputs["L1_ffn_gate_b"]),
                   f32a(inputs["L1_ffn_w1_shared"]),
                   f32a(inputs["L1_ffn_w2_shared"]),
                   f32a(inputs["L1_ffn_w3_shared"]),
                   f32a(inputs["L1_ffn_w1_routed"]),
                   f32a(inputs["L1_ffn_w2_routed"]),
                   f32a(inputs["L1_ffn_w3_routed"])),
    ]
    maps = []
    for c in range(8):
        d = {}
        for m in stage_maps:
            for k, v in m[c].items():
                if k in ("x1", "x2", "x3"):
                    continue
                if k not in d:
                    d[k] = v
        maps.append(d)
    return maps


def kernel(**inputs):
    p = build_fused()
    maps = fused_inmaps(inputs)
    res = run_bass_kernel_spmd(p.nc, maps, core_ids=list(range(8)))
    x4 = scatter_tokens(res.results, "x4")
    return x4.reshape(B, T, D)



GU_OT = 88
DN_IC = 44



def emit_dq_vec(v, sem, base, s_ld, ld_target, bufAi, bufB, bufC, bufD,
                bufE, width, unpack_fn, scale_fn, prewaits=()):
    v.wait_ge(s_ld, ld_target)
    for ps_, pv_ in prewaits:
        v.wait_ge(ps_, pv_)
    unpack_fn(v).then_inc(sem, 1)
    v.wait_ge(sem, base + 3)
    v.scalar_tensor_tensor(bufC[:, 0:width], bufD[:, 0:width], 4.0,
                           bufB[:, 0:width], ALU.mult,
                           ALU.add).then_inc(sem, 1)
    v.wait_ge(sem, base + 6)
    v.scalar_tensor_tensor(bufB[:, 0:width], bufC[:, 0:width], 0.5,
                           bufB[:, 0:width], ALU.mult,
                           ALU.add).then_inc(sem, 1)
    v.wait_ge(sem, base + 7)
    v.scalar_tensor_tensor(bufB[:, 0:width], bufE[:, 0:width], 0.5,
                           bufB[:, 0:width], ALU.mult,
                           ALU.add).then_inc(sem, 1)
    v.wait_ge(sem, base + 8)
    scale_fn(v).then_inc(sem, 1)


def emit_dq_sc(s, sem, base, bufAi, bufB, bufC, bufD, bufE, dqc, width):
    s.wait_ge(sem, base + 1)
    s.activation(bufB[:, 0:width], bufAi[:, 0:width], AF.Identity,
                 bias=dqc[:, 0:1]).then_inc(sem, 1)
    s.activation(bufD[:, 0:width], bufB[:, 0:width], AF.Sign,
                 scale=-1.0, bias=dqc[:, 2:3]).then_inc(sem, 1)
    s.wait_ge(sem, base + 4)
    s.activation(bufE[:, 0:width], bufC[:, 0:width], AF.Relu,
                 bias=dqc[:, 0:1]).then_inc(sem, 1)
    s.activation(bufB[:, 0:width], bufC[:, 0:width], AF.Relu,
                 bias=dqc[:, 1:2]).then_inc(sem, 1)


def emit_dq(blk, sem, c, s_ld, s_st, st_war, bufAi, bufB, bufC, bufD, bufE,
            dqc, out_ap, width, unpack_fn, scale_fn):
    base = c.v

    @blk.vector
    def _(v):
        v.wait_ge(s_ld, emit_dq.ld_target)
        unpack_fn(v).then_inc(sem, 1)
        v.wait_ge(sem, base + 3)
        v.scalar_tensor_tensor(bufC[:, 0:width], bufD[:, 0:width], 4.0,
                               bufB[:, 0:width], ALU.mult,
                               ALU.add).then_inc(sem, 1)
        v.wait_ge(sem, base + 6)
        v.scalar_tensor_tensor(bufB[:, 0:width], bufC[:, 0:width], 0.5,
                               bufB[:, 0:width], ALU.mult,
                               ALU.add).then_inc(sem, 1)
        v.wait_ge(sem, base + 7)
        v.scalar_tensor_tensor(bufB[:, 0:width], bufE[:, 0:width], 0.5,
                               bufB[:, 0:width], ALU.mult,
                               ALU.add).then_inc(sem, 1)
        v.wait_ge(sem, base + 8)
        if st_war is not None:
            v.wait_ge(s_st, st_war)
        scale_fn(v).then_inc(sem, 1)

    @blk.scalar
    def _(s):
        s.wait_ge(sem, base + 1)
        s.activation(bufB[:, 0:width], bufAi[:, 0:width], AF.Identity,
                     bias=dqc[:, 0:1]).then_inc(sem, 1)
        s.activation(bufD[:, 0:width], bufB[:, 0:width], AF.Sign,
                     scale=-1.0, bias=dqc[:, 2:3]).then_inc(sem, 1)
        s.wait_ge(sem, base + 4)
        s.activation(bufE[:, 0:width], bufC[:, 0:width], AF.Relu,
                     bias=dqc[:, 0:1]).then_inc(sem, 1)
        s.activation(bufB[:, 0:width], bufC[:, 0:width], AF.Relu,
                     bias=dqc[:, 1:2]).then_inc(sem, 1)
    c.v = base + 9




def build_l1f(p=None, pfx=""):
    if p is None:
        p = Prog()
    p._pfx = pfx
    nc = _NCPfx(p.nc, pfx)
    x3_d = p.din("x3", (TPC, D), F32)
    gwT_d = p.din("gwT_in", (128, FC * 16), F32)
    gb_d = p.din("gb_rep", (128, 16), F32)
    w1_d = p.din("w1_pre", (17, 11, 128, D), F32)
    w3_d = p.din("w3_pre", (17, 11, 128, D), F32)
    w2_d = p.din("w2_cat", (17, 1408, D), F32)
    ident_d = p.din("ident", (128, 128), BF16)
    x4_d = p.dout("x4", (TPC, D), F32)

    CI = 11
    ldS = Ctr(); ldW = Ctr(); pa = Ctr(); pb = Ctr(); mm = Ctr(); st = Ctr()
    ev = Ctr(); ac = Ctr()

    with (
        nc.sbuf_tensor("idn", [128, 128], BF16) as idn,
        nc.sbuf_tensor("gwT", [128, FC * 16], BF16) as gwT,
        nc.sbuf_tensor("gb", [128, 16], F32) as gb,
        nc.sbuf_tensor("stats", [128, 8], F32) as stats,
        nc.sbuf_tensor("t8", [128, 8], F32) as t8,
        nc.sbuf_tensor("t16", [128, 16], F32) as t16,
        nc.sbuf_tensor("rp", [128, 16 * 16], F32) as rp,
        nc.semaphore("s_ld") as s_ld,
        nc.semaphore("s_lw") as s_lw,
        nc.semaphore("s_pa") as s_pa,
        nc.semaphore("s_pb") as s_pb,
        nc.semaphore("s_mm") as s_mm,
        nc.semaphore("s_st") as s_st,
        nc.semaphore("s_ac") as s_ac,
    ):
        with nc.Block() as blk:
            @blk.sync
            def _(s):
                s.dma_start(idn[:, :], ident_d[:, :]).then_inc(s_ld, 16)
                s.dma_start(gb[:, :], gb_d[:, :]).then_inc(s_ld, 16)
                s.wait_ge(s_ld, ldS.inc(32))

            @blk.gpsimd
            def _(g):
                g.dma_start(gwT[:, :], gwT_d[:, :]).then_inc(s_lw, 16)
                g.dma_start(x4_d[:, :], x3_d[:, :]).then_inc(s_ac, 16)
                g.wait_ge(s_lw, ldW.inc(16))
                g.wait_ge(s_ac, ac.inc(16))

        with nc.sbuf_tensor("hT", [128, FC * TPC], BF16) as hT:
            with contextlib.ExitStack() as es:
                _e = es.enter_context
                xt = _e(nc.sbuf_tensor("xt", [128, 2 * D], F32, side="right"))
                htile = _e(nc.sbuf_tensor("htile", [128, 2 * D], BF16,
                                          side="right"))
                junk = _e(nc.sbuf_tensor("junk", [128, D], BF16, side="right"))
                lstats = _e(nc.sbuf_tensor("lstats", [128, 16], F32,
                                           side="right"))
                tps0 = _e(nc.psum_tensor("tps0", [128, 1024], BF16))
                tps1 = _e(nc.psum_tensor("tps1", [128, 1024], BF16))
                rps = _e(nc.psum_tensor("rps", [128, 512], F32))
                emit_ln_phase(nc, idn, x3_d, hT, xt, htile, junk, lstats,
                              tps0, tps1, s_ld, s_pa, s_mm, s_pb,
                              ldS, pa, mm, pb)

                for tb in range(16):
                    with nc.Block() as blk:
                        base = pa.v

                        @blk.tensor
                        def _(t):
                            mi = None
                            for fc in range(FC):
                                mi = t.matmul(rps[:, 0:16],
                                              hT[:, fc * TPC + tb * 128:
                                                 fc * TPC + (tb + 1) * 128],
                                              gwT[:, fc * 16:(fc + 1) * 16],
                                              start=(fc == 0),
                                              stop=(fc == FC - 1))
                            mi.then_inc(s_mm, 1)
                        mm.inc(1)

                        @blk.scalar
                        def _(s):
                            s.wait_ge(s_mm, mm.v)
                            s.activation(rp[:, tb * 16:(tb + 1) * 16],
                                         rps[:, 0:16],
                                         AF.Sigmoid).then_inc(s_pa, 1)

                        @blk.vector
                        def _(v):
                            R = rp[:, tb * 16:(tb + 1) * 16]
                            v.wait_ge(s_pa, base + 1)
                            v.tensor_tensor(R, R, gb[:, :],
                                            ALU.add).then_inc(s_pa, 1)
                            v.wait_ge(s_pa, base + 2)
                            v.max(out=t8[:, 0:8], in_=R).then_inc(s_pa, 1)
                            v.wait_ge(s_pa, base + 3)
                            v.memset(t8[:, 4:8], -1e30).then_inc(s_pa, 1)
                            v.wait_ge(s_pa, base + 4)
                            v.match_replace(out=t16[:, 0:16],
                                            in_to_replace=t8[:, 0:8],
                                            in_values=R,
                                            imm_value=-1e30).then_inc(s_pa, 1)
                            v.wait_ge(s_pa, base + 5)
                            v.tensor_scalar(t16[:, 0:16], t16[:, 0:16],
                                            -1e30, None,
                                            ALU.is_equal).then_inc(s_pa, 1)
                            v.wait_ge(s_pa, base + 6)
                            v.tensor_tensor(R, R, t16[:, 0:16],
                                            ALU.mult).then_inc(s_pa, 1)
                            v.wait_ge(s_pa, base + 7)
                            v.reduce_sum(stats[:, 0:1], R,
                                         axis=AX.X).then_inc(s_pa, 1)
                            v.wait_ge(s_pa, base + 8)
                            v.reciprocal(stats[:, 1:2], stats[:, 0:1]
                                         ).then_inc(s_pa, 1)
                            v.wait_ge(s_pa, base + 9)
                            v.tensor_scalar(R, R, stats[:, 1:2], None,
                                            ALU.mult).then_inc(s_pa, 1)
                        pa.v = base + 10

            with contextlib.ExitStack() as es:
                _e = es.enter_context
                mT = _e(nc.sbuf_tensor("mT", [128, CI * TPC], BF16,
                                       side="right"))
                w1c = _e(nc.sbuf_tensor("w1c", [128, 2 * D], BF16, side="right"))
                w3c = _e(nc.sbuf_tensor("w3c", [128, 2 * D], BF16, side="right"))
                w2sb = _e(nc.sbuf_tensor("w2sb", [128, CI * D], BF16,
                                         side="right"))
                gsb = _e(nc.sbuf_tensor("gsb", [128, 2048], BF16, side="right"))
                gsb2 = _e(nc.sbuf_tensor("gsb2", [128, 2048], BF16, side="right"))
                tmp0 = _e(nc.sbuf_tensor("tmp0", [128, D], F32, side="right"))
                tmp1 = _e(nc.sbuf_tensor("tmp1", [128, D], F32, side="right"))

                with nc.Block() as blk:
                    @blk.gpsimd
                    def _(g):
                        g.dma_start(w1c[:, 0:D], w1_d[0, 0, :, :]
                                    ).then_inc(s_lw, 16)
                        g.dma_start(w3c[:, 0:D], w3_d[0, 0, :, :]
                                    ).then_inc(s_lw, 16)
                    ldW.inc(32)

                for e in range(17):
                    esm = contextlib.ExitStack()
                    g1p = esm.enter_context(
                        nc.psum_tensor(f"g1p{e}", [128, 2048], F32))
                    g3p = esm.enter_context(
                        nc.psum_tensor(f"g3p{e}", [128, 2048], F32))
                    with nc.Block() as blk:
                        base = pa.v
                        pb0 = pb.v
                        mm0 = mm.v
                        lw0 = ldW.v

                        @blk.gpsimd
                        def _(g):
                            for cc in range(1, CI):
                                ob = cc % 2
                                if cc >= 2:
                                    g.wait_ge(s_mm, mm0 + 2 * cc - 2)
                                g.dma_start(w1c[:, ob * D:(ob + 1) * D],
                                            w1_d[e, cc, :, :]
                                            ).then_inc(s_lw, 16)
                                g.dma_start(w3c[:, ob * D:(ob + 1) * D],
                                            w3_d[e, cc, :, :]
                                            ).then_inc(s_lw, 16)
                            for c2 in range(CI):
                                g.dma_start(
                                    w2sb[:, c2 * D:(c2 + 1) * D],
                                    w2_d[e, c2 * 128:(c2 + 1) * 128, :]
                                    ).then_inc(s_lw, 16)
                        ldW.inc(32 * (CI - 1) + 16 * CI)

                        @blk.tensor
                        def _(t):
                            for j in range(2 * CI):
                                cc = j // 2
                                qh = j % 2
                                ob = cc % 2
                                if qh == 0:
                                    t.wait_ge(s_lw, lw0 + 32 * cc)
                                if j >= 2:
                                    t.wait_ge(s_pb, pb0 + j - 1)
                                mi = None
                                for fc in range(FC):
                                    for q2 in range(2):
                                        co = qh * 1024 + q2 * 512
                                        t.matmul(
                                            g1p[:, co:co + 512],
                                            w1c[:, ob * D + fc * 128:
                                                ob * D + (fc + 1) * 128],
                                            hT[:, fc * TPC + co:
                                               fc * TPC + co + 512],
                                            start=(fc == 0),
                                            stop=(fc == FC - 1))
                                        mi = t.matmul(
                                            g3p[:, co:co + 512],
                                            w3c[:, ob * D + fc * 128:
                                                ob * D + (fc + 1) * 128],
                                            hT[:, fc * TPC + co:
                                               fc * TPC + co + 512],
                                            start=(fc == 0),
                                            stop=(fc == FC - 1))
                                mi.then_inc(s_mm, 1)
                        mm.inc(2 * CI)

                        if e == 0:
                            @blk.scalar
                            def _(s):
                                for j in range(2 * CI):
                                    qh = j % 2
                                    s.wait_ge(s_mm, mm0 + j + 1)
                                    s.activation(
                                        gsb[:, qh * 1024:(qh + 1) * 1024],
                                        g1p[:, qh * 1024:(qh + 1) * 1024],
                                        AF.Silu).then_inc(s_pa, 1)

                            @blk.vector
                            def _(v):
                                for j in range(2 * CI):
                                    cc = j // 2
                                    qh = j % 2
                                    v.wait_ge(s_pa, base + j + 1)
                                    v.tensor_tensor(
                                        mT[:, cc * TPC + qh * 1024:
                                           cc * TPC + (qh + 1) * 1024],
                                        gsb[:, qh * 1024:(qh + 1) * 1024],
                                        g3p[:, qh * 1024:(qh + 1) * 1024],
                                        ALU.mult).then_inc(s_pb, 1)
                            pa.v = base + 2 * CI
                            pb.v = pb0 + 2 * CI
                        else:
                            @blk.scalar
                            def _(s):
                                for j in range(2 * CI):
                                    cc = j // 2
                                    qh = j % 2
                                    s.wait_ge(s_mm, mm0 + j + 1)
                                    s.activation(
                                        gsb[:, qh * 1024:(qh + 1) * 1024],
                                        g1p[:, qh * 1024:(qh + 1) * 1024],
                                        AF.Identity).then_inc(s_pa, 1)
                                    s.wait_ge(s_pa, base + 2 * j + 2)
                                    s.activation(
                                        mT[:, cc * TPC + qh * 1024:
                                           cc * TPC + (qh + 1) * 1024],
                                        gsb2[:, qh * 1024:(qh + 1) * 1024],
                                        AF.Silu).then_inc(s_pb, 1)

                            @blk.vector
                            def _(v):
                                for j in range(2 * CI):
                                    qh = j % 2
                                    v.wait_ge(s_pa, base + 2 * j + 1)
                                    v.tensor_tensor(
                                        gsb2[:, qh * 1024:(qh + 1) * 1024],
                                        gsb[:, qh * 1024:(qh + 1) * 1024],
                                        g3p[:, qh * 1024:(qh + 1) * 1024],
                                        ALU.mult).then_inc(s_pa, 1)
                            pa.v = base + 4 * CI
                            pb.v = pb0 + 2 * CI

                    esm.close()
                    eso = contextlib.ExitStack()
                    op0 = eso.enter_context(
                        nc.psum_tensor(f"op0{e}", [128, 2048], F32))
                    op1 = eso.enter_context(
                        nc.psum_tensor(f"op1{e}", [128, 2048], F32))
                    with nc.Block() as blk:
                        base = pa.v
                        ac0 = ac.v
                        lw_w2 = ldW.v

                        @blk.tensor
                        def _(t):
                            t.wait_ge(s_lw, lw_w2)
                            for tb in range(16):
                                ps = op0 if tb % 2 == 0 else op1
                                if tb >= 2:
                                    t.wait_ge(s_pa, base + tb - 1)
                                mi = None
                                for cc in range(CI):
                                    for q in range(4):
                                        mi = t.matmul(
                                            ps[:, q * 512:(q + 1) * 512],
                                            mT[:, cc * TPC + tb * 128:
                                               cc * TPC + (tb + 1) * 128],
                                            w2sb[:, cc * D + q * 512:
                                                 cc * D + (q + 1) * 512],
                                            start=(cc == 0), stop=(cc == CI - 1))
                                mi.then_inc(s_mm, 1)
                        mm.inc(16)

                        @blk.vector
                        def _(v):
                            for tb in range(16):
                                ps = op0 if tb % 2 == 0 else op1
                                tm = tmp0 if tb % 2 == 0 else tmp1
                                v.wait_ge(s_mm, mm.v - 16 + tb + 1)
                                if tb >= 2:
                                    v.wait_ge(s_ac, ac0 + 16 * (tb - 1))
                                if e == 0:
                                    v.tensor_copy(tm[:, :], ps[:, :]
                                                  ).then_inc(s_pa, 1)
                                else:
                                    v.tensor_scalar(
                                        tm[:, :], ps[:, :],
                                        rp[:, tb * 16 + e - 1:tb * 16 + e],
                                        None, ALU.mult).then_inc(s_pa, 1)

                        @blk.gpsimd
                        def _(g):
                            if e + 1 < 17:
                                g.dma_start(w1c[:, 0:D], w1_d[e + 1, 0, :, :]
                                            ).then_inc(s_lw, 16)
                                g.dma_start(w3c[:, 0:D], w3_d[e + 1, 0, :, :]
                                            ).then_inc(s_lw, 16)
                            for tb in range(16):
                                tm = tmp0 if tb % 2 == 0 else tmp1
                                g.wait_ge(s_pa, base + tb + 1)
                                g.dma_start(
                                    x4_d[tb * 128:(tb + 1) * 128, :],
                                    tm[:, :],
                                    accum_op=ALU.add).then_inc(s_ac, 16)
                        pa.v = base + 16
                        ac.inc(16 * 16)
                        if e + 1 < 17:
                            ldW.inc(32)
                    eso.close()

        with nc.Block() as blk:
            @blk.sync
            def _(s):
                s.wait_ge(s_ld, ldS.v)

            @blk.gpsimd
            def _(g):
                g.wait_ge(s_lw, ldW.v)
                g.wait_ge(s_ac, ac.v)
    return p


def l1f_inmaps(x3_full, gate_w, gate_b, w1s, w2s, w3s, w1r, w2r, w3r):
    gwT = f32a(pre_fmaj(np.ascontiguousarray(gate_w.T)))
    gb_rep = np.ascontiguousarray(
        np.broadcast_to(gate_b[None, :], (128, 16)).astype(np.float32))
    w1_pre = np.zeros((17, 11, 128, D), dtype=np.float32)
    w3_pre = np.zeros((17, 11, 128, D), dtype=np.float32)
    w2_cat = np.zeros((17, 1408, D), dtype=np.float32)
    w1_pre[0] = pre_headmaj(f32a(w1s), hd=128).transpose(0, 1, 2)
    w3_pre[0] = pre_headmaj(f32a(w3s), hd=128)
    w2_cat[0] = w2s
    for e in range(16):
        w1_pre[e + 1] = pre_headmaj(f32a(w1r[e]), hd=128)
        w3_pre[e + 1] = pre_headmaj(f32a(w3r[e]), hd=128)
        w2_cat[e + 1] = w2r[e]
    ident = np.eye(128, dtype=BF)
    maps = []
    for c in range(8):
        maps.append({
            "x3": np.ascontiguousarray(
                x3_full[own_positions(c) + (c // 2) * T]),
            "gwT_in": gwT, "gb_rep": gb_rep, "w1_pre": w1_pre,
            "w3_pre": w3_pre, "w2_cat": w2_cat, "ident": ident,
        })
    return maps



def emit_rmsnorm_row(blk, sem, c, x_ap, out_ap, stats, scol, junk, width,
                     eps=1e-6, mul_ap=None):
    base = c.v

    @blk.scalar
    def _(s):
        s.activation(junk, x_ap, AF.Square,
                     accum_out=stats[:, scol:scol + 1]).then_inc(sem, 1)
        s.wait_ge(sem, base + 2)
        s.activation(stats[:, scol + 2:scol + 3],
                     stats[:, scol + 1:scol + 2], AF.Sqrt).then_inc(sem, 1)

    @blk.vector
    def _(v):
        v.wait_ge(sem, base + 1)
        v.tensor_scalar(stats[:, scol + 1:scol + 2],
                        stats[:, scol:scol + 1], 1.0 / width, eps,
                        ALU.mult, ALU.add).then_inc(sem, 1)
        v.wait_ge(sem, base + 3)
        v.reciprocal(stats[:, scol + 3:scol + 4],
                     stats[:, scol + 2:scol + 3]).then_inc(sem, 1)
        v.wait_ge(sem, base + 4)
        if mul_ap is not None:
            v.tensor_scalar(out_ap, x_ap, stats[:, scol + 3:scol + 4],
                            None, ALU.mult).then_inc(sem, 1)
            v.wait_ge(sem, base + 5)
            v.tensor_tensor(out_ap, out_ap, mul_ap,
                            ALU.mult).then_inc(sem, 1)
            c.v = base + 6
        else:
            v.tensor_scalar(out_ap, x_ap, stats[:, scol + 3:scol + 4],
                            None, ALU.mult).then_inc(sem, 1)
            c.v = base + 5



def build_l1a(p=None, pfx=""):
    if p is None:
        p = Prog()
    p._pfx = pfx
    nc = _NCPfx(p.nc, pfx)
    x2_d = p.din("x2", (TPC, D), F32)
    wqc_d = p.din("wqc_pre", (128, FC * 512), F32)
    wqup_d = p.din("wqup_pre", (128, 4 * D), F32)
    kvw_d = p.din("kvcw_pre", (128, FC * 128), F32)
    kvz_d = p.din("kvcz_pre", (128, FC * 128), F32)
    kpw_d = p.din("kproj_pre", (128, FC * 128), F32)
    vpw_d = p.din("vproj_pre", (128, FC * 128), F32)
    icw_d = p.din("idxcw_pre", (128, FC * 64), F32)
    icz_d = p.din("idxcz_pre", (128, FC * 64), F32)
    wdq_d = p.din("wdq_pre", (128, FC * 64), F32)
    wiuq_d = p.din("wiuq", (64, 256), F32)
    ww_d = p.din("ww_pre", (128, FC * 4), F32)
    gw_d = p.din("gw_pre", (4, 128, 4 * 512), F32)
    fw_d = p.din("final_w", (D, D), F32)
    kvb_d = p.din("kvb_rep", (128, 128), F32)
    icb_d = p.din("icb_rep", (128, 64), F32)
    knw_d = p.din("knw_rep", (128, 128), F32)
    qnw_d = p.din("qnw_col", (128, 1), F32)
    qcs_d = p.din("l1qcos", (32, TPC), BF16)
    qsn_d = p.din("l1qsin", (32, TPC), BF16)
    scs_d = p.din("swcos", (TPC, 32), F32)
    ssn_d = p.din("swsin", (TPC, 32), F32)
    cmA_d = p.din("cmaskA", (TPC, 256), F32)
    cm1_d = p.din("cmask01", (TPC, 256), BF16)
    sw0_d = p.din("swmask0", (128, 2048), BF16)
    sw1_d = p.din("swmask1", (128, 2048), BF16)
    g8_d = p.din("g8_in", (128, 8), BF16)
    ident_d = p.din("ident", (128, 128), BF16)
    ones_d = p.din("ones_bf", (128, 128), BF16)
    x3_d = p.dout("x3", (TPC, D), F32)
    agin = p.dtmp("agin", (4352, 128), BF16)
    agout = p.dtmp("agout", (8704, 128), BF16)
    qT_dram = (p.dout if L1A_DEBUG else p.dtmp)("qT_l1", (16, 128, TPC), BF16)
    qT2_dram = p.dtmp("qT2_l1", (16, 128, 16, 128), BF16)
    if L1A_DEBUG:
        dbg_isc_d = p.dout("dbg_isc", (TPC, 256), F32)
        dbg_selm_d = p.dout("dbg_selm", (TPC, 256), BF16)
        dbg_ckv_d = p.dout("dbg_ckv", (128, 256), BF16)
        dbg_kc_d = p.dout("dbg_kc", (64, 256), BF16)
        dbg_wih_d = p.dout("dbg_wih", (128, 64), F32)
        dbg_dqT_d = p.dout("dbg_dqT", (64, TPC), BF16)
        dbg_swk_d = p.dout("dbg_swk", (128, 34 * 128), BF16)
        dbg_swv_d = p.dout("dbg_swv", (128, 34 * 128), BF16)

    ldS = Ctr(); ldW = Ctr(); pa = Ctr(); pb = Ctr(); mm = Ctr(); st = Ctr()
    ev = Ctr(); ac = Ctr(); ccc = Ctr(); dbc = Ctr()

    with contextlib.ExitStack() as TOP:
        _t = TOP.enter_context
        idn = _t(nc.sbuf_tensor("idn", [128, 128], BF16))
        ones = _t(nc.sbuf_tensor("ones", [128, 128], BF16))
        stats = _t(nc.sbuf_tensor("stats", [128, 12], F32))
        knw = _t(nc.sbuf_tensor("knw", [128, 128], F32))
        qnw = _t(nc.sbuf_tensor("qnw", [128, 1], F32))
        kvb = _t(nc.sbuf_tensor("kvb", [128, 128], F32))
        icb = _t(nc.sbuf_tensor("icb", [128, 64], F32))
        g8 = _t(nc.sbuf_tensor("g8", [128, 8], BF16))
        sw0 = _t(nc.sbuf_tensor("sw0", [128, 2048], BF16))
        sw1 = _t(nc.sbuf_tensor("sw1", [128, 2048], BF16))
        swkT = _t(nc.sbuf_tensor("swkT", [128, 34 * 128], BF16))
        swv = _t(nc.sbuf_tensor("swv", [128, 34 * 128], BF16))
        ckvn = _t(nc.sbuf_tensor("ckvn", [128, 256], BF16))
        ckvnT = _t(nc.sbuf_tensor("ckvnT", [128, 256], BF16))
        kcompT = _t(nc.sbuf_tensor("kcompT", [64, 256], BF16))
        selmT = _t(nc.sbuf_tensor("selmT", [128, 2 * TPC], BF16))
        qcT = _t(nc.sbuf_tensor("qcT", [128, 4 * TPC], BF16))
        dqT = _t(nc.sbuf_tensor("dqT", [64, TPC], BF16))
        wih = _t(nc.sbuf_tensor("wih", [128, 64], F32))
        s_ld = _t(nc.semaphore("s_ld"))
        s_lw = _t(nc.semaphore("s_lw"))
        s_pa = _t(nc.semaphore("s_pa"))
        s_pb = _t(nc.semaphore("s_pb"))
        s_mm = _t(nc.semaphore("s_mm"))
        s_st = _t(nc.semaphore("s_st"))
        s_ev = _t(nc.semaphore("s_ev"))
        s_ac = _t(nc.semaphore("s_ac"))
        s_cc = _t(nc.semaphore("s_cc"))
        s_db = _t(nc.semaphore("s_db"))

        with nc.Block() as blk:
            @blk.sync
            def _(s):
                for ap, dap in [(idn, ident_d), (ones, ones_d), (knw, knw_d),
                                (qnw, qnw_d), (kvb, kvb_d), (icb, icb_d),
                                (g8, g8_d), (sw0, sw0_d), (sw1, sw1_d)]:
                    s.dma_start(ap[:, :], dap[:, :]).then_inc(s_ld, 16)
                s.wait_ge(s_ld, ldS.inc(16 * 9))

            @blk.gpsimd
            def _(g):
                g.dma_start(x3_d[:, :], x2_d[:, :]).then_inc(s_ac, 16)
                g.wait_ge(s_ac, ac.inc(16))

        with nc.sbuf_tensor("hT", [128, FC * TPC], BF16, side="right") as hT:
            with contextlib.ExitStack() as es:
                _e = es.enter_context
                xt = _e(nc.sbuf_tensor("xt", [128, 2 * D], F32, side="right"))
                htile = _e(nc.sbuf_tensor("htile", [128, 2 * D], BF16,
                                          side="right"))
                junk = _e(nc.sbuf_tensor("junk", [128, D], BF16, side="right"))
                lstats = _e(nc.sbuf_tensor("lstats", [128, 16], F32,
                                           side="right"))
                tps0 = _e(nc.psum_tensor("tps0", [128, 1024], BF16))
                tps1 = _e(nc.psum_tensor("tps1", [128, 1024], BF16))
                emit_ln_phase(nc, idn, x2_d, hT, xt, htile, junk, lstats,
                              tps0, tps1, s_ld, s_pa, s_mm, s_pb,
                              ldS, pa, mm, pb)

            with contextlib.ExitStack() as es:
                _e = es.enter_context
                wqc = _e(nc.sbuf_tensor("wqc", [128, FC * 512], BF16, side="right"))
                wkv = _e(nc.sbuf_tensor("wkv", [128, FC * 128], BF16, side="right"))
                wkz = _e(nc.sbuf_tensor("wkz", [128, FC * 128], BF16, side="right"))
                wkp = _e(nc.sbuf_tensor("wkp", [128, FC * 128], BF16, side="right"))
                wvp = _e(nc.sbuf_tensor("wvp", [128, FC * 128], BF16, side="right"))
                wic = _e(nc.sbuf_tensor("wic", [128, FC * 64], BF16, side="right"))
                wiz = _e(nc.sbuf_tensor("wiz", [128, FC * 64], BF16, side="right"))
                wdq = _e(nc.sbuf_tensor("wdq", [128, FC * 64], BF16, side="right"))
                www = _e(nc.sbuf_tensor("www", [128, FC * 4], BF16, side="right"))
                scst = _e(nc.sbuf_tensor("scst", [128, 64], F32, side="right"))
                ssnt = _e(nc.sbuf_tensor("ssnt", [128, 64], F32, side="right"))
                zbuf = _e(nc.sbuf_tensor("zbuf", [128, 128], F32, side="right"))
                ebuf = _e(nc.sbuf_tensor("ebuf", [128, 384], BF16, side="right"))
                pbuf = _e(nc.sbuf_tensor("pbuf", [128, 384], BF16, side="right"))
                sk1 = _e(nc.sbuf_tensor("sk1", [128, 128], F32, side="right"))
                rp1 = _e(nc.sbuf_tensor("rp1", [128, 128], F32, side="right"))
                sk2 = _e(nc.sbuf_tensor("sk2", [128, 256], BF16, side="right"))
                svb = _e(nc.sbuf_tensor("svb", [128, 256], BF16, side="right"))
                stg = _e(nc.sbuf_tensor("stg", [128, 384], BF16, side="right"))
                rcb = _e(nc.sbuf_tensor("rcb", [128, 192], F32, side="right"))
                junk3 = _e(nc.sbuf_tensor("junk3", [128, 128], BF16, side="right"))
                prj = _e(nc.psum_tensor("prj", [128, 2048], F32))
                cps = _e(nc.psum_tensor("cps", [128, 1024], F32))
                qcp = _e(nc.psum_tensor("qcp", [128, 1024], F32))

                with nc.Block() as blk:
                    @blk.gpsimd
                    def _(g):
                        for sb_t, d_t in [(wqc, wqc_d), (wkv, kvw_d),
                                          (wkz, kvz_d), (wkp, kpw_d),
                                          (wvp, vpw_d), (wic, icw_d),
                                          (wiz, icz_d), (wdq, wdq_d),
                                          (www, ww_d)]:
                            g.dma_start(sb_t[:, :], d_t[:, :]).then_inc(s_lw, 16)
                        g.wait_ge(s_lw, ldW.inc(16 * 9))

                with nc.Block() as blk:
                    pa0 = pa.v
                    st00 = st.v
                    mm0 = mm.v
                    ld0 = ldS.v

                    @blk.sync
                    def _(s):
                        for tb in range(16):
                            so = (tb % 2) * 32
                            base = pa0 + 20 * tb
                            if tb >= 2:
                                s.wait_ge(s_pa, base - 40 + 7)
                            s.dma_start(scst[:, so:so + 32],
                                        scs_d[tb * 128:(tb + 1) * 128, :]
                                        ).then_inc(s_ld, 16)
                            s.dma_start(ssnt[:, so:so + 32],
                                        ssn_d[tb * 128:(tb + 1) * 128, :]
                                        ).then_inc(s_ld, 16)
                            ko = (tb % 2) * 128
                            go = (tb % 2) * 192
                            s.wait_ge(s_pa, base + 8)
                            s.dma_start(agin[tb * 128:(tb + 1) * 128, :],
                                        sk2[:, ko:ko + 128]
                                        ).then_inc(s_st, 16)
                            s.wait_ge(s_pa, base + 9)
                            s.dma_start(agin[2048 + tb * 128:
                                             2048 + (tb + 1) * 128, :],
                                        svb[:, ko:ko + 128]
                                        ).then_inc(s_st, 16)
                            s.wait_ge(s_pa, base + 18)
                            s.dma_start(agin[4096 + tb * 8:
                                             4096 + (tb + 1) * 8, :],
                                        stg[0:8, go:go + 128]
                                        ).then_inc(s_st, 16)
                            s.wait_ge(s_pa, base + 20)
                            s.dma_start(agin[4224 + tb * 8:
                                             4224 + (tb + 1) * 8, 0:64],
                                        stg[0:8, go + 128:go + 192]
                                        ).then_inc(s_st, 16)
                    ldS.inc(32 * 16)
                    st.inc(64 * 16)

                    @blk.tensor
                    def _(t):
                        def g_mms(tb):
                            co = (tb % 2) * 512
                            eo = (tb % 2) * 192
                            t.wait_ge(s_pa, pa0 + 20 * tb + 16)
                            t.matmul(cps[0:8, co:co + 128], g8[:, :],
                                     pbuf[:, eo:eo + 128],
                                     start=True, stop=True)
                            t.matmul(cps[0:8, co + 128:co + 256], g8[:, :],
                                     ebuf[:, eo:eo + 128],
                                     start=True, stop=True)
                            t.matmul(cps[0:8, co + 256:co + 320], g8[:, :],
                                     pbuf[:, eo + 128:eo + 192],
                                     start=True, stop=True)
                            t.matmul(cps[0:8, co + 320:co + 384], g8[:, :],
                                     ebuf[:, eo + 128:eo + 192],
                                     start=True,
                                     stop=True).then_inc(s_mm, 1)
                        for tb in range(16):
                            po = (tb % 2) * 1024
                            qo = (tb % 2) * 512
                            co = (tb % 2) * 512
                            if tb >= 1:
                                g_mms(tb - 1)
                                t.wait_ge(s_pa, pa0 + 20 * (tb - 1) + 20)
                            mi = None
                            for fc in range(FC):
                                hs = hT[:, fc * TPC + tb * 128:
                                        fc * TPC + (tb + 1) * 128]
                                for (lo, hi, wt, ww_) in [
                                        (0, 128, wkp, 128),
                                        (128, 256, wvp, 128),
                                        (256, 384, wkv, 128),
                                        (384, 512, wkz, 128),
                                        (512, 576, wic, 64),
                                        (576, 640, wiz, 64),
                                        (640, 644, www, 4)]:
                                    t.matmul(prj[:, po + lo:po + hi], hs,
                                             wt[:, fc * ww_:(fc + 1) * ww_],
                                             start=(fc == 0 and
                                                    lo in (0, 512)),
                                             stop=(fc == FC - 1))
                                t.matmul(qcp[:, qo:qo + 128],
                                         wqc[:, fc * 512:fc * 512 + 128], hs,
                                         start=(fc == 0),
                                         stop=(fc == FC - 1))
                                t.matmul(cps[0:64, co + 384:co + 512],
                                         wdq[:, fc * 64:(fc + 1) * 64], hs,
                                         start=(fc == 0),
                                         stop=(fc == FC - 1))
                                for ob in range(1, 4):
                                    mi = t.matmul(
                                        qcp[:, qo + ob * 128:
                                            qo + (ob + 1) * 128],
                                        wqc[:, fc * 512 + ob * 128:
                                            fc * 512 + (ob + 1) * 128], hs,
                                        start=False, stop=(fc == FC - 1))
                            mi.then_inc(s_mm, 1)
                        g_mms(15)
                    mm.inc(32)

                    @blk.scalar
                    def _(s):
                        for tb in range(16):
                            base = pa0 + 20 * tb
                            p4 = (tb % 2) * 4
                            po = (tb % 2) * 1024
                            eo = (tb % 2) * 192
                            s.wait_ge(s_mm, mm0 + 2 * tb + 1)
                            s.activation(junk3[:, :], prj[:, po:po + 128],
                                         AF.Square,
                                         accum_out=stats[:, p4:p4 + 1]
                                         ).then_inc(s_pa, 1)
                            s.wait_ge(s_pa, base + 2)
                            s.activation(stats[:, p4 + 2:p4 + 3],
                                         stats[:, p4 + 1:p4 + 2],
                                         AF.Sqrt).then_inc(s_pa, 1)
                            s.wait_ge(s_pa, base + 11)
                            s.activation(ebuf[:, eo:eo + 128], zbuf[:, :],
                                         AF.Exp).then_inc(s_pa, 1)
                            s.wait_ge(s_pa, base + 13)
                            s.activation(ebuf[:, eo + 128:eo + 192],
                                         rp1[:, 0:64],
                                         AF.Exp).then_inc(s_pa, 1)

                    @blk.vector
                    def _(v):
                        for tb in range(16):
                            base = pa0 + 20 * tb
                            p4 = (tb % 2) * 4
                            po = (tb % 2) * 1024
                            qo = (tb % 2) * 512
                            co = (tb % 2) * 512
                            so = (tb % 2) * 32
                            ko = (tb % 2) * 128
                            go = (tb % 2) * 192
                            eo = (tb % 2) * 192
                            if tb >= 2:
                                v.wait_ge(s_st, st00 + 64 * (tb - 1))
                            v.wait_ge(s_ld, ld0 + 32 * (tb + 1))
                            v.wait_ge(s_pa, base + 1)
                            v.tensor_scalar(stats[:, p4 + 1:p4 + 2],
                                            stats[:, p4:p4 + 1],
                                            1.0 / 128, 1e-6, ALU.mult,
                                            ALU.add).then_inc(s_pa, 1)
                            v.wait_ge(s_pa, base + 3)
                            v.reciprocal(stats[:, p4 + 3:p4 + 4],
                                         stats[:, p4 + 2:p4 + 3]
                                         ).then_inc(s_pa, 1)
                            v.wait_ge(s_pa, base + 4)
                            v.tensor_scalar(sk1[:, :], prj[:, po:po + 128],
                                            stats[:, p4 + 3:p4 + 4], None,
                                            ALU.mult).then_inc(s_pa, 1)
                            v.wait_ge(s_pa, base + 5)
                            v.tensor_tensor(sk1[:, :], sk1[:, :], knw[:, :],
                                            ALU.mult).then_inc(s_pa, 1)
                            v.wait_ge(s_pa, base + 6)
                            v.tensor_tensor(rp1[:, 0:32], sk1[:, 0:32],
                                            scst[:, so:so + 32], ALU.mult)
                            v.tensor_tensor(rp1[:, 32:64], sk1[:, 0:32],
                                            ssnt[:, so:so + 32], ALU.mult)
                            v.tensor_tensor(rp1[:, 64:96], sk1[:, 32:64],
                                            ssnt[:, so:so + 32], ALU.mult)
                            v.tensor_tensor(rp1[:, 96:128], sk1[:, 32:64],
                                            scst[:, so:so + 32], ALU.mult
                                            ).then_inc(s_pa, 1)
                            v.wait_ge(s_pa, base + 7)
                            v.tensor_tensor(sk2[:, ko:ko + 32],
                                            rp1[:, 0:32],
                                            rp1[:, 64:96], ALU.subtract)
                            v.tensor_tensor(sk2[:, ko + 32:ko + 64],
                                            rp1[:, 32:64],
                                            rp1[:, 96:128], ALU.add)
                            v.tensor_copy(sk2[:, ko + 64:ko + 128],
                                          sk1[:, 64:128]
                                          ).then_inc(s_pa, 1)
                            v.tensor_copy(svb[:, ko:ko + 128],
                                          prj[:, po + 128:po + 256]
                                          ).then_inc(s_pa, 1)
                            v.tensor_tensor(zbuf[:, :],
                                            prj[:, po + 384:po + 512],
                                            kvb[:, :], ALU.add
                                            ).then_inc(s_pa, 1)
                            v.wait_ge(s_pa, base + 10)
                            v.nop().then_inc(s_pa, 1)
                            v.wait_ge(s_pa, base + 12)
                            v.tensor_tensor(pbuf[:, eo:eo + 128],
                                            ebuf[:, eo:eo + 128],
                                            prj[:, po + 256:po + 384],
                                            ALU.mult)
                            v.tensor_tensor(rp1[:, 0:64],
                                            prj[:, po + 576:po + 640],
                                            icb[:, :], ALU.add
                                            ).then_inc(s_pa, 1)
                            v.wait_ge(s_pa, base + 14)
                            v.tensor_tensor(pbuf[:, eo + 128:eo + 192],
                                            ebuf[:, eo + 128:eo + 192],
                                            prj[:, po + 512:po + 576],
                                            ALU.mult).then_inc(s_pa, 1)
                            v.tensor_copy(dqT[:, tb * 128:(tb + 1) * 128],
                                          cps[0:64, co + 384:co + 512])
                            for ob in range(4):
                                v.tensor_copy(
                                    qcT[:, ob * TPC + tb * 128:
                                        ob * TPC + (tb + 1) * 128],
                                    qcp[:, qo + ob * 128:qo + (ob + 1) * 128])
                            v.tensor_copy(wih[:, tb * 4:(tb + 1) * 4],
                                          prj[:, po + 640:po + 644]
                                          ).then_inc(s_pa, 1)
                            v.wait_ge(s_mm, mm0 + 2 * tb + 2)
                            v.reciprocal(rcb[0:8, 0:128],
                                         cps[0:8, co + 128:co + 256]
                                         ).then_inc(s_pa, 1)
                            v.wait_ge(s_pa, base + 17)
                            v.tensor_tensor(stg[0:8, go:go + 128],
                                            cps[0:8, co:co + 128],
                                            rcb[0:8, 0:128], ALU.mult
                                            ).then_inc(s_pa, 1)
                            v.reciprocal(rcb[0:8, 128:192],
                                         cps[0:8, co + 320:co + 384]
                                         ).then_inc(s_pa, 1)
                            v.wait_ge(s_pa, base + 19)
                            v.tensor_tensor(stg[0:8, go + 128:go + 192],
                                            cps[0:8, co + 256:co + 320],
                                            rcb[0:8, 128:192], ALU.mult
                                            ).then_inc(s_pa, 1)
                    pa.v = pa0 + 20 * 16


            with nc.Block() as blk:
                @blk.gpsimd
                def _(g):
                    g.wait_ge(s_st, st.v)
                    g.collective_compute(
                        "AllGather", ALU.bypass,
                        replica_groups=[[0, 1], [2, 3], [4, 5], [6, 7]],
                        ins=[agin[:, :]],
                        outs=[agout[:, :]],
                    ).then_inc(s_cc, 1)
                ccc.inc(1)

            with contextlib.ExitStack() as es:
                _e = es.enter_context
                wqup = _e(nc.sbuf_tensor("wqup", [128, 4 * D], BF16,
                                         side="right"))
                qcs = _e(nc.sbuf_tensor("qcs", [32, TPC], BF16, side="right"))
                qsn = _e(nc.sbuf_tensor("qsn", [32, TPC], BF16, side="right"))
                qraw = _e(nc.sbuf_tensor("qraw", [128, 2 * TPC], BF16,
                                         side="right"))
                qs32 = _e(nc.sbuf_tensor("qs32", [32, 4 * TPC], BF16,
                                         side="right"))
                qrope = _e(nc.sbuf_tensor("qrope", [128, TPC], BF16,
                                          side="right"))
                qsq = _e(nc.sbuf_tensor("qsq", [128, 2 * TPC], BF16,
                                        side="right"))
                rstd = _e(nc.sbuf_tensor("rstd", [128, TPC], F32, side="right"))
                qps = _e(nc.psum_tensor("qps", [128, 2048], F32))
                sps = _e(nc.psum_tensor("sps", [128, 2048], F32))

                with nc.Block() as blk:
                    @blk.gpsimd
                    def _(g):
                        g.dma_start(wqup[:, :], wqup_d[:, :]).then_inc(s_lw, 16)
                        g.wait_ge(s_lw, ldW.inc(16))

                    @blk.sync
                    def _(s):
                        s.dma_start(qcs[:, :], qcs_d[:, :]).then_inc(s_ld, 16)
                        s.dma_start(qsn[:, :], qsn_d[:, :]).then_inc(s_ld, 16)
                        s.wait_ge(s_ld, ldS.inc(32))

                with nc.Block() as blk:
                    pa0 = pa.v
                    mm0 = mm.v
                    st0 = st.v

                    @blk.tensor
                    def _(t):
                        def sps_mm(hp):
                            t.wait_ge(s_pa, pa0 + 11 * hp + 7)
                            if hp >= 1:
                                t.wait_ge(s_pa, pa0 + 11 * (hp - 1) + 8)
                            mi = None
                            for q in range(4):
                                mi = t.matmul(
                                    sps[:, q * 512:(q + 1) * 512], ones[:, :],
                                    qsq[:, (hp % 2) * TPC + q * 512:
                                        (hp % 2) * TPC + (q + 1) * 512],
                                    start=True, stop=True)
                            mi.then_inc(s_ev, 1)
                        for h in range(16):
                            if h >= 1:
                                t.wait_ge(s_pa, pa0 + 11 * (h - 1) + 1)
                            mi = None
                            for cc in range(4):
                                for q in range(4):
                                    mi = t.matmul(
                                        qps[:, q * 512:(q + 1) * 512],
                                        wqup[:, cc * D + h * 128:
                                             cc * D + (h + 1) * 128],
                                        qcT[:, cc * TPC + q * 512:
                                            cc * TPC + (q + 1) * 512],
                                        start=(cc == 0), stop=(cc == 3))
                            mi.then_inc(s_mm, 1)
                            if h >= 1:
                                sps_mm(h - 1)
                        sps_mm(15)
                    mm.inc(16)
                    ev0_p6 = ev.v
                    ev.inc(16)

                    @blk.vector
                    def _(v):
                        for h in range(16):
                            base = pa0 + 11 * h
                            hb = (h % 2) * TPC
                            qrw = qraw[:, hb:hb + TPC]
                            v.wait_ge(s_mm, mm0 + h + 1)
                            if h >= 2:
                                v.wait_ge(s_st, st0 + 32 * (h - 1))
                            v.tensor_copy(qrw, qps[:, :]
                                          ).then_inc(s_pa, 1)
                            v.wait_ge(s_pa, base + 1)
                            v.tensor_copy(qs32[:, 0:TPC], qrw[32:64, :]
                                          ).then_inc(s_pa, 1)
                            v.wait_ge(s_pa, base + 2)
                            v.tensor_tensor(qs32[:, TPC:2 * TPC],
                                            qs32[:, 0:TPC], qsn[:, :],
                                            ALU.mult)
                            v.tensor_tensor(qs32[:, 2 * TPC:3 * TPC],
                                            qrw[0:32, :], qsn[:, :],
                                            ALU.mult)
                            v.tensor_tensor(qs32[:, 3 * TPC:4 * TPC],
                                            qs32[:, 0:TPC], qcs[:, :],
                                            ALU.mult).then_inc(s_pa, 1)
                            v.wait_ge(s_pa, base + 3)
                            v.tensor_tensor(qrope[0:32, :], qrw[0:32, :],
                                            qcs[:, :], ALU.mult
                                            ).then_inc(s_pa, 1)
                            v.wait_ge(s_pa, base + 4)
                            v.tensor_tensor(qrope[0:32, :], qrope[0:32, :],
                                            qs32[:, TPC:2 * TPC],
                                            ALU.subtract)
                            v.tensor_tensor(qs32[:, 0:TPC],
                                            qs32[:, 2 * TPC:3 * TPC],
                                            qs32[:, 3 * TPC:4 * TPC],
                                            ALU.add).then_inc(s_pa, 1)
                            v.wait_ge(s_pa, base + 5)
                            v.tensor_copy(qrope[32:64, :], qs32[:, 0:TPC])
                            v.tensor_copy(qrope[64:128, :], qrw[64:128, :]
                                          ).then_inc(s_pa, 1)
                            v.wait_ge(s_ev, ev0_p6 + h + 1)
                            v.tensor_scalar(rstd[:, :], sps[:, :], 1.0 / 128,
                                            1e-6, ALU.mult,
                                            ALU.add).then_inc(s_pa, 1)
                            v.wait_ge(s_pa, base + 10)
                            v.scalar_tensor_tensor(qrw, qrope[:, :],
                                                   qnw[:, 0:1], rstd[:, :],
                                                   ALU.mult, ALU.mult
                                                   ).then_inc(s_pa, 1)

                    @blk.scalar
                    def _(s):
                        for h in range(16):
                            base = pa0 + 11 * h
                            s.wait_ge(s_pa, base + 6)
                            s.activation(qsq[:, (h % 2) * TPC:
                                              (h % 2 + 1) * TPC],
                                         qrope[:, :],
                                         AF.Square).then_inc(s_pa, 1)
                            s.wait_ge(s_pa, base + 8)
                            s.activation(rstd[:, :], rstd[:, :],
                                         AF.Ln).then_inc(s_pa, 1)
                            s.wait_ge(s_pa, base + 9)
                            s.activation(rstd[:, :], rstd[:, :], AF.Exp,
                                         scale=-0.5).then_inc(s_pa, 1)

                    @blk.sync
                    def _(s):
                        for h in range(16):
                            hb = (h % 2) * TPC
                            s.wait_ge(s_pa, pa0 + 11 * h + 11)
                            s.dma_start(qT_dram[h, :, :],
                                        qraw[:, hb:hb + TPC]
                                        ).then_inc(s_st, 16)
                            s.dma_start(
                                qT2_dram[:, :, h, :].rearrange(
                                    "j p t -> p j t"),
                                qraw[:, hb:hb + TPC].rearrange(
                                    "p (j t) -> p j t",
                                    j=16)).then_inc(s_st, 16)
                    pa.v = pa0 + 11 * 16
                    st.inc(32 * 16)

        with contextlib.ExitStack() as es:
            _e = es.enter_context
            kst = _e(nc.sbuf_tensor("kst", [128, 256], BF16, side="right"))
            junk4 = _e(nc.sbuf_tensor("junk4", [128, 128], BF16, side="right"))
            tpsk = _e(nc.psum_tensor("tpsk", [128, 1024], BF16))
            tpsk2 = _e(nc.psum_tensor("tpsk2", [128, 1024], BF16))

            with nc.Block() as blk:
                @blk.gpsimd
                def _(g):
                    g.wait_ge(s_cc, ccc.v)

            with nc.Block() as blk:
                @blk.vector
                def _(v):
                    v.memset(swkT[:, 0:128], 0)
                    v.memset(swkT[:, 17 * 128:18 * 128], 0)
                    v.memset(swv[:, 0:128], 0)
                    v.memset(swv[:, 17 * 128:18 * 128], 0)

                @blk.sync
                def _(s):
                    for hf in range(2):
                        for j in range(16):
                            s.dma_start(
                                swv[:, (1 + hf * 17 + j) * 128:
                                    (2 + hf * 17 + j) * 128],
                                agout[hf * 4352 + 2048 + j * 128:
                                      hf * 4352 + 2048 + (j + 1) * 128, :]
                                ).then_inc(s_ld, 16)
                    s.wait_ge(s_ld, ldS.inc(32 * 16))

            for hf in range(2):
                for j in range(16):
                    with nc.Block() as blk:
                        base = pa.v
                        ld0 = ldS.v

                        @blk.sync
                        def _(s):
                            s.dma_start(kst[:, (j % 2) * 128:(j % 2 + 1) * 128],
                                        agout[hf * 4352 + j * 128:
                                              hf * 4352 + (j + 1) * 128, :]
                                        ).then_inc(s_ld, 16)
                        ldS.inc(16)

                        @blk.tensor
                        def _(t):
                            t.wait_ge(s_ld, ldS.v)
                            t.transpose(tpsk[:, 0:128],
                                        kst[:, (j % 2) * 128:(j % 2 + 1) * 128],
                                        idn[:, :]).then_inc(s_mm, 1)
                        mm.inc(1)

                        @blk.vector
                        def _(v):
                            v.wait_ge(s_mm, mm.v)
                            v.tensor_copy(swkT[:, (1 + hf * 17 + j) * 128:
                                              (2 + hf * 17 + j) * 128],
                                          tpsk[:, 0:128])

            for nb in range(2):
                with nc.Block() as blk:
                    base = pa.v
                    ld0 = ldS.v

                    @blk.sync
                    def _(s):
                        s.dma_start(kst[:, 0:128],
                                    agout[nb * 4352 + 4096:
                                          nb * 4352 + 4224, :]
                                    ).then_inc(s_ld, 16)
                        s.dma_start(kst[:, 128:192],
                                    agout[nb * 4352 + 4224:
                                          nb * 4352 + 4352, 0:64]
                                    ).then_inc(s_ld, 16)
                    ldS.inc(32)

                    @blk.scalar
                    def _(s):
                        s.wait_ge(s_ld, ldS.v)
                        s.activation(junk4[:, 0:128], kst[:, 0:128], AF.Square,
                                     accum_out=stats[:, 4:5]).then_inc(s_pa, 1)
                        s.wait_ge(s_pa, base + 2)
                        s.activation(stats[:, 6:7], stats[:, 5:6],
                                     AF.Sqrt).then_inc(s_pa, 1)

                    @blk.vector
                    def _(v):
                        v.wait_ge(s_pa, base + 1)
                        v.tensor_scalar(stats[:, 5:6], stats[:, 4:5],
                                        1.0 / 128, 1e-6, ALU.mult,
                                        ALU.add).then_inc(s_pa, 1)
                        v.wait_ge(s_pa, base + 3)
                        v.reciprocal(stats[:, 7:8], stats[:, 6:7]
                                     ).then_inc(s_pa, 1)
                        v.wait_ge(s_pa, base + 4)
                        v.tensor_scalar(ckvn[:, nb * 128:(nb + 1) * 128],
                                        kst[:, 0:128], stats[:, 7:8], None,
                                        ALU.mult).then_inc(s_pa, 1)
                        v.wait_ge(s_pa, base + 5)
                        v.tensor_tensor(ckvn[:, nb * 128:(nb + 1) * 128],
                                        ckvn[:, nb * 128:(nb + 1) * 128],
                                        knw[:, :], ALU.mult).then_inc(s_pa, 1)
                    pa.v = base + 6

                with nc.Block() as blk:
                    mm0 = mm.v

                    @blk.tensor
                    def _(t):
                        t.transpose(tpsk[:, 0:128],
                                    ckvn[:, nb * 128:(nb + 1) * 128],
                                    idn[:, :]).then_inc(s_mm, 1)
                        t.transpose(tpsk2[0:64, 0:128], kst[:, 128:192],
                                    idn[:, :]).then_inc(s_mm, 1)
                    mm.inc(2)

                    @blk.vector
                    def _(v):
                        v.wait_ge(s_mm, mm0 + 1)
                        v.tensor_copy(ckvnT[:, nb * 128:(nb + 1) * 128],
                                      tpsk[:, 0:128])
                        v.wait_ge(s_mm, mm0 + 2)
                        v.tensor_copy(kcompT[:, nb * 128:(nb + 1) * 128],
                                      tpsk2[0:64, 0:128])

        with contextlib.ExitStack() as es:
            _e = es.enter_context
            qiT = _e(nc.sbuf_tensor("qiT", [64, 4 * TPC], BF16, side="right"))
            wiuq = _e(nc.sbuf_tensor("wiuq_sb", [64, 256], BF16, side="right"))
            cmA = _e(nc.sbuf_tensor("cmA", [128, 16 * 256], F32, side="right"))
            cm1 = _e(nc.sbuf_tensor("cm1", [128, 16 * 256], BF16, side="right"))
            hsb = _e(nc.sbuf_tensor("hsb", [128, 256], BF16, side="right"))
            isc = _e(nc.sbuf_tensor("isc", [128, 256], F32, side="right"))
            isc2 = _e(nc.sbuf_tensor("isc2", [128, 256], F32, side="right"))
            t8b = _e(nc.sbuf_tensor("t8b", [128, 8], F32, side="right"))
            selm = _e(nc.sbuf_tensor("selm", [128, 256], BF16, side="right"))
            isp = _e(nc.psum_tensor("isp", [128, 512], F32))
            qip = _e(nc.psum_tensor("qip", [128, 2048], F32))
            tpsm = _e(nc.psum_tensor("tpsm", [128, 1024], BF16))

            with nc.Block() as blk:
                ld0 = ldS.v

                @blk.sync
                def _(s):
                    s.dma_start(cmA[:, :].rearrange("p (a n) -> p a n", n=256),
                                cmA_d.rearrange("(a p) n -> p a n", p=128)
                                ).then_inc(s_ld, 16)
                    s.dma_start(cm1[:, :].rearrange("p (a n) -> p a n", n=256),
                                cm1_d.rearrange("(a p) n -> p a n", p=128)
                                ).then_inc(s_ld, 16)
                    s.wait_ge(s_ld, ldS.inc(32))

                @blk.gpsimd
                def _(g):
                    g.dma_start(wiuq[:, :], wiuq_d[:, :]).then_inc(s_lw, 16)
                    g.wait_ge(s_lw, ldW.inc(16))

            with nc.Block() as blk:
                mm0 = mm.v

                @blk.tensor
                def _(t):
                    qb0 = pa.v
                    mi = None
                    for ob in range(2):
                        for q in range(4):
                            mi = t.matmul(
                                qip[:, q * 512:(q + 1) * 512],
                                wiuq[:, ob * 128:(ob + 1) * 128],
                                dqT[:, q * 512:(q + 1) * 512],
                                start=True, stop=True)
                        mi.then_inc(s_mm, 1)
                        t.wait_ge(s_pa, qb0 + ob + 1)
                    mm.inc(2)

                @blk.vector
                def _(v):
                    for ob in range(2):
                        v.wait_ge(s_mm, mm.v - 2 + ob + 1)
                        v.tensor_copy(qiT[:, (2 * ob) * TPC:
                                          (2 * ob + 1) * TPC],
                                      qip[0:64, :])
                        v.tensor_copy(qiT[:, (2 * ob + 1) * TPC:
                                          (2 * ob + 2) * TPC],
                                      qip[64:128, :]).then_inc(s_pa, 1)
                pa.v = pa.v + 2

            for tb in range(16):
                with nc.Block() as blk:
                    base = pa.v
                    mm0 = mm.v

                    @blk.tensor
                    def _(t):
                        for h in range(4):
                            t.matmul(isp[:, 0:256],
                                     qiT[:, h * TPC + tb * 128:
                                         h * TPC + (tb + 1) * 128],
                                     kcompT[:, :],
                                     start=True, stop=True).then_inc(s_mm, 1)
                            t.wait_ge(s_pa, base + 2 * h + 1)
                    mm.inc(4)

                    @blk.scalar
                    def _(s):
                        for h in range(4):
                            s.wait_ge(s_mm, mm0 + h + 1)
                            if h >= 1:
                                s.wait_ge(s_pa, base + 2 * h)
                            s.activation(hsb[:, :], isp[:, 0:256],
                                         AF.Relu).then_inc(s_pa, 1)

                    @blk.vector
                    def _(v):
                        if L1A_DEBUG and tb >= 1:
                            v.wait_ge(s_db, 32 * tb)
                        for h in range(4):
                            v.wait_ge(s_pa, base + 2 * h + 1)
                            if h == 0:
                                v.tensor_scalar(
                                    isc[:, :], hsb[:, :],
                                    wih[:, tb * 4:tb * 4 + 1], None,
                                    ALU.mult).then_inc(s_pa, 1)
                            else:
                                v.scalar_tensor_tensor(
                                    isc[:, :], hsb[:, :],
                                    wih[:, tb * 4 + h:tb * 4 + h + 1],
                                    isc[:, :], ALU.mult,
                                    ALU.add).then_inc(s_pa, 1)
                        v.wait_ge(s_pa, base + 8)
                        v.tensor_tensor(isc[:, :], isc[:, :],
                                        cmA[:, tb * 256:(tb + 1) * 256],
                                        ALU.add).then_inc(s_pa, 1)
                        cur = isc
                        nxt = isc2
                        w = 9
                        for it in range(8):
                            v.wait_ge(s_pa, base + w)
                            v.max(out=t8b[:, :], in_=cur[:, :]
                                  ).then_inc(s_pa, 1)
                            w += 1
                            v.wait_ge(s_pa, base + w)
                            if L1A_DEBUG and it == 1:
                                v.wait_ge(s_db, 32 * tb + 16)
                            v.match_replace(out=nxt[:, :],
                                            in_to_replace=t8b[:, :],
                                            in_values=cur[:, :],
                                            imm_value=-1e30).then_inc(s_pa, 1)
                            w += 1
                            cur, nxt = nxt, cur
                        v.wait_ge(s_pa, base + w)
                        v.tensor_scalar(cur[:, :], cur[:, :], -1e30, None,
                                        ALU.is_equal).then_inc(s_pa, 1)
                        w += 1
                        v.wait_ge(s_pa, base + w)
                        v.tensor_tensor(selm[:, :], cur[:, :],
                                        cm1[:, tb * 256:(tb + 1) * 256],
                                        ALU.mult).then_inc(s_pa, 1)
                        w += 1
                        pa.v = base + w

                    if L1A_DEBUG:
                        @blk.sync
                        def _(s):
                            s.wait_ge(s_pa, base + 9)
                            s.dma_start(dbg_isc_d[tb * 128:(tb + 1) * 128, :],
                                        isc[:, :]).then_inc(s_db, 16)
                            s.wait_ge(s_pa, base + 27)
                            s.dma_start(dbg_selm_d[tb * 128:(tb + 1) * 128, :],
                                        selm[:, :]).then_inc(s_db, 16)
                        dbc.inc(32)
                with nc.Block() as blk:
                    mm0 = mm.v

                    @blk.tensor
                    def _(t):
                        t.transpose(tpsm[:, 0:128], selm[:, 0:128],
                                    idn[:, :]).then_inc(s_mm, 1)
                        t.transpose(tpsm[:, 128:256], selm[:, 128:256],
                                    idn[:, :]).then_inc(s_mm, 1)
                    mm.inc(2)

                    @blk.vector
                    def _(v):
                        v.wait_ge(s_mm, mm0 + 1)
                        v.tensor_copy(selmT[:, tb * 128:(tb + 1) * 128],
                                      tpsm[:, 0:128])
                        v.wait_ge(s_mm, mm0 + 2)
                        v.tensor_copy(selmT[:, TPC + tb * 128:
                                          TPC + (tb + 1) * 128],
                                      tpsm[:, 128:256])

        if L1A_DEBUG:
            with nc.Block() as blk:
                @blk.sync
                def _(s):
                    s.dma_start(dbg_ckv_d[:, :], ckvnT[:, :]).then_inc(s_db, 16)
                    s.dma_start(dbg_kc_d[:, :], kcompT[:, :]).then_inc(s_db, 16)
                    s.dma_start(dbg_wih_d[:, :], wih[:, :]).then_inc(s_db, 16)
                    s.dma_start(dbg_dqT_d[:, :], dqT[:, :]).then_inc(s_db, 16)
                    s.dma_start(dbg_swk_d[:, :], swkT[:, :]).then_inc(s_db, 16)
                    s.dma_start(dbg_swv_d[:, :], swv[:, :]).then_inc(s_db, 16)
                    s.wait_ge(s_db, dbc.inc(96))

        st_b4_p7 = st.v
        with contextlib.ExitStack() as es:
            _e = es.enter_context
            qTh = _e(nc.sbuf_tensor("qTh", [128, 4 * TPC], BF16, side="right"))
            qjg = _e(nc.sbuf_tensor("qjg", [128, 16 * 512], BF16,
                                    side="right"))
            swo = _e(nc.sbuf_tensor("swo", [128, 1024], F32, side="right"))
            exps = _e(nc.sbuf_tensor("exps", [128, 2048], BF16, side="right"))
            zit = _e(nc.sbuf_tensor("zit", [128, 1024], F32, side="right"))
            attg = _e(nc.sbuf_tensor("attg", [128, 4 * TPC], BF16,
                                     side="right"))
            o2T = _e(nc.sbuf_tensor("o2T", [128, 4 * TPC], BF16, side="right"))
            fwcA = _e(nc.sbuf_tensor("fwcA", [128, 4 * D], BF16, side="right"))
            fwcB = _e(nc.sbuf_tensor("fwcB", [128, 4 * D], BF16, side="right"))
            gwsb = _e(nc.sbuf_tensor("gwsb", [128, 4 * 512], BF16,
                                     side="right"))
            tmp0 = _e(nc.sbuf_tensor("tmp0l", [128, D], F32, side="right"))
            tmp1 = _e(nc.sbuf_tensor("tmp1l", [128, D], F32, side="right"))
            swe = _e(nc.sbuf_tensor("swe", [128, 2048], BF16, side="right"))
            zsw = _e(nc.sbuf_tensor("zsw", [128, 1024], F32, side="right"))
            pA0 = _e(nc.psum_tensor("pA0", [128, 1024], F32))
            pA1 = _e(nc.psum_tensor("pA1", [128, 1024], F32))
            pB = _e(nc.psum_tensor("pB", [128, 1024], F32))
            pC = _e(nc.psum_tensor("pC", [128, 1024], F32))

            fw_ready = {}
            prev_fw_mm = [None]
            for gidx in range(4):
                with nc.Block() as blk:
                    @blk.sync
                    def _(s):
                        s.dma_start(
                            qTh[:, :].rearrange("p (h t) -> p h t", h=4),
                            qT_dram[4 * gidx:4 * gidx + 4, :, :]
                            .rearrange("h p t -> p h t")).then_inc(s_ld, 16)
                        s.wait_ge(s_st, st_b4_p7)
                    ldS.inc(16)

                with nc.Block() as blk:
                    pa0 = pa.v
                    pb0 = pb.v
                    mm0 = mm.v
                    ev0 = ev.v
                    ld0 = ldS.v

                    @blk.tensor
                    def _(t):
                        t.wait_ge(s_ld, ld0)
                        for p8 in range(8):
                            hh = p8 // 2
                            qh = p8 % 2
                            qsb = qTh[:, hh * TPC + qh * 1024:
                                      hh * TPC + qh * 1024 + 1024]
                            if p8 >= 1:
                                t.wait_ge(s_pb, pb0 + 2 * p8 - 1)
                            mi = None
                            for nb2 in range(2):
                                ps = pA0 if nb2 == 0 else pA1
                                for q in range(2):
                                    mi = t.matmul(
                                        ps[:, q * 512:(q + 1) * 512],
                                        ckvnT[:, nb2 * 128:(nb2 + 1) * 128],
                                        qsb[:, q * 512:(q + 1) * 512],
                                        start=True, stop=True)
                            mi.then_inc(s_mm, 1)
                            t.wait_ge(s_pa, pa0 + 2 * p8 + 1)
                            if p8 >= 1:
                                t.wait_ge(s_ev, ev0 + p8)
                            for nb2 in range(2):
                                for q in range(2):
                                    t.matmul(pB[:, q * 512:(q + 1) * 512],
                                             ones[:, :],
                                             exps[:, nb2 * 1024 + q * 512:
                                                  nb2 * 1024 + (q + 1) * 512],
                                             start=(nb2 == 0),
                                             stop=(nb2 == 1))
                                    mi = t.matmul(
                                        pC[:, q * 512:(q + 1) * 512],
                                        ckvn[:, nb2 * 128:(nb2 + 1) * 128],
                                        exps[:, nb2 * 1024 + q * 512:
                                             nb2 * 1024 + (q + 1) * 512],
                                        start=(nb2 == 0), stop=(nb2 == 1))
                            mi.then_inc(s_mm, 1)
                    mm.inc(16)

                    @blk.scalar
                    def _(s):
                        for p8 in range(8):
                            s.wait_ge(s_mm, mm0 + 2 * p8 + 1)
                            s.activation(exps[:, 0:1024], pA0[:, :],
                                         AF.Exp, scale=ISCALE)
                            s.activation(exps[:, 1024:2048], pA1[:, :],
                                         AF.Exp,
                                         scale=ISCALE).then_inc(s_pb, 1)
                            s.wait_ge(s_pa, pa0 + 2 * p8 + 2)
                            s.activation(zit[:, :], zit[:, :], AF.Ln)
                            s.activation(zit[:, :], zit[:, :], AF.Exp,
                                         scale=-1.0).then_inc(s_pb, 1)
                    pb.v = pb0 + 16

                    @blk.vector
                    def _(v):
                        for p8 in range(8):
                            hh = p8 // 2
                            qh = p8 % 2
                            v.wait_ge(s_pb, pb0 + 2 * p8 + 1)
                            v.tensor_tensor(
                                exps[:, 0:1024], exps[:, 0:1024],
                                selmT[:, qh * 1024:qh * 1024 + 1024],
                                ALU.mult)
                            v.tensor_tensor(
                                exps[:, 1024:2048], exps[:, 1024:2048],
                                selmT[:, TPC + qh * 1024:
                                      TPC + qh * 1024 + 1024],
                                ALU.mult).then_inc(s_pa, 1)
                            v.wait_ge(s_mm, mm0 + 2 * p8 + 2)
                            v.tensor_scalar(zit[:, :], pB[:, :], 1.0, 1e-30,
                                            ALU.mult,
                                            ALU.add).then_inc(s_pa, 1)
                            v.wait_ge(s_pb, pb0 + 2 * p8 + 2)
                            v.tensor_tensor(
                                attg[:, hh * TPC + qh * 1024:
                                     hh * TPC + qh * 1024 + 1024],
                                pC[:, :], zit[:, :],
                                ALU.mult).then_inc(s_ev, 1)
                    pa.v = pa0 + 16
                    ev.inc(8)

                if True:
                    with nc.Block() as blk:
                        mm0 = mm.v
                        pa0 = pa.v
                        pb0 = pb.v
                        ev0 = ev.v
                        ld0 = ldS.v

                        @blk.sync
                        def _(s):
                            s.wait_ge(s_st, st_b4_p7)
                            s.dma_start(
                                qjg[:, :].rearrange("p (j n) -> p j n", j=16),
                                qT2_dram[:, :, 4 * gidx:4 * gidx + 4, :]
                                .rearrange("j p h t -> p j (h t)")
                                ).then_inc(s_ld, 16)
                        ldS.inc(16)

                        @blk.tensor
                        def _(t):
                            t.wait_ge(s_ld, ld0 + 16)
                            for j in range(16):
                                slots = [j, 1 + j, 17 + j, 18 + j]
                                if j >= 1:
                                    t.wait_ge(s_pb, pb0 + 2 * j - 1)
                                mi = None
                                for c4, sl in enumerate(slots):
                                    ps = pA0 if c4 < 2 else pA1
                                    mi = t.matmul(
                                        ps[:, (c4 % 2) * 512:
                                           (c4 % 2 + 1) * 512],
                                        swkT[:, sl * 128:(sl + 1) * 128],
                                        qjg[:, j * 512:(j + 1) * 512],
                                        start=True, stop=True)
                                mi.then_inc(s_mm, 1)
                                t.wait_ge(s_pa, pa0 + 2 * j + 1)
                                if j >= 2:
                                    t.wait_ge(s_ev, ev0 + j - 1)
                                for c4, sl in enumerate(slots):
                                    t.matmul(pB[:, (j % 2) * 512:
                                                (j % 2 + 1) * 512],
                                             ones[:, :],
                                             swe[:, c4 * 512:(c4 + 1) * 512],
                                             start=(c4 == 0), stop=(c4 == 3))
                                    mi = t.matmul(
                                        pC[:, (j % 2) * 512:(j % 2 + 1) * 512],
                                        swv[:, sl * 128:(sl + 1) * 128],
                                        swe[:, c4 * 512:(c4 + 1) * 512],
                                        start=(c4 == 0), stop=(c4 == 3))
                                mi.then_inc(s_mm, 1)
                        mm.inc(32)

                        @blk.scalar
                        def _(s):
                            for j in range(16):
                                s.wait_ge(s_mm, mm0 + 2 * j + 1)
                                s.activation(swe[:, 0:1024], pA0[:, :],
                                             AF.Exp, scale=ISCALE)
                                s.activation(swe[:, 1024:2048], pA1[:, :],
                                             AF.Exp,
                                             scale=ISCALE).then_inc(s_pb, 1)
                                s.wait_ge(s_pa, pa0 + 2 * j + 2)
                                zh = zsw[:, (j % 2) * 512:(j % 2 + 1) * 512]
                                s.activation(zh, zh, AF.Ln)
                                s.activation(zh, zh, AF.Exp,
                                             scale=-1.0).then_inc(s_pb, 1)
                        pb.v = pb0 + 32

                        @blk.vector
                        def _(v):
                            for j in range(16):
                                v.wait_ge(s_pb, pb0 + 2 * j + 1)
                                v.tensor_tensor(swe[:, :], swe[:, :],
                                                sw0[:, :] if j == 0
                                                else sw1[:, :],
                                                ALU.mult).then_inc(s_pa, 1)
                                v.wait_ge(s_mm, mm0 + 2 * j + 2)
                                v.tensor_copy(
                                    zsw[:, (j % 2) * 512:(j % 2 + 1) * 512],
                                    pB[:, (j % 2) * 512:(j % 2 + 1) * 512]
                                    ).then_inc(s_pa, 1)
                                v.wait_ge(s_pb, pb0 + 2 * j + 2)
                                v.tensor_tensor(
                                    swo[:, (j % 2) * 512:(j % 2 + 1) * 512],
                                    pC[:, (j % 2) * 512:(j % 2 + 1) * 512],
                                    zsw[:, (j % 2) * 512:(j % 2 + 1) * 512],
                                    ALU.mult)
                                vi = None
                                for hh in range(4):
                                    vi = v.tensor_tensor(
                                        attg[:, hh * TPC + j * 128:
                                             hh * TPC + (j + 1) * 128],
                                        attg[:, hh * TPC + j * 128:
                                             hh * TPC + (j + 1) * 128],
                                        swo[:, (j % 2) * 512 + hh * 128:
                                            (j % 2) * 512 + (hh + 1) * 128],
                                        ALU.add)
                                vi.then_inc(s_ev, 1)
                        pa.v = pa0 + 32
                        ev.inc(16)

                if True:
                    with nc.Block() as blk:
                        base = pa.v
                        mm0 = mm.v
                        lw0 = ldW.v

                        @blk.gpsimd
                        def _(g):
                            g.dma_start(gwsb[:, :], gw_d[gidx, :, :]
                                        ).then_inc(s_lw, 16)
                        ldW.inc(16)

                        @blk.tensor
                        def _(t):
                            t.wait_ge(s_lw, ldW.v)
                            for ob in range(4):
                                mi = None
                                for ic in range(4):
                                    for q in range(4):
                                        mi = t.matmul(
                                            pA0[:, q * 512:(q + 1) * 512]
                                            if q < 2 else
                                            pA1[:, (q - 2) * 512:(q - 1) * 512],
                                            gwsb[:, ic * 512 + ob * 128:
                                                 ic * 512 + (ob + 1) * 128],
                                            attg[:, ic * TPC + q * 512:
                                                 ic * TPC + (q + 1) * 512],
                                            start=(ic == 0), stop=(ic == 3))
                                mi.then_inc(s_mm, 1)
                                t.wait_ge(s_pa, base + ob + 1)
                        mm.inc(4)

                        @blk.vector
                        def _(v):
                            for ob in range(4):
                                v.wait_ge(s_mm, mm0 + ob + 1)
                                v.tensor_copy(
                                    o2T[:, ob * TPC:ob * TPC + 1024],
                                    pA0[:, :])
                                v.tensor_copy(
                                    o2T[:, ob * TPC + 1024:(ob + 1) * TPC],
                                    pA1[:, :]).then_inc(s_pa, 1)
                        pa.v = base + 4

                    with nc.Block() as blk:
                        base = pa.v
                        mm0 = mm.v
                        ac0 = ac.v
                        ev0 = ev.v
                        fwcg = fwcA if gidx % 2 == 0 else fwcB

                        @blk.gpsimd
                        def _(g):
                            if gidx == 0:
                                for ob in range(4):
                                    g.dma_start(
                                        fwcA[:, ob * D:(ob + 1) * D],
                                        fw_d[ob * 128:(ob + 1) * 128, :]
                                        ).then_inc(s_lw, 16)
                                fw_ready[0] = ldW.inc(64)
                            if gidx + 1 < 4:
                                nb = fwcB if gidx % 2 == 0 else fwcA
                                if prev_fw_mm[0] is not None:
                                    g.wait_ge(s_mm, prev_fw_mm[0])
                                for ob in range(4):
                                    g.dma_start(
                                        nb[:, ob * D:(ob + 1) * D],
                                        fw_d[((gidx + 1) * 4 + ob) * 128:
                                             ((gidx + 1) * 4 + ob + 1) * 128,
                                             :]).then_inc(s_lw, 16)
                                fw_ready[gidx + 1] = ldW.inc(64)
                            for tb in range(16):
                                tm = tmp0 if tb % 2 == 0 else tmp1
                                g.wait_ge(s_ev, ev0 + 2 * tb + 2)
                                g.dma_start(
                                    x3_d[tb * 128:(tb + 1) * 128, :],
                                    tm[:, :],
                                    accum_op=ALU.add).then_inc(s_ac, 16)
                        ac.inc(256)

                        @blk.tensor
                        def _(t):
                            t.wait_ge(s_lw, fw_ready[gidx])
                            for tb in range(16):
                                ps = pB if tb % 2 == 0 else pC
                                if tb >= 2:
                                    t.wait_ge(s_ev, ev0 + 2 * tb - 2)
                                mi = None
                                for dh in range(2):
                                    for ob in range(4):
                                        for q in range(2):
                                            mi = t.matmul(
                                                ps[:, q * 512:(q + 1) * 512],
                                                o2T[:, ob * TPC + tb * 128:
                                                    ob * TPC + (tb + 1) * 128],
                                                fwcg[:, ob * D + dh * 1024 +
                                                     q * 512:
                                                     ob * D + dh * 1024 +
                                                     (q + 1) * 512],
                                                start=(ob == 0),
                                                stop=(ob == 3))
                                    mi.then_inc(s_mm, 1)
                                    if dh == 0:
                                        t.wait_ge(s_ev, ev0 + 2 * tb + 1)
                        mm.inc(32)
                        prev_fw_mm[0] = mm.v

                        @blk.vector
                        def _(v):
                            for tb in range(16):
                                ps = pB if tb % 2 == 0 else pC
                                tm = tmp0 if tb % 2 == 0 else tmp1
                                if tb >= 2:
                                    v.wait_ge(s_ac, ac0 + 16 * (tb - 1))
                                elif tb == 1:
                                    v.wait_ge(s_ac, ac0)
                                else:
                                    v.wait_ge(s_ac, max(ac0 - 16, 0))
                                for dh in range(2):
                                    v.wait_ge(s_mm, mm0 + 2 * tb + dh + 1)
                                    ci = v.tensor_copy(
                                        tm[:, dh * 1024:(dh + 1) * 1024],
                                        ps[:, :])
                                    ci.then_inc(s_ev, 1)
                        ev.inc(32)

        with nc.Block() as blk:
            @blk.sync
            def _(s):
                s.wait_ge(s_ld, ldS.v)
                s.wait_ge(s_st, st.v)

            @blk.gpsimd
            def _(g):
                g.wait_ge(s_lw, ldW.v)
                g.wait_ge(s_ac, ac.v)
                g.wait_ge(s_cc, ccc.v)
    return p


def l1a_masks(core):
    q = core % 2
    pos = own_positions(core)
    mblock_end = np.zeros(256, dtype=np.int64)
    for n in range(256):
        hf, jj = divmod(n, 128)
        j8, k8 = divmod(jj, 8)
        gblk = 2 * j8 + hf
        mstart = gblk * 8 + k8
        mblock_end[n] = mstart * 16 + 15
    tglob = pos
    cm01 = (mblock_end[None, :] < tglob[:, None]).astype(np.float32)
    cmA = (cm01 - 1.0) * 30000.0
    kk = np.arange(128)[:, None]
    qq = np.arange(128)[None, :]
    tri_up = (kk > qq).astype(np.float32)
    low_in = (kk <= qq).astype(np.float32)
    full = np.ones((128, 128), np.float32)
    zero = np.zeros((128, 128), np.float32)
    if q == 0:
        c1 = [tri_up, low_in, full, zero]
        c0 = [zero, low_in, zero, zero]
    else:
        c1 = [zero, full, tri_up, low_in]
        c0 = [zero, full, zero, low_in]
    m1 = np.concatenate([np.tile(ch, (1, 4)) for ch in c1], axis=1)
    m0 = np.concatenate([np.tile(ch, (1, 4)) for ch in c0], axis=1)
    return (np.ascontiguousarray(cmA.astype(np.float32)),
            np.ascontiguousarray(cm01.astype(BF)),
            np.ascontiguousarray(m0.astype(BF)),
            np.ascontiguousarray(m1.astype(BF)))


def l1a_inmaps(x2_full, z):
    wqc = f32a(z["L1_attn_w_qc"]); wqup = f32a(z["L1_attn_w_qup"])
    kvw = f32a(z["L1_attn_kvc_w"]); kvz = f32a(z["L1_attn_kvc_wz"])
    kvbias = f32a(z["L1_attn_kvc_bias"])
    kpw = f32a(z["L1_attn_k_proj_w"]); vpw = f32a(z["L1_attn_v_proj_w"])
    icw = f32a(z["L1_attn_idx_c_w"]); icz = f32a(z["L1_attn_idx_c_wz"])
    icbias = f32a(z["L1_attn_idx_c_bias"])
    wdq = f32a(z["L1_attn_w_dq"]); wiuq = f32a(z["L1_attn_w_iuq"])
    ww = f32a(z["L1_attn_w_w"])
    qnwv = f32a(z["L1_attn_q_norm_w"]); knwv = f32a(z["L1_attn_k_norm_w"])
    gweight = f32a(z["L1_attn_group_w"]); fw = f32a(z["L1_attn_final_w"])
    rd = 64
    inv = 1.0 / (10000.0 ** (np.arange(0, rd, 2, dtype=np.float64) / rd))
    ang = np.arange(T, dtype=np.float64)[:, None] * inv
    cosT = np.cos(ang).astype(np.float32)
    sinT = np.sin(ang).astype(np.float32)
    g8m = np.zeros((128, 8), dtype=np.float32)
    for p_ in range(128):
        g8m[p_, p_ // 16] = 1.0
    ident = np.eye(128, dtype=BF)
    onesb = np.ones((128, 128), dtype=BF)
    gw_pre = np.stack([pre_fmaj(gweight[g_]) for g_ in range(4)])
    maps = []
    for c in range(8):
        pos = own_positions(c)
        cmA, cm01, m0, m1 = l1a_masks(c)
        maps.append({
            "x2": np.ascontiguousarray(x2_full[pos + (c // 2) * T]),
            "wqc_pre": f32a(pre_fmaj(wqc)),
            "wqup_pre": f32a(pre_fmaj(wqup)),
            "kvcw_pre": f32a(pre_fmaj(kvw)),
            "kvcz_pre": f32a(pre_fmaj(kvz)),
            "kproj_pre": f32a(pre_fmaj(kpw)),
            "vproj_pre": f32a(pre_fmaj(vpw)),
            "idxcw_pre": f32a(pre_fmaj(icw)),
            "idxcz_pre": f32a(pre_fmaj(icz)),
            "wdq_pre": f32a(pre_fmaj(wdq)),
            "wiuq": wiuq,
            "ww_pre": f32a(pre_fmaj(ww)),
            "gw_pre": f32a(gw_pre),
            "final_w": fw,
            "kvb_rep": np.ascontiguousarray(np.tile(kvbias, (8, 1))),
            "icb_rep": np.ascontiguousarray(np.tile(icbias, (8, 1))),
            "knw_rep": np.ascontiguousarray(
                np.broadcast_to(knwv[None, :], (128, 128)).astype(np.float32)),
            "qnw_col": np.ascontiguousarray(qnwv.reshape(128, 1)),
            "l1qcos": np.ascontiguousarray(cosT[pos].T.astype(BF)),
            "l1qsin": np.ascontiguousarray(sinT[pos].T.astype(BF)),
            "swcos": np.ascontiguousarray(cosT[pos]),
            "swsin": np.ascontiguousarray(sinT[pos]),
            "cmaskA": cmA, "cmask01": cm01,
            "swmask0": m0, "swmask1": m1,
            "g8_in": np.ascontiguousarray(g8m.astype(BF)),
            "ident": ident, "ones_bf": onesb,
        })
    return maps
